# version 07 (final)
# baseline (speedup 1.0000x reference)
import math
import contextlib
import numpy as np

import concourse.bass as bass
import concourse.mybir as mybir
from concourse.bass_utils import run_bass_kernel_spmd
from concourse.alu_op_type import AluOpType

F32 = mybir.dt.float32
BF16 = mybir.dt.bfloat16
I32 = mybir.dt.int32
AF = mybir.ActivationFunctionType
AX = mybir.AxisListType.X

H = 1024
NH = 16
HD = 64
S = 2048
NT = 16
NB = 2
INNER = 4096
SPAN = 256
SCALE = 1.0 / math.sqrt(HD * 3.0)
WBAND = 1280
DMAX = 4
NCORE = 8
WROW = NT * WBAND

TAPS = {}
NH_RUN = NH
NB_RUN = NB
FFN_RUN = True
WO_RUN = True
C2P_ON = True
P2C_ON = True
SAT_ON = True
PREP_RUN = True
REL_RUN = True
LN1_RUN = True
TRUNC = None
MARKS = []
FOLD_BANDS = True


def _cidx(r):
    r = np.asarray(r, np.int64)
    sign = np.sign(r).astype(np.float64)
    mid = 128
    abs_pos = np.where((r < mid) & (r > -mid), float(mid - 1),
                       np.abs(r)).astype(np.float64)
    log_pos = np.ceil(
        np.log(abs_pos / mid)
        / np.float64(np.float32(np.log((512 - 1) / mid))) * (mid - 1)) + mid
    bucket = np.where(abs_pos <= mid, r.astype(np.float64), log_pos * sign)
    return np.clip(bucket.astype(np.int64) + SPAN, 0, 2 * SPAN - 1)


def _g_tables():
    import ml_dtypes
    u = np.arange(WBAND)
    g_rev = np.zeros((2 * SPAN, WBAND), np.float32)
    g_fwd = np.zeros((2 * SPAN, WBAND), np.float32)
    g_rev[_cidx(639 - u), u] = 1.0
    g_fwd[_cidx(u - 639), u] = 1.0
    return (g_rev.astype(ml_dtypes.bfloat16), g_fwd.astype(ml_dtypes.bfloat16))


_UREV = _cidx(639 - np.arange(WBAND))
_UFWD = _cidx(np.arange(WBAND) - 639)
U_SATH_REV = int(np.argmax(_UREV == 511))
U_SATL_REV = int(np.argmax(_UREV == 0))
U_SATH_FWD = int(np.argmax(_UFWD == 511))
U_SATL_FWD = int(np.argmax(_UFWD == 0))


class Plan:
    def __init__(self, nc):
        self.nc = nc
        self.steps = []
        self.sem = {}
        self.cnt = {}
        self.es = contextlib.ExitStack()
        for s_ in ("d_sync", "d_scalar", "d_gpsimd", "p_vector", "p_scalar",
                   "p_tensor", "p_gpsimd"):
            self.sem[s_] = self.es.enter_context(nc.semaphore(s_))
            self.cnt[s_] = 0

    def emit(self, eng, fn):
        self.steps.append((eng, fn))

    def wait(self, eng, tok):
        if tok is None:
            return
        name, val = tok
        h = self.sem[name]
        self.emit(eng, lambda e, h=h, v=val: e.wait_ge(h, v))

    def tok(self, name):
        return (name, self.cnt[name])

    def dma(self, q, out, in_, after=(), slow=False):
        for t in after:
            self.wait(q, t)
        name = "d_" + q
        self.cnt[name] += 16
        h = self.sem[name]
        if slow:
            self.emit(q, lambda e, o=out, i=in_, h=h:
                      e.dma_start(out=o, in_=i,
                                  allow_slow_non_contiguous=True).then_inc(h, 16))
        else:
            self.emit(q, lambda e, o=out, i=in_, h=h:
                      e.dma_start(out=o, in_=i).then_inc(h, 16))
        return (name, self.cnt[name])

    def op(self, eng, method, *args, after=(), **kw):
        for t in after:
            self.wait(eng, t)
        name = "p_" + eng
        h = self.sem[name]
        pre = self.cnt[name]
        self.cnt[name] += 1
        v = self.cnt[name]

        def fn(e, m=method, a=args, k=kw, h=h, pre=pre):
            e.wait_ge(h, pre)
            getattr(e, m)(*a, **k).then_inc(h, 1)
        self.emit(eng, fn)
        return (name, v)

    def top(self, method, *args, after=(), **kw):
        for t in after:
            self.wait("tensor", t)
        cell = {"inc": 0}
        self._last_top = cell
        h = self.sem["p_tensor"]

        def fn(e, m=method, a=args, k=kw, cell=cell, h=h):
            inst = getattr(e, m)(*a, **k)
            if cell["inc"]:
                inst.then_inc(h, cell["inc"])
        self.emit("tensor", fn)

    def tmark(self):
        name = "p_tensor"
        cell = getattr(self, "_last_top", None)
        if cell is None:
            return (name, self.cnt[name])
        self.cnt[name] += 1
        cell["inc"] += 1
        return (name, self.cnt[name])

    def engine_gates(self):
        return [self.tok("p_vector"), self.tok("p_scalar"),
                self.tok("p_gpsimd"), self.tmark(),
                self.tok("d_sync"), self.tok("d_scalar"),
                self.tok("d_gpsimd")]


def rawap(t, offset, dims):
    ap = t.ap().copy() if callable(getattr(t, "ap", None)) else t.copy()
    ap.offset = offset
    while len(ap.ap) > 1:
        ap.ap.pop()
    ap.ap[0] = list(dims[0])
    for d in dims[1:]:
        ap.ap.append(list(d))
    return ap


def build_nc():
    nc = bass.Bass("TRN2", target_bir_lowering=False,
                   detect_race_conditions=False,
                   dynamic_dma_scratch_size=16384)
    p = Plan(nc)

    def _mk(lbl):
        MARKS.append((lbl, len(p.steps)))
    dt = {}

    def din(name, shape, dtype=F32):
        dt[name] = nc.dram_tensor(name, shape, dtype, kind="ExternalInput").ap()
        return dt[name]

    x_in = din("x", [NB * S, H])
    for nm in ("Wq", "Wk", "Wv", "Wo", "Wpk", "Wpq"):
        din(nm, [H, H])
    for nm in ("bq", "bk", "bv", "bo", "bpk", "bpq", "ln_g", "ln_b"):
        din(nm, [H])
    din("rel", [2 * SPAN, H])
    din("wg", [INNER, H])
    din("wu", [INNER, H])
    din("wd", [H, INNER])
    din("Grev", [2 * SPAN, WBAND], BF16)
    din("Gfwd", [2 * SPAN, WBAND], BF16)
    din("identb", [128, 128], BF16)
    din("identf", [128, 128], F32)
    din("bqk_t", [64, 80])

    out_hbm = nc.dram_tensor("out", [NB * S, H], F32, kind="ExternalOutput").ap()
    x2_s = nc.dram_tensor("x2_s", [NB * S, H], F32, kind="Internal").ap()
    tabs_s = nc.dram_tensor("tabs_s", [NH * 64, 2 * WBAND], BF16,
                            kind="Internal").ap()
    ctx_s = nc.dram_tensor("ctx_s", [H, S], BF16, kind="Internal").ap()
    wgT_s = nc.dram_tensor("wgT_s", [H, INNER], BF16, kind="Internal").ap()
    wuT_s = nc.dram_tensor("wuT_s", [H, INNER], BF16, kind="Internal").ap()
    wdT_s = nc.dram_tensor("wdT_s", [INNER, H], BF16, kind="Internal").ap()
    guq_s = nc.dram_tensor("guq_s", [NB * NT, 128 * INNER], BF16,
                           kind="Internal").ap()

    tap_out = {}

    def tap(tag, ap, after):
        if tag not in TAPS:
            return
        shape, dtyp = TAPS[tag]
        nm = "dbg_" + tag
        if nm not in tap_out:
            tap_out[nm] = nc.dram_tensor(nm, shape, dtyp,
                                         kind="ExternalOutput").ap()
        p.dma("sync", tap_out[nm], ap, after=after)

    def sb_in(stk, name, shape, dtyp):
        return stk.enter_context(nc.sbuf_tensor(name, shape, dtyp))

    es0 = p.es

    def sb(name, shape, dtyp):
        return sb_in(es0, name, shape, dtyp)

    identb = sb("identb_t", [128, 128], BF16)
    identf = sb("identf_t", [128, 128], F32)
    ones_bf = sb("ones_bf", [128, 128], BF16)
    ones_fr = sb("ones_fr", [1, 128], F32)
    ones64 = sb("ones64", [128, 64], F32)
    ones_b64 = sb("ones_b64", [64, 1], BF16)
    onesc_f = sb("onesc_f", [128, 1], F32)
    bvec = sb("bvec", [128, 5 * 16], F32)
    scal = sb("scal", [1, 16], F32)
    scal2 = sb("scal2", [128, 8], F32)
    st1 = sb("st1", [128, 8], F32)
    xsq = sb("xsq", [128, H], F32)
    xt_f = [sb(f"xt_f{i}", [128, H], F32) for i in range(2)]
    thrv = sb("thrv", [128, 8], F32)
    rowbuf = sb("rowbuf", [2, S], BF16)
    normbuf = sb("normbuf", [1, 2 * NH], F32)
    dsc_all = sb("dsc_all", [128, NB * NT], F32)

    ps_a = es0.enter_context(nc.psum_tensor("ps_a", [128, 512], F32))
    ps_b = es0.enter_context(nc.psum_tensor("ps_b", [128, 512], F32))
    ps_t = es0.enter_context(nc.psum_tensor("ps_t", [128, 1024], BF16))
    ps_t2 = es0.enter_context(nc.psum_tensor("ps_t2", [128, 1024], BF16))

    psfree = {"a": None, "b": None, "t": None, "t2": None, "ctx": None}
    xtfree = [None, None]

    t_id = p.dma("sync", identb[:], dt["identb"])
    t_if = p.dma("sync", identf[:], dt["identf"])
    p.dma("sync", bvec[0:64, 0:80], dt["bqk_t"])
    t_ldc = p.tok("d_sync")

    t_ones = p.op("vector", "memset", ones_bf[:], 1.0)
    p.op("vector", "memset", ones_fr[:], 1.0)
    t_o64 = p.op("vector", "memset", ones64[:], 1.0)
    p.op("vector", "memset", ones_b64[:], 1.0)
    t_onesf = p.op("vector", "memset", onesc_f[:], 1.0)
    t_bq = p.op("vector", "tensor_scalar_mul", bvec[0:64, 0:16],
                bvec[0:64, 0:16], SCALE, after=[t_ldc])
    t_c127 = p.op("vector", "memset", thrv[:, 5:6], 127.0)

    shared = {"xsq": None}

    def layernorm(xf, tgt, W, gamma=None, beta=None, after=(), apply_after=()):
        aft0 = list(after) + ([shared["xsq"]] if shared["xsq"] else [])
        t0 = p.op("scalar", "activation", xsq[:, 0:W], xf, AF.Square,
                  after=aft0)
        t1 = p.op("vector", "reduce_sum", st1[:, 0:1], xf, AX, after=after)
        t2 = p.op("vector", "reduce_sum", st1[:, 1:2], xsq[:, 0:W], AX,
                  after=[t0])
        shared["xsq"] = t2
        p.op("vector", "tensor_scalar_mul", st1[:, 2:3], st1[:, 0:1], 1.0 / W,
             after=[t1])
        p.op("vector", "tensor_scalar_mul", st1[:, 3:4], st1[:, 1:2], 1.0 / W,
             after=[t2])
        p.op("vector", "tensor_tensor", st1[:, 4:5], st1[:, 2:3], st1[:, 2:3],
             AluOpType.mult)
        tvar = p.op("vector", "scalar_tensor_tensor", st1[:, 5:6], st1[:, 3:4],
                    1e-5, st1[:, 4:5], AluOpType.add, AluOpType.subtract)
        t3a = p.op("scalar", "activation", st1[:, 4:5], st1[:, 5:6], AF.Sqrt,
                   after=[tvar])
        t3 = p.op("vector", "reciprocal", st1[:, 6:7], st1[:, 4:5],
                  after=[t3a])
        if gamma is None:
            return p.op("vector", "tensor_scalar", tgt, xf, st1[:, 2:3],
                        st1[:, 6:7], AluOpType.subtract, AluOpType.mult,
                        after=[t3] + list(apply_after))
        ta = p.op("vector", "tensor_scalar", xsq[:, 0:W], xf, st1[:, 2:3],
                  st1[:, 6:7], AluOpType.subtract, AluOpType.mult, after=[t3])
        tb = p.op("vector", "tensor_tensor", xsq[:, 0:W], xsq[:, 0:W], gamma,
                  AluOpType.mult, after=[ta])
        shared["xsq"] = tb
        return p.op("vector", "tensor_tensor", tgt, xsq[:, 0:W], beta,
                    AluOpType.add, after=[tb] + list(apply_after))

    es_prep = contextlib.ExitStack()
    wtile = [sb_in(es_prep, f"wtile{i}", [128, INNER], F32) for i in range(2)]
    g1big = sb_in(es_prep, "g1big", [128, INNER], F32)
    ternt = sb_in(es_prep, "ternt", [128, INNER], BF16)
    stage = sb_in(es_prep, "stage", [128, INNER], BF16)
    wtfree = [None, None]

    def prep_weight(wname, dst_hbm, nrow, ncol, scol):
        ntile = nrow // 128
        tacc = p.op("vector", "memset", scal2[:, 0:1], 0.0)
        for t in range(ntile):
            td = p.dma(("sync", "scalar", "gpsimd")[t % 3],
                       wtile[t % 2][:, 0:ncol],
                       dt[wname][128 * t:128 * (t + 1), :],
                       after=[wtfree[t % 2]] if wtfree[t % 2] else [])
            tr = p.op("vector", "reduce_sum", scal2[:, 1:2],
                      wtile[t % 2][:, 0:ncol], AX, apply_absolute_value=True,
                      after=[td])
            tacc = p.op("vector", "tensor_tensor", scal2[:, 0:1],
                        scal2[:, 0:1], scal2[:, 1:2], AluOpType.add,
                        after=[tr])
            wtfree[t % 2] = tacc
        p.top("matmul", ps_a[0:1, 0:1], onesc_f[:], scal2[:, 0:1],
              start=True, stop=True, after=[tacc, t_onesf, psfree["a"]])
        tm = p.tmark()
        t5 = p.op("vector", "tensor_scalar", scal[0:1, scol:scol + 1],
                  ps_a[0:1, 0:1], 1.0 / (nrow * ncol), 1e-5,
                  AluOpType.mult, AluOpType.max, after=[tm])
        t6 = p.op("vector", "tensor_scalar_mul", scal[0:1, 15:16],
                  scal[0:1, scol:scol + 1], 0.5)
        p.top("matmul", ps_a[:, 4:5], ones_fr[:],
              scal[0:1, 15:16], start=True, stop=True, after=[t6, t5])
        tb128 = p.tmark()
        tpos = p.op("vector", "tensor_copy", thrv[:, 0:1], ps_a[:, 4:5],
                    after=[tb128])
        psfree["a"] = tpos
        tneg = p.op("vector", "tensor_scalar_mul", thrv[:, 1:2], thrv[:, 0:1],
                    -1.0)
        nb = ncol // 128
        stage_free = None
        tern_free = [None]
        for t in range(ntile):
            td = p.dma(("sync", "scalar", "gpsimd")[t % 3],
                       wtile[t % 2][:, 0:ncol],
                       dt[wname][128 * t:128 * (t + 1), :],
                       after=[wtfree[t % 2]])
            tg2 = p.op("vector", "tensor_scalar", g1big[:, 0:ncol],
                       wtile[t % 2][:, 0:ncol], thrv[:, 1:2], None,
                       AluOpType.is_le, after=[td, tneg])
            tq = p.op("vector", "scalar_tensor_tensor", ternt[:, 0:ncol],
                      wtile[t % 2][:, 0:ncol], thrv[:, 0:1], g1big[:, 0:ncol],
                      AluOpType.is_ge, AluOpType.subtract,
                      after=[tg2, tpos] + ([tern_free[0]] if tern_free[0]
                                           else []))
            wtfree[t % 2] = tq
            tev = None
            for g0 in range(0, nb, 8):
                g1 = min(nb, g0 + 8)
                key = "t" if (g0 // 8) % 2 == 0 else "t2"
                pst = ps_t if key == "t" else ps_t2
                for c in range(g0, g1):
                    aft = [tq, t_id, psfree[key]] if c == g0 else ()
                    p.top("matmul",
                          pst[:, 128 * (c - g0):128 * (c - g0) + 128],
                          ternt[:, 128 * c:128 * (c + 1)], identb[:],
                          start=(c == g0), stop=(c == g1 - 1),
                          is_transpose=True, after=aft)
                tt = p.tmark()
                tev = p.op("scalar", "copy", stage[:, 128 * g0:128 * g1],
                           pst[:, 0:128 * (g1 - g0)],
                           after=[tt, stage_free] if stage_free else [tt])
                psfree[key] = tev
                tern_free[0] = tt
            dstv = rawap(dst_hbm, 128 * t,
                         [[nrow, 128], [128 * nrow, nb], [1, 128]])
            srcv = rawap(stage, 0, [[INNER, 128], [128, nb], [1, 128]])
            stage_free = p.dma("sync", dstv, srcv, after=[tev])
        return stage_free

    if PREP_RUN:
        tw_g = prep_weight("wg", wgT_s, INNER, H, 2)
        tw_u = prep_weight("wu", wuT_s, INNER, H, 3)
        tw_d = prep_weight("wd", wdT_s, H, INNER, 4)
    else:
        tw_g = tw_u = tw_d = p.tok("d_sync")
    t_prep_gates = p.engine_gates()
    es_prep.close()

    es_attn = contextlib.ExitStack()

    def sba(name, shape, dtyp):
        return sb_in(es_attn, name, shape, dtyp)

    grev4 = sba("grev4", [128, 4 * WBAND], BF16)
    gfwd4 = sba("gfwd4", [128, 4 * WBAND], BF16)
    re_fm = sba("re_fm", [128, 8 * 512], BF16)
    xn_fm = sba("xn_fm", [128, 8 * S], BF16)
    ctx_h = sba("ctx_h", [64, S], BF16)
    wo_ctx = [sba(f"wo_ctx{i}", [128, 8 * 128], BF16) for i in range(2)]
    Kx = sba("Kx", [128, S], BF16)
    Qx = sba("Qx", [128, S], BF16)
    QxH = sba("QxH", [128, S], BF16)
    QxL = sba("QxL", [128, S], BF16)
    v_aug = sba("v_aug", [128, NT * (HD + 1)], BF16)
    bc_band = sba("bc_band", [128, WROW], BF16)
    bp_band = [sba(f"bp_band{i}", [128, WBAND], BF16) for i in range(2)]
    pt_buf = [sba(f"pt{i}", [128, S], BF16) for i in range(2)]
    c2pT = [sba(f"c2pT{i}", [128, 9 * 128], BF16) for i in range(2)]
    p2cT = [sba(f"p2cT{i}", [128, 9 * 128], BF16) for i in range(2)]
    ctile = [sba(f"ctile{i}", [128, 9 * 128], BF16) for i in range(2)]
    tabR = sba("tabR", [64, WBAND], BF16)
    tabF = sba("tabF", [64, WBAND], BF16)
    pkpm = sba("pkpm", [128, 4 * 64], BF16)
    pqpm = sba("pqpm", [128, 4 * 64], BF16)
    wsl = sba("wslice", [128, 5 * 512], BF16)
    den = sba("den", [128, S], F32)
    mrow = den[0:1, :]
    xn_tok = sba("xn_tok", [128, H], BF16)
    ps_ctx = es_attn.enter_context(nc.psum_tensor("ps_ctx", [128, S], F32))
    qsq = ctx_h
    es_rel = contextlib.ExitStack()
    lnb_g = sb_in(es_rel, "lnb_g", [128, H], F32)
    lnb_b = sb_in(es_rel, "lnb_b", [128, H], F32)
    re_tok = sb_in(es_rel, "re_tok", [128, H], BF16)

    for c in range(4):
        p.dma("sync", grev4[:, c * WBAND:(c + 1) * WBAND],
              dt["Grev"][c * 128:(c + 1) * 128, :])
        t_gtab = p.dma("sync", gfwd4[:, c * WBAND:(c + 1) * WBAND],
                       dt["Gfwd"][c * 128:(c + 1) * 128, :])
    t_g0 = p.dma("sync", lnb_g[0:1, :], rawap(dt["ln_g"], 0, [[H, 1], [1, H]]))
    t_b0 = p.dma("sync", lnb_b[0:1, :], rawap(dt["ln_b"], 0, [[H, 1], [1, H]]))
    t_g = t_b = None
    for (t0_, buf) in ((t_g0, lnb_g), (t_b0, lnb_b)):
        for nk in range(2):
            key = "a" if nk == 0 else "b"
            pst = ps_a if nk == 0 else ps_b
            p.top("matmul", pst[:, 0:512], ones_fr[:],
                  buf[0:1, 512 * nk:512 * (nk + 1)],
                  start=True, stop=True, after=[t0_, psfree[key]])
            tmm = p.tmark()
            tkn = p.op("vector", "tensor_copy",
                       buf[:, 512 * nk:512 * (nk + 1)], pst[:, 0:512],
                       after=[tmm])
            psfree[key] = tkn
        if buf is lnb_g:
            t_g = tkn
        else:
            t_b = tkn

    t_refm = None
    for rt in range(4 if REL_RUN else 0):
        tl = p.dma("sync", xt_f[rt % 2][:],
                   dt["rel"][128 * rt:128 * (rt + 1), :],
                   after=[xtfree[rt % 2], t_prep_gates[3]]
                   if xtfree[rt % 2] else [t_prep_gates[3]])
        ta = layernorm(xt_f[rt % 2][:], re_tok[:], H, gamma=lnb_g[:],
                       beta=lnb_b[:], after=[tl, t_g, t_b],
                       apply_after=[t_refm] if t_refm else [])
        xtfree[rt % 2] = ta
        for c in range(8):
            aft = [ta, t_id, psfree["t"]] if c == 0 else ()
            p.top("matmul", ps_t[:, 128 * c:128 * (c + 1)],
                  re_tok[:, 128 * c:128 * (c + 1)], identb[:],
                  start=(c == 0), stop=(c == 7), is_transpose=True, after=aft)
        tt = p.tmark()
        t_refm = p.op("scalar", "copy",
                      rawap(re_fm, 128 * rt, [[8 * 512, 128], [512, 8], [1, 128]]),
                      ps_t[:, 0:1024],
                      after=[tt, t_refm] if t_refm else [tt])
        psfree["t"] = t_refm
    if not REL_RUN:
        t_refm = p.tok("d_sync")
    es_rel.close()

    def win(kt):
        return max(0, kt - DMAX), min(NT, kt + DMAX + 1)

    t_batch_gate = None
    t_wo_loaded = None
    t_wsl_free = None
    normtok = [None] * (2 * NH)
    tabstore_tok = [None] * NH
    wo_view = bc_band

    for b in range(NB_RUN):
        _mk(f"b{b}_ln1")
        t_xn = None
        xn_tt = [None]
        for t in range(NT if LN1_RUN else 0):
            aft = [xtfree[t % 2]] if xtfree[t % 2] else []
            if t_batch_gate:
                aft.append(t_batch_gate)
            tl = p.dma("sync", xt_f[t % 2][:],
                       x_in[b * S + 128 * t:b * S + 128 * (t + 1), :],
                       after=aft)
            ta = layernorm(xt_f[t % 2][:], xn_tok[:], H, after=[tl],
                           apply_after=[xn_tt[0]] if xn_tt[0] else [])
            xtfree[t % 2] = ta
            for c in range(8):
                aft = [ta, psfree["t"]] + ([t_batch_gate] if t_batch_gate else []) \
                    if c == 0 else ()
                p.top("matmul", ps_t[:, 128 * c:128 * (c + 1)],
                      xn_tok[:, 128 * c:128 * (c + 1)], identb[:],
                      start=(c == 0), stop=(c == 7), is_transpose=True, after=aft)
            tt = p.tmark()
            xn_tt[0] = tt
            t_xn = p.op("scalar", "copy",
                        rawap(xn_fm, 128 * t, [[8 * S, 128], [S, 8], [1, 128]]),
                        ps_t[:, 0:1024], after=[tt, t_xn] if t_xn else [tt])
            psfree["t"] = t_xn
        if not LN1_RUN:
            t_xn = p.tok("d_sync")
        tap(f"xnfm_{b}", xn_fm[:], [t_xn])

        if b == 0:
            tcr1 = p.op("vector", "memset", rowbuf[0:1, :], 1.0)
            tks = []
            for dst_ten, row in ((Kx, 64), (Kx, 66), (Kx, 68),
                                 (QxH, 65), (QxL, 67)):
                tks.append(p.dma("sync", dst_ten[row:row + 1, :],
                                 rowbuf[0:1, :], after=[tcr1]))
            tcr0 = p.op("vector", "memset", rowbuf[0:1, :], 0.0,
                        after=[tks[-1]])
            for dst_ten, row in ((QxL, 65), (QxL, 66)):
                tks.append(p.dma("sync", dst_ten[row:row + 1, :],
                                 rowbuf[0:1, :], after=[tcr0]))
            t_cr = tks[-1]
            t_vone = p.op("vector", "memset",
                          rawap(v_aug, HD, [[NT * (HD + 1), 128], [HD + 1, NT]]),
                          1.0)
        t_head_gate = t_batch_gate

        for h in range(NH_RUN):
            _mk(f"head{h}_start")
            hg = [t_head_gate] if t_head_gate else []
            for k, nm in enumerate(("Wq", "Wk", "Wv", "Wpk", "Wpq")):
                tw = p.dma("gpsimd",
                           rawap(wsl, 512 * k, [[5 * 512, 128], [64, 8], [1, 64]]),
                           rawap(dt[nm], 64 * h, [[H, 128], [128 * H, 8], [1, 64]]),
                           after=([t_wsl_free] if t_wsl_free else hg)
                           if k == 0 else ())
            _mk(f"h{h}_wsl")
            for (wofs, dst_ten, bcol, sclv) in ((0, Qx, 0, SCALE),
                                                (1, Kx, 1, 1.0)):
                for c in range(8):
                    for nk in range(4):
                        aft = ([tw, t_xn, t_bq, psfree["ctx"]]
                               if c == 0 and nk == 0 else ())
                        p.top("matmul", ps_ctx[0:64, 512 * nk:512 * (nk + 1)],
                              wsl[:, 512 * wofs + 64 * c:512 * wofs + 64 * (c + 1)],
                              xn_fm[:, S * c + 512 * nk:S * c + 512 * (nk + 1)],
                              start=(c == 0), stop=(c == 7), after=aft)
                tmm = p.tmark()
                tev = p.op("vector", "tensor_scalar", dst_ten[0:64, :],
                           ps_ctx[0:64, :], float(sclv),
                           bvec[0:64, 16 * bcol + h:16 * bcol + h + 1],
                           AluOpType.mult, AluOpType.add, after=[tmm] + hg)
                psfree["ctx"] = tev
                if wofs == 0:
                    t_qf = tev
                else:
                    t_kf = tev
            tqc1 = p.op("vector", "tensor_copy", QxH[0:64, :], Qx[0:64, :],
                        after=[t_qf])
            tqc2 = p.op("vector", "tensor_copy", QxL[0:64, :], Qx[0:64, :])
            tap(f"qx_{b}_{h}", Qx[0:64, :], [t_qf])
            tap(f"kx_{b}_{h}", Kx[0:64, :], [t_kf])

            _mk(f"h{h}_qk")
            for nk in range(4):
                for c in range(8):
                    aft = ([t_kf, psfree["ctx"]] + hg if nk == 0 and c == 0
                           else ())
                    p.top("matmul", ps_ctx[0:64, 512 * nk:512 * (nk + 1)],
                          wsl[:, 2 * 512 + 64 * c:2 * 512 + 64 * (c + 1)],
                          xn_fm[:, S * c + 512 * nk:S * c + 512 * (nk + 1)],
                          start=(c == 0), stop=(c == 7), after=aft)
            tmm = p.tmark()
            if b > 0:
                t_wsl_free = tmm
            vd_st = ctx_h
            t_vd = p.op("vector", "tensor_scalar", vd_st[0:64, :],
                        ps_ctx[0:64, :], 1.0,
                        bvec[0:64, 32 + h:32 + h + 1],
                        AluOpType.mult, AluOpType.add, after=[tmm] + hg)
            psfree["ctx"] = t_vd
            for t in range(NT):
                aft = [t_vd, t_id, psfree["t"]] if t == 0 else ()
                p.top("matmul", ps_t[:, 64 * t:64 * (t + 1)],
                      vd_st[0:64, 128 * t:128 * (t + 1)], identb[0:64, 0:64],
                      start=(t == 0), stop=(t == NT - 1),
                      is_transpose=True, after=aft)
            t_vd_read = p.tmark()
            t_vev = p.op("scalar", "copy",
                         rawap(v_aug, 0,
                               [[NT * (HD + 1), 128], [HD + 1, NT], [1, HD]]),
                         ps_t[:, 0:NT * HD], after=[t_vd_read, t_vone])
            psfree["t"] = t_vev
            tap(f"vaug_{b}_{h}", v_aug[:, :], [t_vev])

            _mk(f"h{h}_v")
            if b > 0:
                t_tabR = p.dma("sync", tabR[:],
                               tabs_s[64 * h:64 * h + 64, 0:WBAND],
                               after=hg + [tabstore_tok[h]])
                t_tabF = p.dma("sync", tabF[:],
                               tabs_s[64 * h:64 * h + 64, WBAND:2 * WBAND])
                tap(f"tabR_{b}_{h}", tabR[:], [t_tabR])
                tap(f"tabF_{b}_{h}", tabF[:], [t_tabF])
            for (wofs, dstt, bcol, key) in ((3, pkpm, 48, "a"),
                                            (4, pqpm, 64, "b")) if b == 0 \
                    else ():
                pst = ps_a if key == "a" else ps_b
                pofs = 512 * (wofs - 3)
                pdst = bc_band[0:64, pofs:pofs + 512]
                for c in range(8):
                    aft = ([t_refm, tw, psfree[key]] + hg if c == 0 else ())
                    p.top("matmul", pst[0:64, 0:512],
                          wsl[:, 512 * wofs + 64 * c:512 * wofs + 64 * (c + 1)],
                          re_fm[:, 512 * c:512 * (c + 1)],
                          start=(c == 0), stop=(c == 7), after=aft)
                tmm = p.tmark()
                t_wsl_free = tmm
                t_pd = p.op("vector", "tensor_scalar", pdst,
                            pst[0:64, 0:512], 1.0,
                            bvec[0:64, bcol + h:bcol + h + 1],
                            AluOpType.mult, AluOpType.add, after=[tmm] + hg)
                psfree[key] = t_pd
                for pc in range(4):
                    aft = [t_pd, psfree["t2"]] if pc == 0 else ()
                    p.top("matmul", ps_t2[:, 64 * pc:64 * (pc + 1)],
                          bc_band[0:64, pofs + 128 * pc:pofs + 128 * (pc + 1)],
                          identb[0:64, 0:64], start=(pc == 0), stop=(pc == 3),
                          is_transpose=True, after=aft)
                tt = p.tmark()
                t_pm = p.op("scalar", "copy", dstt[:, 0:256],
                            ps_t2[:, 0:256], after=[tt])
                psfree["t2"] = t_pm
                if wofs == 3:
                    t_pk = t_pm
                else:
                    t_pq = t_pm
            for (srct, gt, dstt, sclt, tprev) in (
                    (pkpm, grev4, tabR, 1.0, t_pk),
                    (pqpm, gfwd4, tabF, SCALE, t_pq)) if b == 0 else ():
                t_tab = None
                for uc in range(3):
                    n0 = 512 * uc
                    n1 = min(WBAND, n0 + 512)
                    key = "a" if uc % 2 == 0 else "b"
                    pst = ps_a if key == "a" else ps_b
                    for pc in range(4):
                        aft = ([tprev, t_gtab, psfree[key]] + hg
                               if pc == 0 else ())
                        p.top("matmul", pst[0:64, 0:n1 - n0],
                              srct[:, 64 * pc:64 * (pc + 1)],
                              gt[:, WBAND * pc + n0:WBAND * pc + n1],
                              start=(pc == 0), stop=(pc == 3), after=aft)
                    tmm = p.tmark()
                    t_tab = p.op("scalar", "activation", dstt[:, n0:n1],
                                 pst[0:64, 0:n1 - n0], AF.Copy,
                                 scale=float(sclt),
                                 after=[tmm, t_tab] if t_tab else [tmm])
                    psfree[key] = t_tab
                if sclt == 1.0:
                    t_tabR = t_tab
                else:
                    t_tabF = t_tab
            if b == 0:
                p.dma("scalar", tabs_s[64 * h:64 * h + 64, 0:WBAND],
                      tabR[:], after=[t_tabR])
                tabstore_tok[h] = p.dma(
                    "scalar", tabs_s[64 * h:64 * h + 64, WBAND:2 * WBAND],
                    tabF[:], after=[t_tabF])
                tap(f"tabR_{b}_{h}", tabR[:], [t_tabR])
                tap(f"tabF_{b}_{h}", tabF[:], [t_tabF])

            _mk(f"h{h}_tabs")
            t_sat = []
            t_rowfree = None
            for (tab, t_tt, ucolA, ucolB, opd, t_opd,
                 dstA, rowA, dstB, rowB) in (
                    (tabR, t_tabR, U_SATH_REV, U_SATL_REV, Qx, t_qf,
                     QxH, 66, QxL, 68),
                    (tabF, t_tabF, U_SATH_FWD, U_SATL_FWD, Kx, t_kf,
                     Kx, 65, Kx, 67)):
                c0, c1 = (ucolA, ucolB) if ucolA < ucolB else (ucolB, ucolA)
                rA = 0 if ucolA == c0 else 1
                trow = None
                for nk in range(4):
                    aft = ([t_tt, t_opd, psfree["a"]] if nk == 0
                           else [psfree["a"]])
                    p.top("matmul", ps_a[0:2, 0:512],
                          rawap(tab, c0, [[WBAND, 64], [c1 - c0, 2]]),
                          opd[0:64, 512 * nk:512 * (nk + 1)],
                          start=True, stop=True, after=aft)
                    tmm = p.tmark()
                    aft2 = [tmm] + ([t_rowfree] if t_rowfree and nk == 0
                                    else [])
                    trow = p.op("scalar", "copy",
                                rowbuf[0:2, 512 * nk:512 * (nk + 1)],
                                ps_a[0:2, 0:512], after=aft2)
                    psfree["a"] = trow
                tA = p.dma("sync", dstA[rowA:rowA + 1, :],
                           rowbuf[rA:rA + 1, :], after=[trow])
                t_rowfree = p.dma("sync", dstB[rowB:rowB + 1, :],
                                  rowbuf[1 - rA:2 - rA, :], after=[trow])
                t_sat += [tA, t_rowfree]

            _mk(f"h{h}_sat")
            t1 = p.op("vector", "reduce_max", scal2[0:64, 2:3], Kx[0:64, :],
                      AX, apply_absolute_value=True, after=[t_kf])
            p.top("matmul", ps_a[0:1, 8:72], scal2[0:64, 2:3],
                  identf[0:64, 0:64], start=True, stop=True, is_transpose=True,
                  after=[t1, t_if, psfree["a"]])
            tmm = p.tmark()
            t2 = p.op("vector", "reduce_max", scal[0:1, 5:6], ps_a[0:1, 8:72],
                      AX, after=[tmm])
            psfree["a"] = t2
            tnorm = {}
            t_qsq_free = None
            for (tab, col, t_tt) in (((tabR, 8, t_tabR), (tabF, 10, t_tabF))
                                     if b == 0 else ()):
                tsq = p.op("scalar", "activation", qsq[0:64, 0:WBAND], tab[:],
                           AF.Square,
                           after=[t_tt] + t_sat + ([t_qsq_free]
                                                   if t_qsq_free else []))
                tmx = None
                for uc in range(3):
                    n0 = 512 * uc
                    n1 = min(WBAND, n0 + 512)
                    p.top("matmul", ps_a[0:1, 0:n1 - n0], ones_b64[:],
                          qsq[0:64, n0:n1], start=True, stop=True,
                          after=[tsq, psfree["a"]] if uc == 0
                          else [psfree["a"]])
                    tmm = p.tmark()
                    tr = p.op("vector", "reduce_max", scal[0:1, 6:7],
                              ps_a[0:1, 0:n1 - n0], AX, after=[tmm])
                    psfree["a"] = tr
                    if tmx is None:
                        tmx = p.op("vector", "tensor_copy",
                                   scal[0:1, col:col + 1], scal[0:1, 6:7],
                                   after=[tr])
                    else:
                        tmx = p.op("vector", "tensor_tensor",
                                   scal[0:1, col:col + 1],
                                   scal[0:1, col:col + 1], scal[0:1, 6:7],
                                   AluOpType.max, after=[tr])
                t_qsq_free = p.tmark()
                nbc = 2 * h + (0 if col == 8 else 1)
                tnorm[col] = p.op("scalar", "activation",
                                  normbuf[0:1, nbc:nbc + 1],
                                  scal[0:1, col:col + 1], AF.Sqrt,
                                  after=[tmx] +
                                  ([normtok[nbc]] if normtok[nbc] else []))
                normtok[nbc] = tnorm[col]
            tn8 = normtok[2 * h]
            tn10 = normtok[2 * h + 1]
            p.op("vector", "tensor_tensor", scal[0:1, 11:12], scal[0:1, 5:6],
                 normbuf[0:1, 2 * h:2 * h + 1], AluOpType.add,
                 after=[tn8, t2])
            tm1 = p.op("vector", "tensor_tensor", scal[0:1, 11:12],
                       scal[0:1, 11:12], scal[0:1, 11:12], AluOpType.mult)
            p.op("vector", "scalar_tensor_tensor", scal[0:1, 12:13],
                 scal[0:1, 5:6], 1.0, normbuf[0:1, 2 * h + 1:2 * h + 2],
                 AluOpType.bypass, AluOpType.mult, after=[tn10])
            tm2 = p.op("vector", "tensor_scalar_add", scal[0:1, 12:13],
                       scal[0:1, 12:13], 1.0)
            tqs = p.op("scalar", "activation", qsq[0:64, 0:S], Qx[0:64, :],
                       AF.Square, after=[t_qf, t_qsq_free] + t_sat)
            t_mr = None
            for nk in range(4):
                p.top("matmul", ps_a[0:1, 0:512], ones_b64[:],
                      qsq[0:64, 512 * nk:512 * (nk + 1)],
                      start=True, stop=True,
                      after=[tqs, psfree["a"]] if nk == 0
                      else [psfree["a"]])
                tmm = p.tmark()
                t_mr = p.op("scalar", "activation",
                            den[0:1, 512 * nk:512 * (nk + 1)],
                            ps_a[0:1, 0:512], AF.Sqrt,
                            scale=scal[0:1, 11:12], after=[tmm, tm1])
                psfree["a"] = t_mr
            trb = p.op("vector", "tensor_scalar", rowbuf[0:1, :], den[0:1, :],
                       scal[0:1, 12:13], -1.0, AluOpType.add, AluOpType.mult,
                       after=[t_mr, tm2, t_rowfree])
            p.dma("sync", Qx[64:65, :], rowbuf[0:1, :], after=[trb])
            p.dma("sync", QxH[64:65, :], rowbuf[0:1, :], after=[tqc1])
            t_m = p.dma("sync", QxL[64:65, :], rowbuf[0:1, :], after=[tqc2])
            tap(f"mrow_{b}_{h}", den[0:1, :], [t_mr])

            _mk(f"h{h}_mrow")
            t_bc = None
            cfree = [psfree["ctx"]] * 4
            ridx = 0
            for qt in range(NT):
                for uc in range(3):
                    n0 = 512 * uc
                    n1 = min(WBAND, n0 + 512)
                    r = ridx % 4
                    ridx += 1
                    aft = [t_tabR, t_qf, cfree[r]]
                    p.top("matmul", ps_ctx[:, 512 * r:512 * r + n1 - n0],
                          Qx[0:64, 128 * qt:128 * (qt + 1)],
                          tabR[:, n0:n1], start=True, stop=True, after=aft)
                    tmm = p.tmark()
                    t_bc = p.op("vector" if uc == 1 else "scalar",
                                "tensor_copy" if uc == 1 else "copy",
                                bc_band[:, WBAND * qt + n0:WBAND * qt + n1],
                                ps_ctx[:, 512 * r:512 * r + n1 - n0],
                                after=[tmm])
                    cfree[r] = t_bc
            tap(f"bc_{b}_{h}", bc_band[:, 0:WBAND], [t_bc])

            def make_bp(kt, buf, rdtok=None, extra=()):
                tb = None
                for uc in range(3):
                    n0 = 512 * uc
                    n1 = min(WBAND, n0 + 512)
                    key = "a" if uc % 2 == 0 else "b"
                    pst = ps_a if key == "a" else ps_b
                    aft = list(extra) + [psfree[key], t_kf, t_tabF]
                    p.top("matmul", pst[:, 0:n1 - n0],
                          Kx[0:64, 128 * kt:128 * (kt + 1)],
                          tabF[:, n0:n1], start=True, stop=True, after=aft)
                    tmm = p.tmark()
                    if uc == 1:
                        tb = p.op("scalar", "copy", buf[:, n0:n1],
                                  pst[:, 0:n1 - n0], after=[tmm])
                    else:
                        tb = p.op("vector", "tensor_copy", buf[:, n0:n1],
                                  pst[:, 0:n1 - n0],
                                  after=[tmm, rdtok] if (rdtok and uc == 0)
                                  else [tmm])
                    psfree[key] = tb
                return tb

            bfree = {"ctile": [None, None], "p2cT": [None, None],
                     "pt": [None, None], "c2pT": [None, None],
                     "bp": [None, None], "bpdma": [None, None]}
            t_bp = [None, None]

            def prep_kt_dma(kt):
                i = kt % 2
                qlo, qhi = win(kt)
                nd = qhi - qlo
                srcv = rawap(bp_band[i], 639 + 128 * (qlo - kt),
                             [[WBAND - 1, 128], [1, 128 * nd]])
                tp2 = p.dma("sync", p2cT[i][:, 0:128 * nd], srcv,
                            after=[t_bp[i], bfree["p2cT"][i]])
                bfree["bpdma"][i] = tp2
                srcv = rawap(bc_band, WBAND * kt + 639 + 1152 * (qlo - kt),
                             [[WROW - 1, 128], [1152, nd], [1, 128]])
                tct = p.dma("sync", ctile[i][:, 0:128 * nd], srcv,
                            after=[t_bc, bfree["ctile"][i]])
                return tp2, tct

            def prep_kt_pe(kt, tp2, tct):
                i = kt % 2
                qlo, qhi = win(kt)
                nd = qhi - qlo
                tev = None
                for d in range(nd):
                    if d == 0:
                        aft = [tct, psfree["t2"]]
                    elif d % 8 == 0:
                        aft = [tev]
                    else:
                        aft = ()
                    p.top("matmul",
                          ps_t2[:, 128 * (d % 8):128 * (d % 8) + 128],
                          ctile[i][:, 128 * d:128 * (d + 1)], identb[:],
                          start=(d % 8 == 0),
                          stop=(d % 8 == 7 or d == nd - 1),
                          is_transpose=True, after=aft)
                    if d % 8 == 7 or d == nd - 1:
                        g0 = (d // 8) * 8
                        tt = p.tmark()
                        aft2 = [tt] + ([bfree["c2pT"][i]]
                                       if bfree["c2pT"][i] else [])
                        if FOLD_BANDS:
                            if g0 == 0:
                                aft2.append(tp2)
                            tev = p.op("vector", "tensor_tensor",
                                       c2pT[i][:, 128 * g0:128 * (d + 1)],
                                       ps_t2[:, 0:128 * (d + 1 - g0)],
                                       p2cT[i][:, 128 * g0:128 * (d + 1)],
                                       AluOpType.add, after=aft2)
                        else:
                            tev = p.op("scalar", "copy",
                                       c2pT[i][:, 128 * g0:128 * (d + 1)],
                                       ps_t2[:, 0:128 * (d + 1 - g0)],
                                       after=aft2)
                psfree["t2"] = tev
                bfree["ctile"][i] = p.tok("p_tensor")
                if FOLD_BANDS:
                    bfree["p2cT"][i] = tev
                return tp2, tev

            _mk(f"h{h}_bc")
            t_bp[0] = make_bp(0, bp_band[0])
            t_prep = [None, None]
            t_prep[0] = prep_kt_pe(0, *prep_kt_dma(0))

            for kt in range(NT):
                _mk(f"h{h}_kt{kt}")
                i = kt % 2
                qlo, qhi = win(kt)
                pt = pt_buf[i]
                tp2, tc2 = t_prep[i]
                if kt + 1 < NT:
                    j = (kt + 1) % 2
                    t_bp[j] = make_bp(kt + 1, bp_band[j],
                                      rdtok=bfree["bpdma"][j])
                    dma_next = prep_kt_dma(kt + 1)
                exp_toks = []
                for ch in range(4):
                    c0 = 512 * ch
                    key = "a" if ch % 2 == 0 else "b"
                    pst = ps_a if key == "a" else ps_b
                    qt0 = c0 // 128
                    segs = []
                    for qt in range(qt0, qt0 + 4):
                        r = "B" if qlo <= qt < qhi else ("L" if qt < qlo else "H")
                        if segs and segs[-1][0] == r:
                            segs[-1][2] = qt + 1
                        else:
                            segs.append([r, qt, qt + 1])
                    blo = max(c0, 128 * qlo)
                    bhi = min(c0 + 512, 128 * qhi)
                    fold = FOLD_BANDS and P2C_ON and C2P_ON
                    nband = ((1 if fold else int(P2C_ON) + int(C2P_ON))
                             if blo < bhi else 0)
                    nmm = len(segs) + nband
                    k = 0
                    aftc = [t_m, tp2, tc2, bfree["pt"][i], psfree[key],
                            t_cr, tqc1, tqc2] + t_sat
                    for (r, q0_, q1_) in segs:
                        n0 = 128 * q0_
                        n1 = 128 * q1_
                        if r == "B" or not SAT_ON:
                            lhsT = Kx[0:65, 128 * kt:128 * (kt + 1)]
                            rhs = Qx[0:65, n0:n1]
                        elif r == "L":
                            lhsT = Kx[0:69, 128 * kt:128 * (kt + 1)]
                            rhs = QxL[0:69, n0:n1]
                        else:
                            lhsT = Kx[0:67, 128 * kt:128 * (kt + 1)]
                            rhs = QxH[0:67, n0:n1]
                        k += 1
                        p.top("matmul", pst[:, n0 - c0:n1 - c0], lhsT, rhs,
                              start=(k == 1), stop=(k == nmm),
                              after=aftc if k == 1 else ())
                    if nband:
                        off = blo - 128 * qlo
                        nb_ = 1 if fold else int(P2C_ON) + int(C2P_ON)
                        if P2C_ON and not fold:
                            p.top("matmul", pst[:, blo - c0:bhi - c0], identb[:],
                                  p2cT[i][:, off:off + bhi - blo],
                                  start=False, stop=(nb_ == 1))
                        if C2P_ON:
                            p.top("matmul", pst[:, blo - c0:bhi - c0], identb[:],
                                  c2pT[i][:, off:off + bhi - blo],
                                  start=False, stop=True)
                        if nb_ == 0:
                            p.top("matmul", pst[:, c0 - c0:c0 - c0 + 1]
                                  if False else pst[:, 0:512],
                                  Kx[0:65, 128 * kt:128 * (kt + 1)],
                                  Qx[0:65, c0:c0 + 512],
                                  start=False, stop=True)
                    tmm = p.tmark()
                    tex = p.op("scalar", "activation", pt[:, c0:c0 + 512],
                               pst[:, 0:512], AF.Exp, after=[tmm])
                    psfree[key] = tex
                    exp_toks.append(tex)
                if not (FOLD_BANDS and P2C_ON and C2P_ON):
                    bfree["p2cT"][i] = p.tok("p_tensor")
                bfree["c2pT"][i] = p.tok("p_tensor")
                for ch in range(4):
                    aft = ([exp_toks[0], t_vev, t_vone]
                           if ch == 0 else [exp_toks[ch]])
                    if kt == 0:
                        aft = aft + [cfree[ch]]
                    p.top("matmul", ps_ctx[0:65, 512 * ch:512 * (ch + 1)],
                          v_aug[:, (HD + 1) * kt:(HD + 1) * (kt + 1)],
                          pt[:, 512 * ch:512 * (ch + 1)],
                          start=(kt == 0), stop=(kt == NT - 1), after=aft)
                tap(f"pt_{b}_{h}_{kt}", pt[:, :], exp_toks)
                bfree["pt"][i] = p.tmark()
                if kt + 1 < NT:
                    t_prep[j] = prep_kt_pe(kt + 1, *dma_next)

            _mk(f"h{h}_ctx")
            t_pv_done = bfree["pt"][(NT - 1) % 2]
            tden = p.op("scalar", "copy", den[64:65, :], ps_ctx[64:65, :],
                        after=[t_pv_done])
            tap(f"den_{b}_{h}", den[64:65, :], [tden])
            trec = p.op("vector", "reciprocal", den[64:65, :], den[64:65, :],
                        after=[tden])
            t_ctxh = None
            for ch in range(4):
                key = "a" if ch % 2 == 0 else "b"
                pst = ps_a if key == "a" else ps_b
                p.top("matmul", pst[0:64, 0:512], ones64[64:65, 0:64],
                      den[64:65, 512 * ch:512 * (ch + 1)],
                      start=True, stop=True,
                      after=[trec, t_o64, psfree[key]])
                tmm = p.tmark()
                aft_cb = [tmm] + ([t_ctxh] if t_ctxh else []) +                     ([shared["xsq"]] if shared["xsq"] else [])
                tcb = p.op("scalar", "copy", xsq[0:64, 0:512],
                           pst[0:64, 0:512], after=aft_cb)
                psfree[key] = tcb
                t_ctxh = p.op("vector", "tensor_tensor",
                              ctx_h[:, 512 * ch:512 * (ch + 1)],
                              ps_ctx[0:64, 512 * ch:512 * (ch + 1)],
                              xsq[0:64, 0:512], AluOpType.mult,
                              after=[tcb, t_ctxh] if t_ctxh else [tcb])
                shared["xsq"] = t_ctxh
            psfree["ctx"] = t_ctxh
            tap(f"ctxh_{b}_{h}", ctx_h[:, :], [t_ctxh])
            t_head_gate = p.dma("scalar", ctx_s[64 * h:64 * h + 64, :],
                                ctx_h[:, :], after=[t_ctxh])

        _mk(f"b{b}_wo")
        if WO_RUN:
            t_d_gate = p.tok("d_sync")
            t_bo = p.dma("sync", den[0:1, 0:H], rawap(dt["bo"], 0, [[H, 1], [1, H]]),
                         after=[t_head_gate])
            t_wo_loaded = None
            for c in range(8):
                t_wo_loaded = p.dma("gpsimd", wo_view[:, H * c:H * (c + 1)],
                                    dt["Wo"][128 * c:128 * (c + 1), :],
                                    after=[t_d_gate, t_head_gate] if c == 0 else ())
            t_x2 = None
            woc_free = [None, None]
            for t in range(NT):
                tlx = p.dma("scalar", xt_f[t % 2][:],
                            x_in[b * S + 128 * t:b * S + 128 * (t + 1), :],
                            after=[xtfree[t % 2]] if xtfree[t % 2] else [])
                tctx = p.dma("sync", rawap(wo_ctx[t % 2], 0,
                                           [[8 * 128, 128], [128, 8], [1, 128]]),
                             rawap(ctx_s, 128 * t, [[S, 128], [128 * S, 8], [1, 128]]),
                             after=[t_head_gate] + ([woc_free[t % 2]]
                                                    if woc_free[t % 2] else []))
                for nk in range(2):
                    key = "a" if nk == 0 else "b"
                    pst = ps_a if key == "a" else ps_b
                    for c in range(8):
                        aft = ([tctx, t_wo_loaded, psfree[key]]
                               if c == 0 else ())
                        p.top("matmul", pst[:, 0:512],
                              wo_ctx[t % 2][:, 128 * c:128 * (c + 1)],
                              wo_view[:, H * c + 512 * nk:H * c + 512 * (nk + 1)],
                              start=(c == 0), stop=False, after=aft)
                    p.top("matmul", pst[:, 0:512], ones_fr[0:1, :],
                          den[0:1, 512 * nk:512 * (nk + 1)],
                          start=False, stop=True,
                          after=[t_bo] if t == 0 and nk == 0 else ())
                tmm = p.tmark()
                ta = p.op("vector", "tensor_tensor", xt_f[t % 2][:, 0:512],
                          xt_f[t % 2][:, 0:512], ps_a[:, 0:512], AluOpType.add,
                          after=[tmm, tlx])
                tb = p.op("vector", "tensor_tensor", xt_f[t % 2][:, 512:1024],
                          xt_f[t % 2][:, 512:1024], ps_b[:, 0:512],
                          AluOpType.add)
                psfree["a"] = tb
                psfree["b"] = tb
                woc_free[t % 2] = p.tmark()
                xtfree[t % 2] = tb
                t_x2 = p.dma("sync", x2_s[b * S + 128 * t:b * S + 128 * (t + 1), :],
                             xt_f[t % 2][:], after=[tb])
                xtfree[t % 2] = t_x2

        else:
            t_x2 = t_head_gate
        t_batch_gate = t_x2

    gates = p.engine_gates()

    es_attn.close()

    if FFN_RUN:
        _mk("ffnA")
        es_fa = contextlib.ExitStack()
        wgT = sb_in(es_fa, "wgT", [128, 8 * INNER], BF16)
        wuT = sb_in(es_fa, "wuT", [128, 8 * INNER], BF16)
        gu = sb_in(es_fa, "gu", [128, INNER], F32)
        xn2 = sb_in(es_fa, "xn2", [128, H], F32)
        yq = sb_in(es_fa, "yq", [128, 1024], F32)
        qint = sb_in(es_fa, "qint", [128, 1024], I32)
        qflt = sb_in(es_fa, "qflt", [128, 1024], F32)
        gq1 = sb_in(es_fa, "gq1", [128, 1024], F32)
        gq2 = qint.bitcast(F32)
        xq_tok = sb_in(es_fa, "xq_tok", [128, H], BF16)
        xqT = sb_in(es_fa, "xqT", [128, H], BF16)
        guq_tok = sb_in(es_fa, "guq_tok", [128, INNER], BF16)
        stage2 = sb_in(es_fa, "stage2", [128, INNER], BF16)
        sgate = xsq
        dscv = sb_in(es_fa, "dscv", [128, 8], F32)
        ps_g = [es_fa.enter_context(nc.psum_tensor(f"ps_g{i}", [128, 512], F32))
                for i in range(2)]
        ps_u = [es_fa.enter_context(nc.psum_tensor(f"ps_u{i}", [128, 512], F32))
                for i in range(2)]

        tsw = None
        for col in (2, 3, 4):
            aft = (gates + [tw_g, tw_u, tw_d, psfree["a"]] if col == 2
                   else [psfree["a"]])
            p.top("matmul", ps_a[:, 4:5], ones_fr[:],
                  scal[0:1, col:col + 1], start=True, stop=True, after=aft)
            tmm = p.tmark()
            tsw = p.op("vector", "tensor_copy", thrv[:, col:col + 1],
                       ps_a[:, 4:5], after=[tmm])
            psfree["a"] = tsw

        t_wres = None
        for c in range(8):
            p.dma("sync", wgT[:, INNER * c:INNER * (c + 1)],
                  wgT_s[128 * c:128 * (c + 1), :],
                  after=gates + [tw_g] if c == 0 else [])
            t_wres = p.dma("sync", wuT[:, INNER * c:INNER * (c + 1)],
                           wuT_s[128 * c:128 * (c + 1), :])

        qshared = [None]
        qscale_war = [None]
        qfin = [None]

        def quantize(src, W, out_bf, t_src, dcol, chunk_toks=None,
                     scale_war=None, out_war=None):
            if chunk_toks is None:
                chunk_toks = []
            t0 = p.op("vector", "reduce_max", dscv[:, 0:1], src[:, 0:W], AX,
                      apply_absolute_value=True,
                      after=[t_src] + ([qscale_war[0]] if qscale_war[0]
                                       else []))
            t1a = p.op("vector", "tensor_single_scalar", dscv[:, 0:1],
                       dscv[:, 0:1], 1e-5, AluOpType.max, after=[t0])
            t1 = p.op("vector", "tensor_scalar_mul", dscv[:, dcol:dcol + 1],
                      dscv[:, 0:1], 1.0 / 127.0,
                      after=[scale_war] if scale_war else ())
            t2r = p.op("vector", "reciprocal", dscv[:, 1:2], dscv[:, 0:1])
            t2 = p.op("vector", "tensor_scalar_mul", dscv[:, 1:2],
                      dscv[:, 1:2], 127.0)
            qscale_war[0] = t2
            tq = None
            tzp = qshared[0]
            for q0 in range(0, W, 1024):
                ty = p.op("vector", "tensor_scalar_mul", yq[:, 0:1024],
                          src[:, q0:q0 + 1024], dscv[:, 1:2],
                          after=[t2] + ([tzp] if tzp else [])
                          + ([tq] if tq else ([qfin[0]] if qfin[0] else [])))
                tz = p.op("scalar", "copy", qint[:, 0:1024], yq[:, 0:1024],
                          after=[ty])
                tzp = tz
                tzf = p.op("scalar", "copy", qflt[:, 0:1024], qint[:, 0:1024])
                tg1 = p.op("vector", "scalar_tensor_tensor", gq1[:, 0:1024],
                           qflt[:, 0:1024], 0.5, yq[:, 0:1024], AluOpType.add,
                           AluOpType.is_le, after=[tzf])
                tg2 = p.op("vector", "scalar_tensor_tensor", gq2[:, 0:1024],
                           qflt[:, 0:1024], -0.5, yq[:, 0:1024], AluOpType.add,
                           AluOpType.is_ge)
                td = p.op("gpsimd", "tensor_tensor", gq1[:, 0:1024],
                          gq1[:, 0:1024], gq2[:, 0:1024], AluOpType.subtract,
                          after=[tg1, tg2])
                tq = p.op("gpsimd", "tensor_tensor", out_bf[:, q0:q0 + 1024],
                          qflt[:, 0:1024], gq1[:, 0:1024], AluOpType.add,
                          after=[td] + ([out_war] if out_war and q0 == 0
                                        else []))
                chunk_toks.append(tq)
            qshared[0] = tzp
            qfin[0] = tq
            return tq

        stage_free = None
        pipe = {"tsg": None, "tumm": None, "tdu": None, "xq_tt": None,
                "tds": None, "guq_tt": None}

        def ffn_front(t):
            aft = [xtfree[t % 2]] if xtfree[t % 2] else []
            if t == 0:
                aft = aft + gates
            tl = p.dma("sync", xt_f[t % 2][:], x2_s[128 * t:128 * (t + 1), :],
                       after=aft)
            tln = layernorm(xt_f[t % 2][:], xn2[:], H, after=[tl],
                            apply_after=[pipe["tqx"]] if pipe.get("tqx")
                            else ())
            xtfree[t % 2] = tln
            tqx = quantize(xn2, H, xq_tok, tln, 5,
                           scale_war=pipe["tdu"], out_war=pipe["xq_tt"])
            pipe["tqx"] = tqx
            for c in range(8):
                aft2 = [tqx, psfree["t"]] if c == 0 else ()
                p.top("matmul", ps_t[:, 128 * c:128 * (c + 1)],
                      xq_tok[:, 128 * c:128 * (c + 1)], identb[:],
                      start=(c == 0), stop=(c == 7), is_transpose=True,
                      after=aft2)
            tt2 = p.tmark()
            pipe["xq_tt"] = tt2
            txq = p.op("scalar", "copy", xqT[:], ps_t[:, 0:1024],
                       after=[tt2, pipe["tumm"]] if pipe["tumm"] else [tt2])
            psfree["t"] = txq
            return tqx, txq

        nxt_front = ffn_front(0)
        for t in range(NB * NT):
            tqx, txq = nxt_front
            tdg = p.op("vector", "tensor_tensor", dscv[:, 6:7], dscv[:, 5:6],
                       thrv[:, 2:3], AluOpType.mult,
                       after=[tqx, tsw] +
                       ([pipe["tsg"]] if pipe["tsg"] else []))
            tdu = p.op("vector", "tensor_tensor", dscv[:, 7:8], dscv[:, 5:6],
                       thrv[:, 3:4], AluOpType.mult)
            pipe["tdu"] = tdu
            t_gu = None
            t_gufree = [[], []]
            for ic in range(8):
                pg, pu = ps_g[ic % 2], ps_u[ic % 2]
                for c in range(8):
                    aft = (([txq, t_wres] if ic < 2 else [])
                           + t_gufree[ic % 2] if c == 0 else ())
                    p.top("matmul", pg[:, 0:512],
                          xqT[:, 128 * c:128 * (c + 1)],
                          wgT[:, INNER * c + 512 * ic:INNER * c + 512 * (ic + 1)],
                          start=(c == 0), stop=(c == 7), after=aft)
                tg_mm = p.tmark()
                for c in range(8):
                    p.top("matmul", pu[:, 0:512],
                          xqT[:, 128 * c:128 * (c + 1)],
                          wuT[:, INNER * c + 512 * ic:INNER * c + 512 * (ic + 1)],
                          start=(c == 0), stop=(c == 7))
                tu_mm = p.tmark()
                tsg = p.op("scalar", "activation", sgate[:, 0:512],
                           pg[:, 0:512],
                           AF.Silu, scale=dscv[:, 6:7],
                           after=[tg_mm, tdu] +
                           ([t_gu] if t_gu else
                            ([shared["xsq"]] if shared["xsq"] else [])))
                tsu = p.op("vector", "tensor_scalar_mul",
                           gu[:, 512 * ic:512 * (ic + 1)], pu[:, 0:512],
                           dscv[:, 7:8],
                           after=[tu_mm] + ([t_gu] if t_gu else []))
                t_gufree[ic % 2] = [tsg, tsu]
                t_gu = p.op("vector", "tensor_tensor",
                            gu[:, 512 * ic:512 * (ic + 1)],
                            gu[:, 512 * ic:512 * (ic + 1)], sgate[:, 0:512],
                            AluOpType.mult, after=[tsg, tsu])
                if ic == 7:
                    pipe["tsg"] = tsg
                    pipe["tumm"] = tu_mm
            shared["xsq"] = t_gu
            if t + 1 < NB * NT:
                nxt_front = ffn_front(t + 1)
            guq_toks = []
            tqg = quantize(gu, INNER, guq_tok, t_gu, 3, chunk_toks=guq_toks,
                           scale_war=pipe["tds"], out_war=pipe["guq_tt"])
            tds = p.op("vector", "tensor_copy", dsc_all[:, t:t + 1],
                       dscv[:, 3:4], after=[tqg])
            pipe["tds"] = tds
            for g0 in range(0, 32, 8):
                key = "t" if (g0 // 8) % 2 == 0 else "t2"
                pst = ps_t if key == "t" else ps_t2
                for c in range(g0, g0 + 8):
                    aft = [guq_toks[g0 // 8], psfree[key]] if c == g0 else ()
                    p.top("matmul", pst[:, 128 * (c - g0):128 * (c - g0) + 128],
                          guq_tok[:, 128 * c:128 * (c + 1)], identb[:],
                          start=(c == g0), stop=(c == g0 + 7),
                          is_transpose=True, after=aft)
                tt = p.tmark()
                if g0 == 24:
                    pipe["guq_tt"] = tt
                tev = p.op("scalar", "copy", stage2[:, 128 * g0:128 * (g0 + 8)],
                           pst[:, 0:1024],
                           after=[tt, stage_free] if stage_free else [tt])
                psfree[key] = tev
            srcv = rawap(stage2, 0, [[INNER, 128], [128, 32], [1, 128]])
            dstv = rawap(guq_s, t * (128 * INNER),
                         [[128, 128], [128 * 128, 32], [1, 128]])
            stage_free = p.dma("sync", dstv, srcv, after=[tev])
        t_a_gates = p.engine_gates()
        es_fa.close()

        _mk("ffnB")
        es_fb = contextlib.ExitStack()
        wdres = sb_in(es_fb, "wdres", [128, 32 * H], BF16)
        guqT_in = [sb_in(es_fb, f"guqT_in{i}", [128, INNER], BF16)
                   for i in range(2)]
        ps_d = es_fb.enter_context(nc.psum_tensor("ps_d", [128, 1024], F32))

        t_wd2 = None
        for c in range(32):
            t_wd2 = p.dma("sync", wdres[:, H * c:H * (c + 1)],
                          wdT_s[128 * c:128 * (c + 1), :],
                          after=t_a_gates + [tw_d] if c == 0 else [])
        ginfree = [None, None]
        t_psd_free = None
        for t in range(NB * NT):
            tg = p.dma("sync", guqT_in[t % 2][:],
                       rawap(guq_s, t * (128 * INNER),
                             [[128, 128], [16384, 32], [1, 128]]),
                       after=[ginfree[t % 2]] if ginfree[t % 2] else [])
            aft = [xtfree[t % 2]] if xtfree[t % 2] else []
            if t == 0:
                aft = aft + t_a_gates
            tlx = p.dma("scalar", xt_f[t % 2][:], x2_s[128 * t:128 * (t + 1), :],
                        after=aft)
            for nk in range(2):
                for c in range(32):
                    aft = [a for a in (tg, t_wd2, t_psd_free) if a]
                    p.top("matmul", ps_d[:, 512 * nk:512 * (nk + 1)],
                          guqT_in[t % 2][:, 128 * c:128 * (c + 1)],
                          wdres[:, H * c + 512 * nk:H * c + 512 * (nk + 1)],
                          start=(c == 0), stop=(c == 31),
                          after=aft if nk == 0 and c == 0 else ())
            tmm = p.tmark()
            ginfree[t % 2] = tmm
            tsc = p.op("vector", "tensor_tensor", dscv[0:128, 0:1]
                       if False else scal2[:, 3:4], dsc_all[:, t:t + 1],
                       thrv[:, 4:5], AluOpType.mult, after=[tmm, tsw])
            ta = p.op("vector", "scalar_tensor_tensor", xt_f[t % 2][:, 0:512],
                      ps_d[:, 0:512], scal2[:, 3:4], xt_f[t % 2][:, 0:512],
                      AluOpType.mult, AluOpType.add, after=[tsc, tlx])
            tb = p.op("vector", "scalar_tensor_tensor", xt_f[t % 2][:, 512:1024],
                      ps_d[:, 512:1024], scal2[:, 3:4], xt_f[t % 2][:, 512:1024],
                      AluOpType.mult, AluOpType.add)
            t_psd_free = tb
            tst = p.dma("sync", out_hbm[128 * t:128 * (t + 1), :],
                        xt_f[t % 2][:], after=[tb])
            xtfree[t % 2] = tst


    final_d = p.tok("d_sync")
    final_ds = p.tok("d_scalar")
    final_dg = p.tok("d_gpsimd")
    if TRUNC is not None:
        p.steps = p.steps[:TRUNC]

    with nc.Block() as block:
        @block.sync
        def _(sync):
            for e, fn in p.steps:
                if e == "sync":
                    fn(sync)
            if TRUNC is None:
                sync.wait_ge(p.sem["d_sync"], final_d[1])
                sync.wait_ge(p.sem["d_scalar"], final_ds[1])
                sync.wait_ge(p.sem["d_gpsimd"], final_dg[1])

        @block.vector
        def _(vector):
            for e, fn in p.steps:
                if e == "vector":
                    fn(vector)

        @block.scalar
        def _(scalar):
            for e, fn in p.steps:
                if e == "scalar":
                    fn(scalar)

        @block.tensor
        def _(tensor):
            for e, fn in p.steps:
                if e == "tensor":
                    fn(tensor)

        @block.gpsimd
        def _(gpsimd):
            for e, fn in p.steps:
                if e == "gpsimd":
                    fn(gpsimd)

    if FFN_RUN:
        es_fb.close()
    p.es.close()
    return nc


_last_results = None


def kernel(**inputs):
    global _last_results
    import ml_dtypes
    g_rev, g_fwd = _g_tables()
    ident = np.eye(128, dtype=np.float32)

    x = np.ascontiguousarray(np.asarray(inputs["L0_attn_hidden_states"],
                                        np.float32))
    base = {
        "Wq": np.ascontiguousarray(inputs["L0_attn_Wq"], dtype=np.float32),
        "bq": np.ascontiguousarray(inputs["L0_attn_bq"], dtype=np.float32),
        "Wk": np.ascontiguousarray(inputs["L0_attn_Wk"], dtype=np.float32),
        "bk": np.ascontiguousarray(inputs["L0_attn_bk"], dtype=np.float32),
        "Wv": np.ascontiguousarray(inputs["L0_attn_Wv"], dtype=np.float32),
        "bv": np.ascontiguousarray(inputs["L0_attn_bv"], dtype=np.float32),
        "Wo": np.ascontiguousarray(inputs["L0_attn_Wo"], dtype=np.float32),
        "bo": np.ascontiguousarray(inputs["L0_attn_bo"], dtype=np.float32),
        "Wpk": np.ascontiguousarray(inputs["L0_attn_Wpk"], dtype=np.float32),
        "bpk": np.ascontiguousarray(inputs["L0_attn_bpk"], dtype=np.float32),
        "Wpq": np.ascontiguousarray(inputs["L0_attn_Wpq"], dtype=np.float32),
        "bpq": np.ascontiguousarray(inputs["L0_attn_bpq"], dtype=np.float32),
        "rel": np.ascontiguousarray(inputs["L0_attn_rel_emb"], dtype=np.float32),
        "ln_g": np.ascontiguousarray(inputs["L0_attn_ln_g"], dtype=np.float32),
        "ln_b": np.ascontiguousarray(inputs["L0_attn_ln_b"], dtype=np.float32),
        "wg": np.ascontiguousarray(inputs["L0_ffn_w_gate"], dtype=np.float32),
        "wu": np.ascontiguousarray(inputs["L0_ffn_w_up"], dtype=np.float32),
        "wd": np.ascontiguousarray(inputs["L0_ffn_w_down"], dtype=np.float32),
        "Grev": g_rev, "Gfwd": g_fwd,
        "identb": ident.astype(ml_dtypes.bfloat16), "identf": ident,
        "bqk_t": np.ascontiguousarray(np.concatenate(
            [np.asarray(inputs["L0_attn_" + nm], np.float32).reshape(16, 64).T
             for nm in ("bq", "bk", "bv", "bpk", "bpq")], axis=1)),
    }
    in_maps = []
    for i in range(NCORE):
        m = dict(base)
        m["x"] = np.ascontiguousarray(x[NB * i:NB * (i + 1)].reshape(NB * S, H))
        in_maps.append(m)

    nc = build_nc()
    res = run_bass_kernel_spmd(nc, in_maps, core_ids=list(range(NCORE)))
    _last_results = res
    return np.concatenate(
        [res.results[i]["out"].reshape(NB, S, H) for i in range(NCORE)],
        axis=0)


# version 08
# speedup vs baseline: 1.0062x; 1.0062x over previous
import math
import contextlib
import numpy as np

import concourse.bass as bass
import concourse.mybir as mybir
from concourse.bass_utils import run_bass_kernel_spmd
from concourse.alu_op_type import AluOpType

F32 = mybir.dt.float32
BF16 = mybir.dt.bfloat16
I32 = mybir.dt.int32
AF = mybir.ActivationFunctionType
AX = mybir.AxisListType.X

H = 1024
NH = 16
HD = 64
S = 2048
NT = 16
NB = 2
INNER = 4096
SPAN = 256
SCALE = 1.0 / math.sqrt(HD * 3.0)
WBAND = 1280
DMAX = 4
NCORE = 8
WROW = NT * WBAND

TAPS = {}
NH_RUN = NH
NB_RUN = NB
FFN_RUN = True
WO_RUN = True
C2P_ON = True
P2C_ON = True
SAT_ON = True
PREP_RUN = True
REL_RUN = True
LN1_RUN = True
TRUNC = None
MARKS = []
FOLD_BANDS = True


def _cidx(r):
    r = np.asarray(r, np.int64)
    sign = np.sign(r).astype(np.float64)
    mid = 128
    abs_pos = np.where((r < mid) & (r > -mid), float(mid - 1),
                       np.abs(r)).astype(np.float64)
    log_pos = np.ceil(
        np.log(abs_pos / mid)
        / np.float64(np.float32(np.log((512 - 1) / mid))) * (mid - 1)) + mid
    bucket = np.where(abs_pos <= mid, r.astype(np.float64), log_pos * sign)
    return np.clip(bucket.astype(np.int64) + SPAN, 0, 2 * SPAN - 1)


def _g_tables():
    import ml_dtypes
    u = np.arange(WBAND)
    g_rev = np.zeros((2 * SPAN, WBAND), np.float32)
    g_fwd = np.zeros((2 * SPAN, WBAND), np.float32)
    g_rev[_cidx(639 - u), u] = 1.0
    g_fwd[_cidx(u - 639), u] = 1.0
    return (g_rev.astype(ml_dtypes.bfloat16), g_fwd.astype(ml_dtypes.bfloat16))


_UREV = _cidx(639 - np.arange(WBAND))
_UFWD = _cidx(np.arange(WBAND) - 639)
U_SATH_REV = int(np.argmax(_UREV == 511))
U_SATL_REV = int(np.argmax(_UREV == 0))
U_SATH_FWD = int(np.argmax(_UFWD == 511))
U_SATL_FWD = int(np.argmax(_UFWD == 0))


class Plan:
    def __init__(self, nc):
        self.nc = nc
        self.steps = []
        self.sem = {}
        self.cnt = {}
        self.es = contextlib.ExitStack()
        for s_ in ("d_sync", "d_scalar", "d_gpsimd", "p_vector", "p_scalar",
                   "p_tensor", "p_gpsimd"):
            self.sem[s_] = self.es.enter_context(nc.semaphore(s_))
            self.cnt[s_] = 0

    def emit(self, eng, fn):
        self.steps.append((eng, fn))

    def wait(self, eng, tok):
        if tok is None:
            return
        name, val = tok
        h = self.sem[name]
        self.emit(eng, lambda e, h=h, v=val: e.wait_ge(h, v))

    def tok(self, name):
        return (name, self.cnt[name])

    def dma(self, q, out, in_, after=(), slow=False):
        for t in after:
            self.wait(q, t)
        name = "d_" + q
        self.cnt[name] += 16
        h = self.sem[name]
        if slow:
            self.emit(q, lambda e, o=out, i=in_, h=h:
                      e.dma_start(out=o, in_=i,
                                  allow_slow_non_contiguous=True).then_inc(h, 16))
        else:
            self.emit(q, lambda e, o=out, i=in_, h=h:
                      e.dma_start(out=o, in_=i).then_inc(h, 16))
        return (name, self.cnt[name])

    def op(self, eng, method, *args, after=(), **kw):
        for t in after:
            self.wait(eng, t)
        name = "p_" + eng
        h = self.sem[name]
        pre = self.cnt[name]
        self.cnt[name] += 1
        v = self.cnt[name]

        def fn(e, m=method, a=args, k=kw, h=h, pre=pre):
            e.wait_ge(h, pre)
            getattr(e, m)(*a, **k).then_inc(h, 1)
        self.emit(eng, fn)
        return (name, v)

    def top(self, method, *args, after=(), **kw):
        for t in after:
            self.wait("tensor", t)
        cell = {"inc": 0}
        self._last_top = cell
        h = self.sem["p_tensor"]

        def fn(e, m=method, a=args, k=kw, cell=cell, h=h):
            inst = getattr(e, m)(*a, **k)
            if cell["inc"]:
                inst.then_inc(h, cell["inc"])
        self.emit("tensor", fn)

    def tmark(self):
        name = "p_tensor"
        cell = getattr(self, "_last_top", None)
        if cell is None:
            return (name, self.cnt[name])
        self.cnt[name] += 1
        cell["inc"] += 1
        return (name, self.cnt[name])

    def engine_gates(self):
        return [self.tok("p_vector"), self.tok("p_scalar"),
                self.tok("p_gpsimd"), self.tmark(),
                self.tok("d_sync"), self.tok("d_scalar"),
                self.tok("d_gpsimd")]


def rawap(t, offset, dims):
    ap = t.ap().copy() if callable(getattr(t, "ap", None)) else t.copy()
    ap.offset = offset
    while len(ap.ap) > 1:
        ap.ap.pop()
    ap.ap[0] = list(dims[0])
    for d in dims[1:]:
        ap.ap.append(list(d))
    return ap


def build_nc():
    nc = bass.Bass("TRN2", target_bir_lowering=False,
                   detect_race_conditions=False,
                   dynamic_dma_scratch_size=16384)
    p = Plan(nc)

    def _mk(lbl):
        MARKS.append((lbl, len(p.steps)))
    dt = {}

    def din(name, shape, dtype=F32):
        dt[name] = nc.dram_tensor(name, shape, dtype, kind="ExternalInput").ap()
        return dt[name]

    x_in = din("x", [NB * S, H])
    for nm in ("Wq", "Wk", "Wv", "Wo", "Wpk", "Wpq"):
        din(nm, [H, H])
    for nm in ("bq", "bk", "bv", "bo", "bpk", "bpq", "ln_g", "ln_b"):
        din(nm, [H])
    din("rel", [2 * SPAN, H])
    din("wg", [INNER, H])
    din("wu", [INNER, H])
    din("wd", [H, INNER])
    din("Grev", [2 * SPAN, WBAND], BF16)
    din("Gfwd", [2 * SPAN, WBAND], BF16)
    din("identb", [128, 128], BF16)
    din("identf", [128, 128], F32)
    din("bqk_t", [64, 80])

    out_hbm = nc.dram_tensor("out", [NB * S, H], F32, kind="ExternalOutput").ap()
    x2_s = nc.dram_tensor("x2_s", [NB * S, H], F32, kind="Internal").ap()
    tabs_s = nc.dram_tensor("tabs_s", [NH * 64, 2 * WBAND], BF16,
                            kind="Internal").ap()
    ctx_s = nc.dram_tensor("ctx_s", [H, S], BF16, kind="Internal").ap()
    wgT_s = nc.dram_tensor("wgT_s", [H, INNER], BF16, kind="Internal").ap()
    wuT_s = nc.dram_tensor("wuT_s", [H, INNER], BF16, kind="Internal").ap()
    wdT_s = nc.dram_tensor("wdT_s", [INNER, H], BF16, kind="Internal").ap()
    guq_s = nc.dram_tensor("guq_s", [NB * NT, 128 * INNER], BF16,
                           kind="Internal").ap()

    tap_out = {}

    def tap(tag, ap, after):
        if tag not in TAPS:
            return
        shape, dtyp = TAPS[tag]
        nm = "dbg_" + tag
        if nm not in tap_out:
            tap_out[nm] = nc.dram_tensor(nm, shape, dtyp,
                                         kind="ExternalOutput").ap()
        p.dma("sync", tap_out[nm], ap, after=after)

    def sb_in(stk, name, shape, dtyp):
        return stk.enter_context(nc.sbuf_tensor(name, shape, dtyp))

    es0 = p.es

    def sb(name, shape, dtyp):
        return sb_in(es0, name, shape, dtyp)

    identb = sb("identb_t", [128, 128], BF16)
    identf = sb("identf_t", [128, 128], F32)
    ones_bf = sb("ones_bf", [128, 128], BF16)
    ones_fr = sb("ones_fr", [1, 128], F32)
    ones64 = sb("ones64", [128, 64], F32)
    ones_b64 = sb("ones_b64", [64, 1], BF16)
    onesc_f = sb("onesc_f", [128, 1], F32)
    bvec = sb("bvec", [128, 5 * 16], F32)
    scal = sb("scal", [1, 16], F32)
    scal2 = sb("scal2", [128, 8], F32)
    st1 = sb("st1", [128, 8], F32)
    xsq = sb("xsq", [128, H], F32)
    xt_f = [sb(f"xt_f{i}", [128, H], F32) for i in range(2)]
    thrv = sb("thrv", [128, 8], F32)
    rowbuf = sb("rowbuf", [2, S], BF16)
    normbuf = sb("normbuf", [1, 2 * NH], F32)
    dsc_all = sb("dsc_all", [128, NB * NT], F32)

    ps_a = es0.enter_context(nc.psum_tensor("ps_a", [128, 512], F32))
    ps_b = es0.enter_context(nc.psum_tensor("ps_b", [128, 512], F32))
    ps_t = es0.enter_context(nc.psum_tensor("ps_t", [128, 1024], BF16))
    ps_t2 = es0.enter_context(nc.psum_tensor("ps_t2", [128, 1024], BF16))

    psfree = {"a": None, "b": None, "t": None, "t2": None, "ctx": None}
    xtfree = [None, None]

    t_id = p.dma("sync", identb[:], dt["identb"])
    t_if = p.dma("sync", identf[:], dt["identf"])
    p.dma("sync", bvec[0:64, 0:80], dt["bqk_t"])
    t_ldc = p.tok("d_sync")

    t_ones = p.op("vector", "memset", ones_bf[:], 1.0)
    p.op("vector", "memset", ones_fr[:], 1.0)
    t_o64 = p.op("vector", "memset", ones64[:], 1.0)
    p.op("vector", "memset", ones_b64[:], 1.0)
    t_onesf = p.op("vector", "memset", onesc_f[:], 1.0)
    t_bq = p.op("vector", "tensor_scalar_mul", bvec[0:64, 0:16],
                bvec[0:64, 0:16], SCALE, after=[t_ldc])
    t_c127 = p.op("vector", "memset", thrv[:, 5:6], 127.0)

    shared = {"xsq": None}

    def layernorm(xf, tgt, W, gamma=None, beta=None, after=(), apply_after=()):
        aft0 = list(after) + ([shared["xsq"]] if shared["xsq"] else [])
        t0 = p.op("scalar", "activation", xsq[:, 0:W], xf, AF.Square,
                  after=aft0)
        t1 = p.op("vector", "reduce_sum", st1[:, 0:1], xf, AX, after=after)
        t2 = p.op("vector", "reduce_sum", st1[:, 1:2], xsq[:, 0:W], AX,
                  after=[t0])
        shared["xsq"] = t2
        p.op("vector", "tensor_scalar_mul", st1[:, 2:3], st1[:, 0:1], 1.0 / W,
             after=[t1])
        p.op("vector", "tensor_scalar_mul", st1[:, 3:4], st1[:, 1:2], 1.0 / W,
             after=[t2])
        p.op("vector", "tensor_tensor", st1[:, 4:5], st1[:, 2:3], st1[:, 2:3],
             AluOpType.mult)
        tvar = p.op("vector", "scalar_tensor_tensor", st1[:, 5:6], st1[:, 3:4],
                    1e-5, st1[:, 4:5], AluOpType.add, AluOpType.subtract)
        t3a = p.op("scalar", "activation", st1[:, 4:5], st1[:, 5:6], AF.Sqrt,
                   after=[tvar])
        t3 = p.op("vector", "reciprocal", st1[:, 6:7], st1[:, 4:5],
                  after=[t3a])
        if gamma is None:
            return p.op("vector", "tensor_scalar", tgt, xf, st1[:, 2:3],
                        st1[:, 6:7], AluOpType.subtract, AluOpType.mult,
                        after=[t3] + list(apply_after))
        ta = p.op("vector", "tensor_scalar", xsq[:, 0:W], xf, st1[:, 2:3],
                  st1[:, 6:7], AluOpType.subtract, AluOpType.mult, after=[t3])
        tb = p.op("vector", "tensor_tensor", xsq[:, 0:W], xsq[:, 0:W], gamma,
                  AluOpType.mult, after=[ta])
        shared["xsq"] = tb
        return p.op("vector", "tensor_tensor", tgt, xsq[:, 0:W], beta,
                    AluOpType.add, after=[tb] + list(apply_after))

    es_prep = contextlib.ExitStack()
    wtile = [sb_in(es_prep, f"wtile{i}", [128, INNER], F32) for i in range(2)]
    g1big = sb_in(es_prep, "g1big", [128, INNER], F32)
    ternt = sb_in(es_prep, "ternt", [128, INNER], BF16)
    stage = sb_in(es_prep, "stage", [128, INNER], BF16)
    wtfree = [None, None]

    def prep_weight(wname, dst_hbm, nrow, ncol, scol):
        ntile = nrow // 128
        tacc = p.op("vector", "memset", scal2[:, 0:1], 0.0)
        for t in range(ntile):
            td = p.dma(("sync", "scalar", "gpsimd")[t % 3],
                       wtile[t % 2][:, 0:ncol],
                       dt[wname][128 * t:128 * (t + 1), :],
                       after=[wtfree[t % 2]] if wtfree[t % 2] else [])
            tr = p.op("vector", "reduce_sum", scal2[:, 1:2],
                      wtile[t % 2][:, 0:ncol], AX, apply_absolute_value=True,
                      after=[td])
            tacc = p.op("vector", "tensor_tensor", scal2[:, 0:1],
                        scal2[:, 0:1], scal2[:, 1:2], AluOpType.add,
                        after=[tr])
            wtfree[t % 2] = tacc
        p.top("matmul", ps_a[0:1, 0:1], onesc_f[:], scal2[:, 0:1],
              start=True, stop=True, after=[tacc, t_onesf, psfree["a"]])
        tm = p.tmark()
        t5 = p.op("vector", "tensor_scalar", scal[0:1, scol:scol + 1],
                  ps_a[0:1, 0:1], 1.0 / (nrow * ncol), 1e-5,
                  AluOpType.mult, AluOpType.max, after=[tm])
        t6 = p.op("vector", "tensor_scalar_mul", scal[0:1, 15:16],
                  scal[0:1, scol:scol + 1], 0.5)
        p.top("matmul", ps_a[:, 4:5], ones_fr[:],
              scal[0:1, 15:16], start=True, stop=True, after=[t6, t5])
        tb128 = p.tmark()
        tpos = p.op("vector", "tensor_copy", thrv[:, 0:1], ps_a[:, 4:5],
                    after=[tb128])
        psfree["a"] = tpos
        tneg = p.op("vector", "tensor_scalar_mul", thrv[:, 1:2], thrv[:, 0:1],
                    -1.0)
        nb = ncol // 128
        stage_free = None
        tern_free = [None]
        for t in range(ntile):
            td = p.dma(("sync", "scalar", "gpsimd")[t % 3],
                       wtile[t % 2][:, 0:ncol],
                       dt[wname][128 * t:128 * (t + 1), :],
                       after=[wtfree[t % 2]])
            tg2 = p.op("vector", "tensor_scalar", g1big[:, 0:ncol],
                       wtile[t % 2][:, 0:ncol], thrv[:, 1:2], None,
                       AluOpType.is_le, after=[td, tneg])
            tq = p.op("vector", "scalar_tensor_tensor", ternt[:, 0:ncol],
                      wtile[t % 2][:, 0:ncol], thrv[:, 0:1], g1big[:, 0:ncol],
                      AluOpType.is_ge, AluOpType.subtract,
                      after=[tg2, tpos] + ([tern_free[0]] if tern_free[0]
                                           else []))
            wtfree[t % 2] = tq
            tev = None
            for g0 in range(0, nb, 8):
                g1 = min(nb, g0 + 8)
                key = "t" if (g0 // 8) % 2 == 0 else "t2"
                pst = ps_t if key == "t" else ps_t2
                for c in range(g0, g1):
                    aft = [tq, t_id, psfree[key]] if c == g0 else ()
                    p.top("matmul",
                          pst[:, 128 * (c - g0):128 * (c - g0) + 128],
                          ternt[:, 128 * c:128 * (c + 1)], identb[:],
                          start=(c == g0), stop=(c == g1 - 1),
                          is_transpose=True, after=aft)
                tt = p.tmark()
                tev = p.op("scalar", "copy", stage[:, 128 * g0:128 * g1],
                           pst[:, 0:128 * (g1 - g0)],
                           after=[tt, stage_free] if stage_free else [tt])
                psfree[key] = tev
                tern_free[0] = tt
            dstv = rawap(dst_hbm, 128 * t,
                         [[nrow, 128], [128 * nrow, nb], [1, 128]])
            srcv = rawap(stage, 0, [[INNER, 128], [128, nb], [1, 128]])
            stage_free = p.dma("sync", dstv, srcv, after=[tev])
        return stage_free

    if PREP_RUN:
        tw_g = prep_weight("wg", wgT_s, INNER, H, 2)
        tw_u = prep_weight("wu", wuT_s, INNER, H, 3)
        tw_d = prep_weight("wd", wdT_s, H, INNER, 4)
    else:
        tw_g = tw_u = tw_d = p.tok("d_sync")
    t_prep_gates = p.engine_gates()
    es_prep.close()

    es_attn = contextlib.ExitStack()

    def sba(name, shape, dtyp):
        return sb_in(es_attn, name, shape, dtyp)

    grev4 = sba("grev4", [128, 4 * WBAND], BF16)
    gfwd4 = sba("gfwd4", [128, 4 * WBAND], BF16)
    re_fm = sba("re_fm", [128, 8 * 512], BF16)
    xn_fm = sba("xn_fm", [128, 8 * S], BF16)
    ctx_h = sba("ctx_h", [64, S], BF16)
    wo_ctx = [sba(f"wo_ctx{i}", [128, 8 * 128], BF16) for i in range(2)]
    Kx = sba("Kx", [128, S], BF16)
    Qx = sba("Qx", [128, S], BF16)
    QxH = sba("QxH", [128, S], BF16)
    QxL = sba("QxL", [128, S], BF16)
    v_aug = sba("v_aug", [128, NT * (HD + 1)], BF16)
    bc_band = sba("bc_band", [128, WROW], BF16)
    bp_band = [sba(f"bp_band{i}", [128, WBAND], BF16) for i in range(2)]
    pt_buf = [sba(f"pt{i}", [128, S], BF16) for i in range(2)]
    c2pT = [sba(f"c2pT{i}", [128, 9 * 128], BF16) for i in range(2)]
    p2cT = [sba(f"p2cT{i}", [128, 9 * 128], BF16) for i in range(2)]
    ctile = [sba(f"ctile{i}", [128, 9 * 128], BF16) for i in range(2)]
    tabR = sba("tabR", [64, WBAND], BF16)
    tabF = sba("tabF", [64, WBAND], BF16)
    pkpm = sba("pkpm", [128, 4 * 64], BF16)
    pqpm = sba("pqpm", [128, 4 * 64], BF16)
    wsl = sba("wslice", [128, 5 * 512], BF16)
    den = sba("den", [128, S], F32)
    mrow = den[0:1, :]
    xn_tok = sba("xn_tok", [128, H], BF16)
    ps_ctx = es_attn.enter_context(nc.psum_tensor("ps_ctx", [128, S], F32))
    qsq = ctx_h
    es_rel = contextlib.ExitStack()
    lnb_g = sb_in(es_rel, "lnb_g", [128, H], F32)
    lnb_b = sb_in(es_rel, "lnb_b", [128, H], F32)
    re_tok = sb_in(es_rel, "re_tok", [128, H], BF16)

    for c in range(4):
        p.dma("sync", grev4[:, c * WBAND:(c + 1) * WBAND],
              dt["Grev"][c * 128:(c + 1) * 128, :])
        t_gtab = p.dma("sync", gfwd4[:, c * WBAND:(c + 1) * WBAND],
                       dt["Gfwd"][c * 128:(c + 1) * 128, :])
    t_g0 = p.dma("sync", lnb_g[0:1, :], rawap(dt["ln_g"], 0, [[H, 1], [1, H]]))
    t_b0 = p.dma("sync", lnb_b[0:1, :], rawap(dt["ln_b"], 0, [[H, 1], [1, H]]))
    t_g = t_b = None
    for (t0_, buf) in ((t_g0, lnb_g), (t_b0, lnb_b)):
        for nk in range(2):
            key = "a" if nk == 0 else "b"
            pst = ps_a if nk == 0 else ps_b
            p.top("matmul", pst[:, 0:512], ones_fr[:],
                  buf[0:1, 512 * nk:512 * (nk + 1)],
                  start=True, stop=True, after=[t0_, psfree[key]])
            tmm = p.tmark()
            tkn = p.op("vector", "tensor_copy",
                       buf[:, 512 * nk:512 * (nk + 1)], pst[:, 0:512],
                       after=[tmm])
            psfree[key] = tkn
        if buf is lnb_g:
            t_g = tkn
        else:
            t_b = tkn

    t_refm = None
    for rt in range(4 if REL_RUN else 0):
        tl = p.dma("sync", xt_f[rt % 2][:],
                   dt["rel"][128 * rt:128 * (rt + 1), :],
                   after=[xtfree[rt % 2], t_prep_gates[3]]
                   if xtfree[rt % 2] else [t_prep_gates[3]])
        ta = layernorm(xt_f[rt % 2][:], re_tok[:], H, gamma=lnb_g[:],
                       beta=lnb_b[:], after=[tl, t_g, t_b],
                       apply_after=[t_refm] if t_refm else [])
        xtfree[rt % 2] = ta
        for c in range(8):
            aft = [ta, t_id, psfree["t"]] if c == 0 else ()
            p.top("matmul", ps_t[:, 128 * c:128 * (c + 1)],
                  re_tok[:, 128 * c:128 * (c + 1)], identb[:],
                  start=(c == 0), stop=(c == 7), is_transpose=True, after=aft)
        tt = p.tmark()
        t_refm = p.op("scalar", "copy",
                      rawap(re_fm, 128 * rt, [[8 * 512, 128], [512, 8], [1, 128]]),
                      ps_t[:, 0:1024],
                      after=[tt, t_refm] if t_refm else [tt])
        psfree["t"] = t_refm
    if not REL_RUN:
        t_refm = p.tok("d_sync")
    es_rel.close()

    def win(kt):
        return max(0, kt - DMAX), min(NT, kt + DMAX + 1)

    t_batch_gate = None
    t_wo_loaded = None
    t_wsl_free = None
    normtok = [None] * (2 * NH)
    tabstore_tok = [None] * NH
    wo_view = bc_band

    for b in range(NB_RUN):
        _mk(f"b{b}_ln1")
        t_xn = None
        xn_tt = [None]
        for t in range(NT if LN1_RUN else 0):
            aft = [xtfree[t % 2]] if xtfree[t % 2] else []
            if t_batch_gate:
                aft.append(t_batch_gate)
            tl = p.dma("sync", xt_f[t % 2][:],
                       x_in[b * S + 128 * t:b * S + 128 * (t + 1), :],
                       after=aft)
            ta = layernorm(xt_f[t % 2][:], xn_tok[:], H, after=[tl],
                           apply_after=[xn_tt[0]] if xn_tt[0] else [])
            xtfree[t % 2] = ta
            for c in range(8):
                aft = [ta, psfree["t"]] + ([t_batch_gate] if t_batch_gate else []) \
                    if c == 0 else ()
                p.top("matmul", ps_t[:, 128 * c:128 * (c + 1)],
                      xn_tok[:, 128 * c:128 * (c + 1)], identb[:],
                      start=(c == 0), stop=(c == 7), is_transpose=True, after=aft)
            tt = p.tmark()
            xn_tt[0] = tt
            t_xn = p.op("scalar", "copy",
                        rawap(xn_fm, 128 * t, [[8 * S, 128], [S, 8], [1, 128]]),
                        ps_t[:, 0:1024], after=[tt, t_xn] if t_xn else [tt])
            psfree["t"] = t_xn
        if not LN1_RUN:
            t_xn = p.tok("d_sync")
        tap(f"xnfm_{b}", xn_fm[:], [t_xn])

        if b == 0:
            tcr1 = p.op("vector", "memset", rowbuf[0:1, :], 1.0)
            tks = []
            for dst_ten, row in ((Kx, 64), (Kx, 66), (Kx, 68),
                                 (QxH, 65), (QxL, 67)):
                tks.append(p.dma("sync", dst_ten[row:row + 1, :],
                                 rowbuf[0:1, :], after=[tcr1]))
            tcr0 = p.op("vector", "memset", rowbuf[0:1, :], 0.0,
                        after=[tks[-1]])
            for dst_ten, row in ((QxL, 65), (QxL, 66)):
                tks.append(p.dma("sync", dst_ten[row:row + 1, :],
                                 rowbuf[0:1, :], after=[tcr0]))
            t_cr = tks[-1]
            t_vone = p.op("vector", "memset",
                          rawap(v_aug, HD, [[NT * (HD + 1), 128], [HD + 1, NT]]),
                          1.0)
        t_head_gate = t_batch_gate

        for h in range(NH_RUN):
            _mk(f"head{h}_start")
            hg = [t_head_gate] if t_head_gate else []
            for k, nm in enumerate(("Wq", "Wk", "Wv", "Wpk", "Wpq")):
                tw = p.dma("gpsimd",
                           rawap(wsl, 512 * k, [[5 * 512, 128], [64, 8], [1, 64]]),
                           rawap(dt[nm], 64 * h, [[H, 128], [128 * H, 8], [1, 64]]),
                           after=([t_wsl_free] if t_wsl_free else hg)
                           if k == 0 else ())
            _mk(f"h{h}_wsl")
            for (wofs, dst_ten, bcol, sclv) in ((0, Qx, 0, SCALE),
                                                (1, Kx, 1, 1.0)):
                for c in range(8):
                    for nk in range(4):
                        aft = ([tw, t_xn, t_bq, psfree["ctx"]]
                               if c == 0 and nk == 0 else ())
                        p.top("matmul", ps_ctx[0:64, 512 * nk:512 * (nk + 1)],
                              wsl[:, 512 * wofs + 64 * c:512 * wofs + 64 * (c + 1)],
                              xn_fm[:, S * c + 512 * nk:S * c + 512 * (nk + 1)],
                              start=(c == 0), stop=(c == 7), after=aft)
                tmm = p.tmark()
                tev = p.op("vector", "tensor_scalar", dst_ten[0:64, :],
                           ps_ctx[0:64, :], float(sclv),
                           bvec[0:64, 16 * bcol + h:16 * bcol + h + 1],
                           AluOpType.mult, AluOpType.add, after=[tmm] + hg)
                psfree["ctx"] = tev
                if wofs == 0:
                    t_qf = tev
                else:
                    t_kf = tev
            tqc1 = p.op("vector", "tensor_copy", QxH[0:64, :], Qx[0:64, :],
                        after=[t_qf])
            tqc2 = p.op("vector", "tensor_copy", QxL[0:64, :], Qx[0:64, :])
            tap(f"qx_{b}_{h}", Qx[0:64, :], [t_qf])
            tap(f"kx_{b}_{h}", Kx[0:64, :], [t_kf])

            _mk(f"h{h}_qk")
            for nk in range(4):
                for c in range(8):
                    aft = ([t_kf, psfree["ctx"]] + hg if nk == 0 and c == 0
                           else ())
                    p.top("matmul", ps_ctx[0:64, 512 * nk:512 * (nk + 1)],
                          wsl[:, 2 * 512 + 64 * c:2 * 512 + 64 * (c + 1)],
                          xn_fm[:, S * c + 512 * nk:S * c + 512 * (nk + 1)],
                          start=(c == 0), stop=(c == 7), after=aft)
            tmm = p.tmark()
            if b > 0:
                t_wsl_free = tmm
            vd_st = ctx_h
            t_vd = p.op("vector", "tensor_scalar", vd_st[0:64, :],
                        ps_ctx[0:64, :], 1.0,
                        bvec[0:64, 32 + h:32 + h + 1],
                        AluOpType.mult, AluOpType.add, after=[tmm] + hg)
            psfree["ctx"] = t_vd
            for t in range(NT):
                aft = [t_vd, t_id, psfree["t"]] if t == 0 else ()
                p.top("matmul", ps_t[:, 64 * t:64 * (t + 1)],
                      vd_st[0:64, 128 * t:128 * (t + 1)], identb[0:64, 0:64],
                      start=(t == 0), stop=(t == NT - 1),
                      is_transpose=True, after=aft)
            t_vd_read = p.tmark()
            t_vev = p.op("scalar", "copy",
                         rawap(v_aug, 0,
                               [[NT * (HD + 1), 128], [HD + 1, NT], [1, HD]]),
                         ps_t[:, 0:NT * HD], after=[t_vd_read, t_vone])
            psfree["t"] = t_vev
            tap(f"vaug_{b}_{h}", v_aug[:, :], [t_vev])

            _mk(f"h{h}_v")
            if b > 0:
                t_tabR = p.dma("sync", tabR[:],
                               tabs_s[64 * h:64 * h + 64, 0:WBAND],
                               after=hg + [tabstore_tok[h]])
                t_tabF = p.dma("sync", tabF[:],
                               tabs_s[64 * h:64 * h + 64, WBAND:2 * WBAND])
                tap(f"tabR_{b}_{h}", tabR[:], [t_tabR])
                tap(f"tabF_{b}_{h}", tabF[:], [t_tabF])
            for (wofs, dstt, bcol, key) in ((3, pkpm, 48, "a"),
                                            (4, pqpm, 64, "b")) if b == 0 \
                    else ():
                pst = ps_a if key == "a" else ps_b
                pofs = 512 * (wofs - 3)
                pdst = bc_band[0:64, pofs:pofs + 512]
                for c in range(8):
                    aft = ([t_refm, tw, psfree[key]] + hg if c == 0 else ())
                    p.top("matmul", pst[0:64, 0:512],
                          wsl[:, 512 * wofs + 64 * c:512 * wofs + 64 * (c + 1)],
                          re_fm[:, 512 * c:512 * (c + 1)],
                          start=(c == 0), stop=(c == 7), after=aft)
                tmm = p.tmark()
                t_wsl_free = tmm
                t_pd = p.op("vector", "tensor_scalar", pdst,
                            pst[0:64, 0:512], 1.0,
                            bvec[0:64, bcol + h:bcol + h + 1],
                            AluOpType.mult, AluOpType.add, after=[tmm] + hg)
                psfree[key] = t_pd
                for pc in range(4):
                    aft = [t_pd, psfree["t2"]] if pc == 0 else ()
                    p.top("matmul", ps_t2[:, 64 * pc:64 * (pc + 1)],
                          bc_band[0:64, pofs + 128 * pc:pofs + 128 * (pc + 1)],
                          identb[0:64, 0:64], start=(pc == 0), stop=(pc == 3),
                          is_transpose=True, after=aft)
                tt = p.tmark()
                t_pm = p.op("scalar", "copy", dstt[:, 0:256],
                            ps_t2[:, 0:256], after=[tt])
                psfree["t2"] = t_pm
                if wofs == 3:
                    t_pk = t_pm
                else:
                    t_pq = t_pm
            for (srct, gt, dstt, sclt, tprev) in (
                    (pkpm, grev4, tabR, 1.0, t_pk),
                    (pqpm, gfwd4, tabF, SCALE, t_pq)) if b == 0 else ():
                t_tab = None
                for uc in range(3):
                    n0 = 512 * uc
                    n1 = min(WBAND, n0 + 512)
                    key = "a" if uc % 2 == 0 else "b"
                    pst = ps_a if key == "a" else ps_b
                    for pc in range(4):
                        aft = ([tprev, t_gtab, psfree[key]] + hg
                               if pc == 0 else ())
                        p.top("matmul", pst[0:64, 0:n1 - n0],
                              srct[:, 64 * pc:64 * (pc + 1)],
                              gt[:, WBAND * pc + n0:WBAND * pc + n1],
                              start=(pc == 0), stop=(pc == 3), after=aft)
                    tmm = p.tmark()
                    t_tab = p.op("scalar", "activation", dstt[:, n0:n1],
                                 pst[0:64, 0:n1 - n0], AF.Copy,
                                 scale=float(sclt),
                                 after=[tmm, t_tab] if t_tab else [tmm])
                    psfree[key] = t_tab
                if sclt == 1.0:
                    t_tabR = t_tab
                else:
                    t_tabF = t_tab
            if b == 0:
                p.dma("scalar", tabs_s[64 * h:64 * h + 64, 0:WBAND],
                      tabR[:], after=[t_tabR])
                tabstore_tok[h] = p.dma(
                    "scalar", tabs_s[64 * h:64 * h + 64, WBAND:2 * WBAND],
                    tabF[:], after=[t_tabF])
                tap(f"tabR_{b}_{h}", tabR[:], [t_tabR])
                tap(f"tabF_{b}_{h}", tabF[:], [t_tabF])

            _mk(f"h{h}_tabs")
            t_sat = []
            t_rowfree = None
            for (tab, t_tt, ucolA, ucolB, opd, t_opd,
                 dstA, rowA, dstB, rowB) in (
                    (tabR, t_tabR, U_SATH_REV, U_SATL_REV, Qx, t_qf,
                     QxH, 66, QxL, 68),
                    (tabF, t_tabF, U_SATH_FWD, U_SATL_FWD, Kx, t_kf,
                     Kx, 65, Kx, 67)):
                c0, c1 = (ucolA, ucolB) if ucolA < ucolB else (ucolB, ucolA)
                rA = 0 if ucolA == c0 else 1
                trow = None
                for nk in range(4):
                    aft = ([t_tt, t_opd, psfree["a"]] if nk == 0
                           else [psfree["a"]])
                    p.top("matmul", ps_a[0:2, 0:512],
                          rawap(tab, c0, [[WBAND, 64], [c1 - c0, 2]]),
                          opd[0:64, 512 * nk:512 * (nk + 1)],
                          start=True, stop=True, after=aft)
                    tmm = p.tmark()
                    aft2 = [tmm] + ([t_rowfree] if t_rowfree and nk == 0
                                    else [])
                    trow = p.op("scalar", "copy",
                                rowbuf[0:2, 512 * nk:512 * (nk + 1)],
                                ps_a[0:2, 0:512], after=aft2)
                    psfree["a"] = trow
                tA = p.dma("sync", dstA[rowA:rowA + 1, :],
                           rowbuf[rA:rA + 1, :], after=[trow])
                t_rowfree = p.dma("sync", dstB[rowB:rowB + 1, :],
                                  rowbuf[1 - rA:2 - rA, :], after=[trow])
                t_sat += [tA, t_rowfree]

            _mk(f"h{h}_sat")
            t1 = p.op("vector", "reduce_max", scal2[0:64, 2:3], Kx[0:64, :],
                      AX, apply_absolute_value=True, after=[t_kf])
            p.top("matmul", ps_a[0:1, 8:72], scal2[0:64, 2:3],
                  identf[0:64, 0:64], start=True, stop=True, is_transpose=True,
                  after=[t1, t_if, psfree["a"]])
            tmm = p.tmark()
            t2 = p.op("vector", "reduce_max", scal[0:1, 5:6], ps_a[0:1, 8:72],
                      AX, after=[tmm])
            psfree["a"] = t2
            tnorm = {}
            t_qsq_free = None
            for (tab, col, t_tt) in (((tabR, 8, t_tabR), (tabF, 10, t_tabF))
                                     if b == 0 else ()):
                tsq = p.op("scalar", "activation", qsq[0:64, 0:WBAND], tab[:],
                           AF.Square,
                           after=[t_tt] + t_sat + ([t_qsq_free]
                                                   if t_qsq_free else []))
                tmx = None
                for uc in range(3):
                    n0 = 512 * uc
                    n1 = min(WBAND, n0 + 512)
                    p.top("matmul", ps_a[0:1, 0:n1 - n0], ones_b64[:],
                          qsq[0:64, n0:n1], start=True, stop=True,
                          after=[tsq, psfree["a"]] if uc == 0
                          else [psfree["a"]])
                    tmm = p.tmark()
                    tr = p.op("vector", "reduce_max", scal[0:1, 6:7],
                              ps_a[0:1, 0:n1 - n0], AX, after=[tmm])
                    psfree["a"] = tr
                    if tmx is None:
                        tmx = p.op("vector", "tensor_copy",
                                   scal[0:1, col:col + 1], scal[0:1, 6:7],
                                   after=[tr])
                    else:
                        tmx = p.op("vector", "tensor_tensor",
                                   scal[0:1, col:col + 1],
                                   scal[0:1, col:col + 1], scal[0:1, 6:7],
                                   AluOpType.max, after=[tr])
                t_qsq_free = p.tmark()
                nbc = 2 * h + (0 if col == 8 else 1)
                tnorm[col] = p.op("scalar", "activation",
                                  normbuf[0:1, nbc:nbc + 1],
                                  scal[0:1, col:col + 1], AF.Sqrt,
                                  after=[tmx] +
                                  ([normtok[nbc]] if normtok[nbc] else []))
                normtok[nbc] = tnorm[col]
            tn8 = normtok[2 * h]
            tn10 = normtok[2 * h + 1]
            p.op("vector", "tensor_tensor", scal[0:1, 11:12], scal[0:1, 5:6],
                 normbuf[0:1, 2 * h:2 * h + 1], AluOpType.add,
                 after=[tn8, t2])
            tm1 = p.op("vector", "tensor_tensor", scal[0:1, 11:12],
                       scal[0:1, 11:12], scal[0:1, 11:12], AluOpType.mult)
            p.op("vector", "scalar_tensor_tensor", scal[0:1, 12:13],
                 scal[0:1, 5:6], 1.0, normbuf[0:1, 2 * h + 1:2 * h + 2],
                 AluOpType.bypass, AluOpType.mult, after=[tn10])
            tm2 = p.op("vector", "tensor_scalar_add", scal[0:1, 12:13],
                       scal[0:1, 12:13], 1.0)
            tqs = p.op("scalar", "activation", qsq[0:64, 0:S], Qx[0:64, :],
                       AF.Square, after=[t_qf, t_qsq_free] + t_sat)
            t_mr = None
            for nk in range(4):
                p.top("matmul", ps_a[0:1, 0:512], ones_b64[:],
                      qsq[0:64, 512 * nk:512 * (nk + 1)],
                      start=True, stop=True,
                      after=[tqs, psfree["a"]] if nk == 0
                      else [psfree["a"]])
                tmm = p.tmark()
                t_mr = p.op("scalar", "activation",
                            den[0:1, 512 * nk:512 * (nk + 1)],
                            ps_a[0:1, 0:512], AF.Sqrt,
                            scale=scal[0:1, 11:12], after=[tmm, tm1])
                psfree["a"] = t_mr
            trb = p.op("vector", "tensor_scalar", rowbuf[0:1, :], den[0:1, :],
                       scal[0:1, 12:13], -1.0, AluOpType.add, AluOpType.mult,
                       after=[t_mr, tm2, t_rowfree])
            p.dma("sync", Qx[64:65, :], rowbuf[0:1, :], after=[trb])
            p.dma("sync", QxH[64:65, :], rowbuf[0:1, :], after=[tqc1])
            t_m = p.dma("sync", QxL[64:65, :], rowbuf[0:1, :], after=[tqc2])
            tap(f"mrow_{b}_{h}", den[0:1, :], [t_mr])

            _mk(f"h{h}_mrow")
            t_bc = None
            cfree = [psfree["ctx"]] * 4
            ridx = 0
            for qt in range(NT):
                for uc in range(3):
                    n0 = 512 * uc
                    n1 = min(WBAND, n0 + 512)
                    r = ridx % 4
                    ridx += 1
                    aft = [t_tabR, t_qf, cfree[r]]
                    p.top("matmul", ps_ctx[:, 512 * r:512 * r + n1 - n0],
                          Qx[0:64, 128 * qt:128 * (qt + 1)],
                          tabR[:, n0:n1], start=True, stop=True, after=aft)
                    tmm = p.tmark()
                    t_bc = p.op("vector" if uc == 1 else "scalar",
                                "tensor_copy" if uc == 1 else "copy",
                                bc_band[:, WBAND * qt + n0:WBAND * qt + n1],
                                ps_ctx[:, 512 * r:512 * r + n1 - n0],
                                after=[tmm])
                    cfree[r] = t_bc
            tap(f"bc_{b}_{h}", bc_band[:, 0:WBAND], [t_bc])

            def make_bp(kt, buf, rdtok=None, extra=()):
                tb = None
                for uc in range(3):
                    n0 = 512 * uc
                    n1 = min(WBAND, n0 + 512)
                    key = "a" if uc % 2 == 0 else "b"
                    pst = ps_a if key == "a" else ps_b
                    aft = list(extra) + [psfree[key], t_kf, t_tabF]
                    p.top("matmul", pst[:, 0:n1 - n0],
                          Kx[0:64, 128 * kt:128 * (kt + 1)],
                          tabF[:, n0:n1], start=True, stop=True, after=aft)
                    tmm = p.tmark()
                    if uc == 1:
                        tb = p.op("scalar", "copy", buf[:, n0:n1],
                                  pst[:, 0:n1 - n0], after=[tmm])
                    else:
                        tb = p.op("vector", "tensor_copy", buf[:, n0:n1],
                                  pst[:, 0:n1 - n0],
                                  after=[tmm, rdtok] if (rdtok and uc == 0)
                                  else [tmm])
                    psfree[key] = tb
                return tb

            bfree = {"ctile": [None, None], "p2cT": [None, None],
                     "pt": [None, None], "c2pT": [None, None],
                     "bp": [None, None], "bpdma": [None, None]}
            t_bp = [None, None]

            def prep_kt_dma(kt):
                i = kt % 2
                qlo, qhi = win(kt)
                nd = qhi - qlo
                srcv = rawap(bp_band[i], 639 + 128 * (qlo - kt),
                             [[WBAND - 1, 128], [1, 128 * nd]])
                tp2 = p.dma("sync", p2cT[i][:, 0:128 * nd], srcv,
                            after=[t_bp[i], bfree["p2cT"][i]])
                bfree["bpdma"][i] = tp2
                srcv = rawap(bc_band, WBAND * kt + 639 + 1152 * (qlo - kt),
                             [[WROW - 1, 128], [1152, nd], [1, 128]])
                tct = p.dma("scalar", ctile[i][:, 0:128 * nd], srcv,
                            after=[t_bc, bfree["ctile"][i]])
                return tp2, tct

            def prep_kt_pe(kt, tp2, tct):
                i = kt % 2
                qlo, qhi = win(kt)
                nd = qhi - qlo
                tev = None
                for d in range(nd):
                    if d == 0:
                        aft = [tct, psfree["t2"]]
                    elif d % 8 == 0:
                        aft = [tev]
                    else:
                        aft = ()
                    p.top("matmul",
                          ps_t2[:, 128 * (d % 8):128 * (d % 8) + 128],
                          ctile[i][:, 128 * d:128 * (d + 1)], identb[:],
                          start=(d % 8 == 0),
                          stop=(d % 8 == 7 or d == nd - 1),
                          is_transpose=True, after=aft)
                    if d % 8 == 7 or d == nd - 1:
                        g0 = (d // 8) * 8
                        tt = p.tmark()
                        aft2 = [tt] + ([bfree["c2pT"][i]]
                                       if bfree["c2pT"][i] else [])
                        if FOLD_BANDS:
                            if g0 == 0:
                                aft2.append(tp2)
                            tev = p.op("vector", "tensor_tensor",
                                       c2pT[i][:, 128 * g0:128 * (d + 1)],
                                       ps_t2[:, 0:128 * (d + 1 - g0)],
                                       p2cT[i][:, 128 * g0:128 * (d + 1)],
                                       AluOpType.add, after=aft2)
                        else:
                            tev = p.op("scalar", "copy",
                                       c2pT[i][:, 128 * g0:128 * (d + 1)],
                                       ps_t2[:, 0:128 * (d + 1 - g0)],
                                       after=aft2)
                psfree["t2"] = tev
                bfree["ctile"][i] = p.tok("p_tensor")
                if FOLD_BANDS:
                    bfree["p2cT"][i] = tev
                return tp2, tev

            _mk(f"h{h}_bc")
            t_bp[0] = make_bp(0, bp_band[0])
            t_prep = [None, None]
            t_prep[0] = prep_kt_pe(0, *prep_kt_dma(0))

            for kt in range(NT):
                _mk(f"h{h}_kt{kt}")
                i = kt % 2
                qlo, qhi = win(kt)
                pt = pt_buf[i]
                tp2, tc2 = t_prep[i]
                if kt + 1 < NT:
                    j = (kt + 1) % 2
                    t_bp[j] = make_bp(kt + 1, bp_band[j],
                                      rdtok=bfree["bpdma"][j])
                    dma_next = prep_kt_dma(kt + 1)
                exp_toks = []
                for ch in range(4):
                    c0 = 512 * ch
                    key = "a" if ch % 2 == 0 else "b"
                    pst = ps_a if key == "a" else ps_b
                    qt0 = c0 // 128
                    segs = []
                    for qt in range(qt0, qt0 + 4):
                        r = "B" if qlo <= qt < qhi else ("L" if qt < qlo else "H")
                        if segs and segs[-1][0] == r:
                            segs[-1][2] = qt + 1
                        else:
                            segs.append([r, qt, qt + 1])
                    blo = max(c0, 128 * qlo)
                    bhi = min(c0 + 512, 128 * qhi)
                    fold = FOLD_BANDS and P2C_ON and C2P_ON
                    nband = ((1 if fold else int(P2C_ON) + int(C2P_ON))
                             if blo < bhi else 0)
                    nmm = len(segs) + nband
                    k = 0
                    aftc = [t_m, tp2, tc2, bfree["pt"][i], psfree[key],
                            t_cr, tqc1, tqc2] + t_sat
                    for (r, q0_, q1_) in segs:
                        n0 = 128 * q0_
                        n1 = 128 * q1_
                        if r == "B" or not SAT_ON:
                            lhsT = Kx[0:65, 128 * kt:128 * (kt + 1)]
                            rhs = Qx[0:65, n0:n1]
                        elif r == "L":
                            lhsT = Kx[0:69, 128 * kt:128 * (kt + 1)]
                            rhs = QxL[0:69, n0:n1]
                        else:
                            lhsT = Kx[0:67, 128 * kt:128 * (kt + 1)]
                            rhs = QxH[0:67, n0:n1]
                        k += 1
                        p.top("matmul", pst[:, n0 - c0:n1 - c0], lhsT, rhs,
                              start=(k == 1), stop=(k == nmm),
                              after=aftc if k == 1 else ())
                    if nband:
                        off = blo - 128 * qlo
                        nb_ = 1 if fold else int(P2C_ON) + int(C2P_ON)
                        if P2C_ON and not fold:
                            p.top("matmul", pst[:, blo - c0:bhi - c0], identb[:],
                                  p2cT[i][:, off:off + bhi - blo],
                                  start=False, stop=(nb_ == 1))
                        if C2P_ON:
                            p.top("matmul", pst[:, blo - c0:bhi - c0], identb[:],
                                  c2pT[i][:, off:off + bhi - blo],
                                  start=False, stop=True)
                        if nb_ == 0:
                            p.top("matmul", pst[:, c0 - c0:c0 - c0 + 1]
                                  if False else pst[:, 0:512],
                                  Kx[0:65, 128 * kt:128 * (kt + 1)],
                                  Qx[0:65, c0:c0 + 512],
                                  start=False, stop=True)
                    tmm = p.tmark()
                    tex = p.op("scalar", "activation", pt[:, c0:c0 + 512],
                               pst[:, 0:512], AF.Exp, after=[tmm])
                    psfree[key] = tex
                    exp_toks.append(tex)
                if not (FOLD_BANDS and P2C_ON and C2P_ON):
                    bfree["p2cT"][i] = p.tok("p_tensor")
                bfree["c2pT"][i] = p.tok("p_tensor")
                for ch in range(4):
                    aft = ([exp_toks[0], t_vev, t_vone]
                           if ch == 0 else [exp_toks[ch]])
                    if kt == 0:
                        aft = aft + [cfree[ch]]
                    p.top("matmul", ps_ctx[0:65, 512 * ch:512 * (ch + 1)],
                          v_aug[:, (HD + 1) * kt:(HD + 1) * (kt + 1)],
                          pt[:, 512 * ch:512 * (ch + 1)],
                          start=(kt == 0), stop=(kt == NT - 1), after=aft)
                tap(f"pt_{b}_{h}_{kt}", pt[:, :], exp_toks)
                bfree["pt"][i] = p.tmark()
                if kt + 1 < NT:
                    t_prep[j] = prep_kt_pe(kt + 1, *dma_next)

            _mk(f"h{h}_ctx")
            t_pv_done = bfree["pt"][(NT - 1) % 2]
            tden = p.op("scalar", "copy", den[64:65, :], ps_ctx[64:65, :],
                        after=[t_pv_done])
            tap(f"den_{b}_{h}", den[64:65, :], [tden])
            trec = p.op("vector", "reciprocal", den[64:65, :], den[64:65, :],
                        after=[tden])
            t_ctxh = None
            for ch in range(4):
                key = "a" if ch % 2 == 0 else "b"
                pst = ps_a if key == "a" else ps_b
                p.top("matmul", pst[0:64, 0:512], ones64[64:65, 0:64],
                      den[64:65, 512 * ch:512 * (ch + 1)],
                      start=True, stop=True,
                      after=[trec, t_o64, psfree[key]])
                tmm = p.tmark()
                aft_cb = [tmm] + ([t_ctxh] if t_ctxh else []) +                     ([shared["xsq"]] if shared["xsq"] else [])
                tcb = p.op("scalar", "copy", xsq[0:64, 0:512],
                           pst[0:64, 0:512], after=aft_cb)
                psfree[key] = tcb
                t_ctxh = p.op("vector", "tensor_tensor",
                              ctx_h[:, 512 * ch:512 * (ch + 1)],
                              ps_ctx[0:64, 512 * ch:512 * (ch + 1)],
                              xsq[0:64, 0:512], AluOpType.mult,
                              after=[tcb, t_ctxh] if t_ctxh else [tcb])
                shared["xsq"] = t_ctxh
            psfree["ctx"] = t_ctxh
            tap(f"ctxh_{b}_{h}", ctx_h[:, :], [t_ctxh])
            t_head_gate = p.dma("scalar", ctx_s[64 * h:64 * h + 64, :],
                                ctx_h[:, :], after=[t_ctxh])

        _mk(f"b{b}_wo")
        if WO_RUN:
            t_d_gate = p.tok("d_sync")
            t_bo = p.dma("sync", den[0:1, 0:H], rawap(dt["bo"], 0, [[H, 1], [1, H]]),
                         after=[t_head_gate])
            t_wo_loaded = None
            for c in range(8):
                t_wo_loaded = p.dma("gpsimd", wo_view[:, H * c:H * (c + 1)],
                                    dt["Wo"][128 * c:128 * (c + 1), :],
                                    after=[t_d_gate, t_head_gate] if c == 0 else ())
            t_x2 = None
            woc_free = [None, None]
            for t in range(NT):
                tlx = p.dma("scalar", xt_f[t % 2][:],
                            x_in[b * S + 128 * t:b * S + 128 * (t + 1), :],
                            after=[xtfree[t % 2]] if xtfree[t % 2] else [])
                tctx = p.dma("sync", rawap(wo_ctx[t % 2], 0,
                                           [[8 * 128, 128], [128, 8], [1, 128]]),
                             rawap(ctx_s, 128 * t, [[S, 128], [128 * S, 8], [1, 128]]),
                             after=[t_head_gate] + ([woc_free[t % 2]]
                                                    if woc_free[t % 2] else []))
                for nk in range(2):
                    key = "a" if nk == 0 else "b"
                    pst = ps_a if key == "a" else ps_b
                    for c in range(8):
                        aft = ([tctx, t_wo_loaded, psfree[key]]
                               if c == 0 else ())
                        p.top("matmul", pst[:, 0:512],
                              wo_ctx[t % 2][:, 128 * c:128 * (c + 1)],
                              wo_view[:, H * c + 512 * nk:H * c + 512 * (nk + 1)],
                              start=(c == 0), stop=False, after=aft)
                    p.top("matmul", pst[:, 0:512], ones_fr[0:1, :],
                          den[0:1, 512 * nk:512 * (nk + 1)],
                          start=False, stop=True,
                          after=[t_bo] if t == 0 and nk == 0 else ())
                tmm = p.tmark()
                ta = p.op("vector", "tensor_tensor", xt_f[t % 2][:, 0:512],
                          xt_f[t % 2][:, 0:512], ps_a[:, 0:512], AluOpType.add,
                          after=[tmm, tlx])
                tb = p.op("vector", "tensor_tensor", xt_f[t % 2][:, 512:1024],
                          xt_f[t % 2][:, 512:1024], ps_b[:, 0:512],
                          AluOpType.add)
                psfree["a"] = tb
                psfree["b"] = tb
                woc_free[t % 2] = p.tmark()
                xtfree[t % 2] = tb
                t_x2 = p.dma("sync", x2_s[b * S + 128 * t:b * S + 128 * (t + 1), :],
                             xt_f[t % 2][:], after=[tb])
                xtfree[t % 2] = t_x2

        else:
            t_x2 = t_head_gate
        t_batch_gate = t_x2

    gates = p.engine_gates()

    es_attn.close()

    if FFN_RUN:
        _mk("ffnA")
        es_fa = contextlib.ExitStack()
        wgT = sb_in(es_fa, "wgT", [128, 8 * INNER], BF16)
        wuT = sb_in(es_fa, "wuT", [128, 8 * INNER], BF16)
        gu = sb_in(es_fa, "gu", [128, INNER], F32)
        xn2 = sb_in(es_fa, "xn2", [128, H], F32)
        yq = sb_in(es_fa, "yq", [128, 1024], F32)
        qint = sb_in(es_fa, "qint", [128, 1024], I32)
        qflt = sb_in(es_fa, "qflt", [128, 1024], F32)
        gq1 = sb_in(es_fa, "gq1", [128, 1024], F32)
        gq2 = qint.bitcast(F32)
        xq_tok = sb_in(es_fa, "xq_tok", [128, H], BF16)
        xqT = sb_in(es_fa, "xqT", [128, H], BF16)
        guq_tok = sb_in(es_fa, "guq_tok", [128, INNER], BF16)
        stage2 = sb_in(es_fa, "stage2", [128, INNER], BF16)
        sgate = xsq
        dscv = sb_in(es_fa, "dscv", [128, 8], F32)
        ps_g = [es_fa.enter_context(nc.psum_tensor(f"ps_g{i}", [128, 512], F32))
                for i in range(2)]
        ps_u = [es_fa.enter_context(nc.psum_tensor(f"ps_u{i}", [128, 512], F32))
                for i in range(2)]

        tsw = None
        for col in (2, 3, 4):
            aft = (gates + [tw_g, tw_u, tw_d, psfree["a"]] if col == 2
                   else [psfree["a"]])
            p.top("matmul", ps_a[:, 4:5], ones_fr[:],
                  scal[0:1, col:col + 1], start=True, stop=True, after=aft)
            tmm = p.tmark()
            tsw = p.op("vector", "tensor_copy", thrv[:, col:col + 1],
                       ps_a[:, 4:5], after=[tmm])
            psfree["a"] = tsw

        t_wres = None
        for c in range(8):
            p.dma("sync", wgT[:, INNER * c:INNER * (c + 1)],
                  wgT_s[128 * c:128 * (c + 1), :],
                  after=gates + [tw_g] if c == 0 else [])
            t_wres = p.dma("sync", wuT[:, INNER * c:INNER * (c + 1)],
                           wuT_s[128 * c:128 * (c + 1), :])

        qshared = [None]
        qscale_war = [None]
        qfin = [None]

        def quantize(src, W, out_bf, t_src, dcol, chunk_toks=None,
                     scale_war=None, out_war=None):
            if chunk_toks is None:
                chunk_toks = []
            t0 = p.op("vector", "reduce_max", dscv[:, 0:1], src[:, 0:W], AX,
                      apply_absolute_value=True,
                      after=[t_src] + ([qscale_war[0]] if qscale_war[0]
                                       else []))
            t1a = p.op("vector", "tensor_single_scalar", dscv[:, 0:1],
                       dscv[:, 0:1], 1e-5, AluOpType.max, after=[t0])
            t1 = p.op("vector", "tensor_scalar_mul", dscv[:, dcol:dcol + 1],
                      dscv[:, 0:1], 1.0 / 127.0,
                      after=[scale_war] if scale_war else ())
            t2r = p.op("vector", "reciprocal", dscv[:, 1:2], dscv[:, 0:1])
            t2 = p.op("vector", "tensor_scalar_mul", dscv[:, 1:2],
                      dscv[:, 1:2], 127.0)
            qscale_war[0] = t2
            tq = None
            tzp = qshared[0]
            for q0 in range(0, W, 1024):
                ty = p.op("vector", "tensor_scalar_mul", yq[:, 0:1024],
                          src[:, q0:q0 + 1024], dscv[:, 1:2],
                          after=[t2] + ([tzp] if tzp else [])
                          + ([tq] if tq else ([qfin[0]] if qfin[0] else [])))
                tz = p.op("scalar", "copy", qint[:, 0:1024], yq[:, 0:1024],
                          after=[ty])
                tzp = tz
                tzf = p.op("scalar", "copy", qflt[:, 0:1024], qint[:, 0:1024])
                tg1 = p.op("vector", "scalar_tensor_tensor", gq1[:, 0:1024],
                           qflt[:, 0:1024], 0.5, yq[:, 0:1024], AluOpType.add,
                           AluOpType.is_le, after=[tzf])
                tg2 = p.op("vector", "scalar_tensor_tensor", gq2[:, 0:1024],
                           qflt[:, 0:1024], -0.5, yq[:, 0:1024], AluOpType.add,
                           AluOpType.is_ge)
                td = p.op("gpsimd", "tensor_tensor", gq1[:, 0:1024],
                          gq1[:, 0:1024], gq2[:, 0:1024], AluOpType.subtract,
                          after=[tg1, tg2])
                tq = p.op("gpsimd", "tensor_tensor", out_bf[:, q0:q0 + 1024],
                          qflt[:, 0:1024], gq1[:, 0:1024], AluOpType.add,
                          after=[td] + ([out_war] if out_war and q0 == 0
                                        else []))
                chunk_toks.append(tq)
            qshared[0] = tzp
            qfin[0] = tq
            return tq

        stage_free = None
        pipe = {"tsg": None, "tumm": None, "tdu": None, "xq_tt": None,
                "tds": None, "guq_tt": None}

        def ffn_front(t):
            aft = [xtfree[t % 2]] if xtfree[t % 2] else []
            if t == 0:
                aft = aft + gates
            tl = p.dma("sync", xt_f[t % 2][:], x2_s[128 * t:128 * (t + 1), :],
                       after=aft)
            tln = layernorm(xt_f[t % 2][:], xn2[:], H, after=[tl],
                            apply_after=[pipe["tqx"]] if pipe.get("tqx")
                            else ())
            xtfree[t % 2] = tln
            tqx = quantize(xn2, H, xq_tok, tln, 5,
                           scale_war=pipe["tdu"], out_war=pipe["xq_tt"])
            pipe["tqx"] = tqx
            for c in range(8):
                aft2 = [tqx, psfree["t"]] if c == 0 else ()
                p.top("matmul", ps_t[:, 128 * c:128 * (c + 1)],
                      xq_tok[:, 128 * c:128 * (c + 1)], identb[:],
                      start=(c == 0), stop=(c == 7), is_transpose=True,
                      after=aft2)
            tt2 = p.tmark()
            pipe["xq_tt"] = tt2
            txq = p.op("scalar", "copy", xqT[:], ps_t[:, 0:1024],
                       after=[tt2, pipe["tumm"]] if pipe["tumm"] else [tt2])
            psfree["t"] = txq
            return tqx, txq

        nxt_front = ffn_front(0)
        for t in range(NB * NT):
            tqx, txq = nxt_front
            tdg = p.op("vector", "tensor_tensor", dscv[:, 6:7], dscv[:, 5:6],
                       thrv[:, 2:3], AluOpType.mult,
                       after=[tqx, tsw] +
                       ([pipe["tsg"]] if pipe["tsg"] else []))
            tdu = p.op("vector", "tensor_tensor", dscv[:, 7:8], dscv[:, 5:6],
                       thrv[:, 3:4], AluOpType.mult)
            pipe["tdu"] = tdu
            t_gu = None
            t_gufree = [[], []]
            for ic in range(8):
                pg, pu = ps_g[ic % 2], ps_u[ic % 2]
                for c in range(8):
                    aft = (([txq, t_wres] if ic < 2 else [])
                           + t_gufree[ic % 2] if c == 0 else ())
                    p.top("matmul", pg[:, 0:512],
                          xqT[:, 128 * c:128 * (c + 1)],
                          wgT[:, INNER * c + 512 * ic:INNER * c + 512 * (ic + 1)],
                          start=(c == 0), stop=(c == 7), after=aft)
                tg_mm = p.tmark()
                for c in range(8):
                    p.top("matmul", pu[:, 0:512],
                          xqT[:, 128 * c:128 * (c + 1)],
                          wuT[:, INNER * c + 512 * ic:INNER * c + 512 * (ic + 1)],
                          start=(c == 0), stop=(c == 7))
                tu_mm = p.tmark()
                tsg = p.op("scalar", "activation", sgate[:, 0:512],
                           pg[:, 0:512],
                           AF.Silu, scale=dscv[:, 6:7],
                           after=[tg_mm, tdu] +
                           ([t_gu] if t_gu else
                            ([shared["xsq"]] if shared["xsq"] else [])))
                tsu = p.op("vector", "tensor_scalar_mul",
                           gu[:, 512 * ic:512 * (ic + 1)], pu[:, 0:512],
                           dscv[:, 7:8],
                           after=[tu_mm] + ([t_gu] if t_gu else []))
                t_gufree[ic % 2] = [tsg, tsu]
                t_gu = p.op("vector", "tensor_tensor",
                            gu[:, 512 * ic:512 * (ic + 1)],
                            gu[:, 512 * ic:512 * (ic + 1)], sgate[:, 0:512],
                            AluOpType.mult, after=[tsg, tsu])
                if ic == 7:
                    pipe["tsg"] = tsg
                    pipe["tumm"] = tu_mm
            shared["xsq"] = t_gu
            if t + 1 < NB * NT:
                nxt_front = ffn_front(t + 1)
            guq_toks = []
            tqg = quantize(gu, INNER, guq_tok, t_gu, 3, chunk_toks=guq_toks,
                           scale_war=pipe["tds"], out_war=pipe["guq_tt"])
            tds = p.op("vector", "tensor_copy", dsc_all[:, t:t + 1],
                       dscv[:, 3:4], after=[tqg])
            pipe["tds"] = tds
            for g0 in range(0, 32, 8):
                key = "t" if (g0 // 8) % 2 == 0 else "t2"
                pst = ps_t if key == "t" else ps_t2
                for c in range(g0, g0 + 8):
                    aft = [guq_toks[g0 // 8], psfree[key]] if c == g0 else ()
                    p.top("matmul", pst[:, 128 * (c - g0):128 * (c - g0) + 128],
                          guq_tok[:, 128 * c:128 * (c + 1)], identb[:],
                          start=(c == g0), stop=(c == g0 + 7),
                          is_transpose=True, after=aft)
                tt = p.tmark()
                if g0 == 24:
                    pipe["guq_tt"] = tt
                tev = p.op("scalar", "copy", stage2[:, 128 * g0:128 * (g0 + 8)],
                           pst[:, 0:1024],
                           after=[tt, stage_free] if stage_free else [tt])
                psfree[key] = tev
            srcv = rawap(stage2, 0, [[INNER, 128], [128, 32], [1, 128]])
            dstv = rawap(guq_s, t * (128 * INNER),
                         [[128, 128], [128 * 128, 32], [1, 128]])
            stage_free = p.dma("sync", dstv, srcv, after=[tev])
        t_a_gates = p.engine_gates()
        es_fa.close()

        _mk("ffnB")
        es_fb = contextlib.ExitStack()
        wdres = sb_in(es_fb, "wdres", [128, 32 * H], BF16)
        guqT_in = [sb_in(es_fb, f"guqT_in{i}", [128, INNER], BF16)
                   for i in range(2)]
        ps_d = es_fb.enter_context(nc.psum_tensor("ps_d", [128, 1024], F32))

        t_wd2 = None
        for c in range(32):
            t_wd2 = p.dma("sync", wdres[:, H * c:H * (c + 1)],
                          wdT_s[128 * c:128 * (c + 1), :],
                          after=t_a_gates + [tw_d] if c == 0 else [])
        ginfree = [None, None]
        t_psd_free = None
        for t in range(NB * NT):
            tg = p.dma("sync", guqT_in[t % 2][:],
                       rawap(guq_s, t * (128 * INNER),
                             [[128, 128], [16384, 32], [1, 128]]),
                       after=[ginfree[t % 2]] if ginfree[t % 2] else [])
            aft = [xtfree[t % 2]] if xtfree[t % 2] else []
            if t == 0:
                aft = aft + t_a_gates
            tlx = p.dma("scalar", xt_f[t % 2][:], x2_s[128 * t:128 * (t + 1), :],
                        after=aft)
            for nk in range(2):
                for c in range(32):
                    aft = [a for a in (tg, t_wd2, t_psd_free) if a]
                    p.top("matmul", ps_d[:, 512 * nk:512 * (nk + 1)],
                          guqT_in[t % 2][:, 128 * c:128 * (c + 1)],
                          wdres[:, H * c + 512 * nk:H * c + 512 * (nk + 1)],
                          start=(c == 0), stop=(c == 31),
                          after=aft if nk == 0 and c == 0 else ())
            tmm = p.tmark()
            ginfree[t % 2] = tmm
            tsc = p.op("vector", "tensor_tensor", dscv[0:128, 0:1]
                       if False else scal2[:, 3:4], dsc_all[:, t:t + 1],
                       thrv[:, 4:5], AluOpType.mult, after=[tmm, tsw])
            ta = p.op("vector", "scalar_tensor_tensor", xt_f[t % 2][:, 0:512],
                      ps_d[:, 0:512], scal2[:, 3:4], xt_f[t % 2][:, 0:512],
                      AluOpType.mult, AluOpType.add, after=[tsc, tlx])
            tb = p.op("vector", "scalar_tensor_tensor", xt_f[t % 2][:, 512:1024],
                      ps_d[:, 512:1024], scal2[:, 3:4], xt_f[t % 2][:, 512:1024],
                      AluOpType.mult, AluOpType.add)
            t_psd_free = tb
            tst = p.dma("sync", out_hbm[128 * t:128 * (t + 1), :],
                        xt_f[t % 2][:], after=[tb])
            xtfree[t % 2] = tst


    final_d = p.tok("d_sync")
    final_ds = p.tok("d_scalar")
    final_dg = p.tok("d_gpsimd")
    if TRUNC is not None:
        p.steps = p.steps[:TRUNC]

    with nc.Block() as block:
        @block.sync
        def _(sync):
            for e, fn in p.steps:
                if e == "sync":
                    fn(sync)
            if TRUNC is None:
                sync.wait_ge(p.sem["d_sync"], final_d[1])
                sync.wait_ge(p.sem["d_scalar"], final_ds[1])
                sync.wait_ge(p.sem["d_gpsimd"], final_dg[1])

        @block.vector
        def _(vector):
            for e, fn in p.steps:
                if e == "vector":
                    fn(vector)

        @block.scalar
        def _(scalar):
            for e, fn in p.steps:
                if e == "scalar":
                    fn(scalar)

        @block.tensor
        def _(tensor):
            for e, fn in p.steps:
                if e == "tensor":
                    fn(tensor)

        @block.gpsimd
        def _(gpsimd):
            for e, fn in p.steps:
                if e == "gpsimd":
                    fn(gpsimd)

    if FFN_RUN:
        es_fb.close()
    p.es.close()
    return nc


_last_results = None


def kernel(**inputs):
    global _last_results
    import ml_dtypes
    g_rev, g_fwd = _g_tables()
    ident = np.eye(128, dtype=np.float32)

    x = np.ascontiguousarray(np.asarray(inputs["L0_attn_hidden_states"],
                                        np.float32))
    base = {
        "Wq": np.ascontiguousarray(inputs["L0_attn_Wq"], dtype=np.float32),
        "bq": np.ascontiguousarray(inputs["L0_attn_bq"], dtype=np.float32),
        "Wk": np.ascontiguousarray(inputs["L0_attn_Wk"], dtype=np.float32),
        "bk": np.ascontiguousarray(inputs["L0_attn_bk"], dtype=np.float32),
        "Wv": np.ascontiguousarray(inputs["L0_attn_Wv"], dtype=np.float32),
        "bv": np.ascontiguousarray(inputs["L0_attn_bv"], dtype=np.float32),
        "Wo": np.ascontiguousarray(inputs["L0_attn_Wo"], dtype=np.float32),
        "bo": np.ascontiguousarray(inputs["L0_attn_bo"], dtype=np.float32),
        "Wpk": np.ascontiguousarray(inputs["L0_attn_Wpk"], dtype=np.float32),
        "bpk": np.ascontiguousarray(inputs["L0_attn_bpk"], dtype=np.float32),
        "Wpq": np.ascontiguousarray(inputs["L0_attn_Wpq"], dtype=np.float32),
        "bpq": np.ascontiguousarray(inputs["L0_attn_bpq"], dtype=np.float32),
        "rel": np.ascontiguousarray(inputs["L0_attn_rel_emb"], dtype=np.float32),
        "ln_g": np.ascontiguousarray(inputs["L0_attn_ln_g"], dtype=np.float32),
        "ln_b": np.ascontiguousarray(inputs["L0_attn_ln_b"], dtype=np.float32),
        "wg": np.ascontiguousarray(inputs["L0_ffn_w_gate"], dtype=np.float32),
        "wu": np.ascontiguousarray(inputs["L0_ffn_w_up"], dtype=np.float32),
        "wd": np.ascontiguousarray(inputs["L0_ffn_w_down"], dtype=np.float32),
        "Grev": g_rev, "Gfwd": g_fwd,
        "identb": ident.astype(ml_dtypes.bfloat16), "identf": ident,
        "bqk_t": np.ascontiguousarray(np.concatenate(
            [np.asarray(inputs["L0_attn_" + nm], np.float32).reshape(16, 64).T
             for nm in ("bq", "bk", "bv", "bpk", "bpq")], axis=1)),
    }
    in_maps = []
    for i in range(NCORE):
        m = dict(base)
        m["x"] = np.ascontiguousarray(x[NB * i:NB * (i + 1)].reshape(NB * S, H))
        in_maps.append(m)

    nc = build_nc()
    res = run_bass_kernel_spmd(nc, in_maps, core_ids=list(range(NCORE)))
    _last_results = res
    return np.concatenate(
        [res.results[i]["out"].reshape(NB, S, H) for i in range(NCORE)],
        axis=0)


# version 09
# speedup vs baseline: 1.1834x; 1.1761x over previous
import math
import contextlib
import numpy as np

import concourse.bass as bass
import concourse.mybir as mybir
from concourse.bass_utils import run_bass_kernel_spmd
from concourse.alu_op_type import AluOpType

F32 = mybir.dt.float32
BF16 = mybir.dt.bfloat16
I32 = mybir.dt.int32
AF = mybir.ActivationFunctionType
AX = mybir.AxisListType.X

H = 1024
NH = 16
HD = 64
S = 2048
NT = 16
NB = 2
INNER = 4096
SPAN = 256
SCALE = 1.0 / math.sqrt(HD * 3.0)
WBAND = 1280
DMAX = 4
NCORE = 8
WROW = NT * WBAND

TAPS = {}
NH_RUN = NH
NB_RUN = NB
FFN_RUN = True
WO_RUN = True
C2P_ON = True
P2C_ON = True
SAT_ON = True
PREP_RUN = True
REL_RUN = True
LN1_RUN = True
TRUNC = None
MARKS = []
FOLD_BANDS = True


def _cidx(r):
    r = np.asarray(r, np.int64)
    sign = np.sign(r).astype(np.float64)
    mid = 128
    abs_pos = np.where((r < mid) & (r > -mid), float(mid - 1),
                       np.abs(r)).astype(np.float64)
    log_pos = np.ceil(
        np.log(abs_pos / mid)
        / np.float64(np.float32(np.log((512 - 1) / mid))) * (mid - 1)) + mid
    bucket = np.where(abs_pos <= mid, r.astype(np.float64), log_pos * sign)
    return np.clip(bucket.astype(np.int64) + SPAN, 0, 2 * SPAN - 1)


def _g_tables():
    import ml_dtypes
    u = np.arange(WBAND)
    g_rev = np.zeros((2 * SPAN, WBAND), np.float32)
    g_fwd = np.zeros((2 * SPAN, WBAND), np.float32)
    g_rev[_cidx(639 - u), u] = 1.0
    g_fwd[_cidx(u - 639), u] = 1.0
    return (g_rev.astype(ml_dtypes.bfloat16), g_fwd.astype(ml_dtypes.bfloat16))


_UREV = _cidx(639 - np.arange(WBAND))
_UFWD = _cidx(np.arange(WBAND) - 639)
U_SATH_REV = int(np.argmax(_UREV == 511))
U_SATL_REV = int(np.argmax(_UREV == 0))
U_SATH_FWD = int(np.argmax(_UFWD == 511))
U_SATL_FWD = int(np.argmax(_UFWD == 0))


class Plan:
    def __init__(self, nc):
        self.nc = nc
        self.steps = []
        self.sem = {}
        self.cnt = {}
        self.es = contextlib.ExitStack()
        for s_ in ("d_sync", "d_scalar", "d_gpsimd", "p_vector", "p_scalar",
                   "p_tensor", "p_gpsimd"):
            self.sem[s_] = self.es.enter_context(nc.semaphore(s_))
            self.cnt[s_] = 0

    def emit(self, eng, fn):
        self.steps.append((eng, fn))

    def wait(self, eng, tok):
        if tok is None:
            return
        name, val = tok
        h = self.sem[name]
        self.emit(eng, lambda e, h=h, v=val: e.wait_ge(h, v))

    def tok(self, name):
        return (name, self.cnt[name])

    def dma(self, q, out, in_, after=(), slow=False):
        for t in after:
            self.wait(q, t)
        name = "d_" + q
        self.cnt[name] += 16
        h = self.sem[name]
        if slow:
            self.emit(q, lambda e, o=out, i=in_, h=h:
                      e.dma_start(out=o, in_=i,
                                  allow_slow_non_contiguous=True).then_inc(h, 16))
        else:
            self.emit(q, lambda e, o=out, i=in_, h=h:
                      e.dma_start(out=o, in_=i).then_inc(h, 16))
        return (name, self.cnt[name])

    def op(self, eng, method, *args, after=(), **kw):
        for t in after:
            self.wait(eng, t)
        name = "p_" + eng
        h = self.sem[name]
        pre = self.cnt[name]
        self.cnt[name] += 1
        v = self.cnt[name]

        def fn(e, m=method, a=args, k=kw, h=h, pre=pre):
            e.wait_ge(h, pre)
            getattr(e, m)(*a, **k).then_inc(h, 1)
        self.emit(eng, fn)
        return (name, v)

    def top(self, method, *args, after=(), **kw):
        for t in after:
            self.wait("tensor", t)
        cell = {"inc": 0}
        self._last_top = cell
        h = self.sem["p_tensor"]

        def fn(e, m=method, a=args, k=kw, cell=cell, h=h):
            inst = getattr(e, m)(*a, **k)
            if cell["inc"]:
                inst.then_inc(h, cell["inc"])
        self.emit("tensor", fn)

    def tmark(self):
        name = "p_tensor"
        cell = getattr(self, "_last_top", None)
        if cell is None:
            return (name, self.cnt[name])
        self.cnt[name] += 1
        cell["inc"] += 1
        return (name, self.cnt[name])

    def engine_gates(self):
        return [self.tok("p_vector"), self.tok("p_scalar"),
                self.tok("p_gpsimd"), self.tmark(),
                self.tok("d_sync"), self.tok("d_scalar"),
                self.tok("d_gpsimd")]


def rawap(t, offset, dims):
    ap = t.ap().copy() if callable(getattr(t, "ap", None)) else t.copy()
    ap.offset = offset
    while len(ap.ap) > 1:
        ap.ap.pop()
    ap.ap[0] = list(dims[0])
    for d in dims[1:]:
        ap.ap.append(list(d))
    return ap


def build_nc():
    nc = bass.Bass("TRN2", target_bir_lowering=False,
                   detect_race_conditions=False,
                   dynamic_dma_scratch_size=16384)
    p = Plan(nc)

    def _mk(lbl):
        MARKS.append((lbl, len(p.steps)))
    dt = {}

    def din(name, shape, dtype=F32):
        dt[name] = nc.dram_tensor(name, shape, dtype, kind="ExternalInput").ap()
        return dt[name]

    x_in = din("x", [NB * S, H])
    for nm in ("Wq", "Wk", "Wv", "Wo", "Wpk", "Wpq"):
        din(nm, [H, H])
    for nm in ("bq", "bk", "bv", "bo", "bpk", "bpq", "ln_g", "ln_b"):
        din(nm, [H])
    din("rel", [2 * SPAN, H])
    din("wg", [INNER, H])
    din("wu", [INNER, H])
    din("wd", [H, INNER])
    din("Grev", [2 * SPAN, WBAND], BF16)
    din("Gfwd", [2 * SPAN, WBAND], BF16)
    din("identb", [128, 128], BF16)
    din("identf", [128, 128], F32)
    din("bqk_t", [64, 80])

    out_hbm = nc.dram_tensor("out", [NB * S, H], F32, kind="ExternalOutput").ap()
    x2_s = nc.dram_tensor("x2_s", [NB * S, H], F32, kind="Internal").ap()
    tabs_s = nc.dram_tensor("tabs_s", [NH * 64, 2 * WBAND], BF16,
                            kind="Internal").ap()
    ctx_s = nc.dram_tensor("ctx_s", [H, S], BF16, kind="Internal").ap()
    wgT_s = nc.dram_tensor("wgT_s", [H, INNER], BF16, kind="Internal").ap()
    wuT_s = nc.dram_tensor("wuT_s", [H, INNER], BF16, kind="Internal").ap()
    wdT_s = nc.dram_tensor("wdT_s", [INNER, H], BF16, kind="Internal").ap()
    guq_s = nc.dram_tensor("guq_s", [NB * NT, 128 * INNER], BF16,
                           kind="Internal").ap()

    tap_out = {}

    def tap(tag, ap, after):
        if tag not in TAPS:
            return
        shape, dtyp = TAPS[tag]
        nm = "dbg_" + tag
        if nm not in tap_out:
            tap_out[nm] = nc.dram_tensor(nm, shape, dtyp,
                                         kind="ExternalOutput").ap()
        p.dma("sync", tap_out[nm], ap, after=after)

    def sb_in(stk, name, shape, dtyp):
        return stk.enter_context(nc.sbuf_tensor(name, shape, dtyp))

    es0 = p.es

    def sb(name, shape, dtyp):
        return sb_in(es0, name, shape, dtyp)

    identb = sb("identb_t", [128, 128], BF16)
    identf = sb("identf_t", [128, 128], F32)
    ones_bf = sb("ones_bf", [128, 128], BF16)
    ones_fr = sb("ones_fr", [1, 128], F32)
    ones64 = sb("ones64", [128, 64], F32)
    ones_b64 = sb("ones_b64", [64, 1], BF16)
    onesc_f = sb("onesc_f", [128, 1], F32)
    bvec = sb("bvec", [128, 5 * 16], F32)
    scal = sb("scal", [1, 16], F32)
    scal2 = sb("scal2", [128, 8], F32)
    st1 = sb("st1", [128, 8], F32)
    xsq = sb("xsq", [128, H], F32)
    xt_f = [sb(f"xt_f{i}", [128, H], F32) for i in range(2)]
    thrv = sb("thrv", [128, 8], F32)
    rowbuf = sb("rowbuf", [2, S], BF16)
    normbuf = sb("normbuf", [1, 2 * NH], F32)
    dsc_all = sb("dsc_all", [128, NB * NT], F32)

    ps_a = es0.enter_context(nc.psum_tensor("ps_a", [128, 512], F32))
    ps_b = es0.enter_context(nc.psum_tensor("ps_b", [128, 512], F32))
    ps_t = es0.enter_context(nc.psum_tensor("ps_t", [128, 1024], BF16))
    ps_t2 = es0.enter_context(nc.psum_tensor("ps_t2", [128, 1024], BF16))

    psfree = {"a": None, "b": None, "t": None, "t2": None, "ctx": None}
    xtfree = [None, None]

    t_id = p.dma("sync", identb[:], dt["identb"])
    t_if = p.dma("sync", identf[:], dt["identf"])
    p.dma("sync", bvec[0:64, 0:80], dt["bqk_t"])
    t_ldc = p.tok("d_sync")

    t_ones = p.op("vector", "memset", ones_bf[:], 1.0)
    p.op("vector", "memset", ones_fr[:], 1.0)
    t_o64 = p.op("vector", "memset", ones64[:], 1.0)
    p.op("vector", "memset", ones_b64[:], 1.0)
    t_onesf = p.op("vector", "memset", onesc_f[:], 1.0)
    t_bq = p.op("vector", "tensor_scalar_mul", bvec[0:64, 0:16],
                bvec[0:64, 0:16], SCALE, after=[t_ldc])
    t_c127 = p.op("vector", "memset", thrv[:, 5:6], 127.0)

    shared = {"xsq": None}

    def layernorm(xf, tgt, W, gamma=None, beta=None, after=(), apply_after=()):
        aft0 = list(after) + ([shared["xsq"]] if shared["xsq"] else [])
        t0 = p.op("scalar", "activation", xsq[:, 0:W], xf, AF.Square,
                  after=aft0)
        t1 = p.op("vector", "reduce_sum", st1[:, 0:1], xf, AX, after=after)
        t2 = p.op("vector", "reduce_sum", st1[:, 1:2], xsq[:, 0:W], AX,
                  after=[t0])
        shared["xsq"] = t2
        p.op("vector", "tensor_scalar_mul", st1[:, 2:3], st1[:, 0:1], 1.0 / W,
             after=[t1])
        p.op("vector", "tensor_scalar_mul", st1[:, 3:4], st1[:, 1:2], 1.0 / W,
             after=[t2])
        p.op("vector", "tensor_tensor", st1[:, 4:5], st1[:, 2:3], st1[:, 2:3],
             AluOpType.mult)
        tvar = p.op("vector", "scalar_tensor_tensor", st1[:, 5:6], st1[:, 3:4],
                    1e-5, st1[:, 4:5], AluOpType.add, AluOpType.subtract)
        t3a = p.op("scalar", "activation", st1[:, 4:5], st1[:, 5:6], AF.Sqrt,
                   after=[tvar])
        t3 = p.op("vector", "reciprocal", st1[:, 6:7], st1[:, 4:5],
                  after=[t3a])
        if gamma is None:
            return p.op("vector", "tensor_scalar", tgt, xf, st1[:, 2:3],
                        st1[:, 6:7], AluOpType.subtract, AluOpType.mult,
                        after=[t3] + list(apply_after))
        ta = p.op("vector", "tensor_scalar", xsq[:, 0:W], xf, st1[:, 2:3],
                  st1[:, 6:7], AluOpType.subtract, AluOpType.mult, after=[t3])
        tb = p.op("vector", "tensor_tensor", xsq[:, 0:W], xsq[:, 0:W], gamma,
                  AluOpType.mult, after=[ta])
        shared["xsq"] = tb
        return p.op("vector", "tensor_tensor", tgt, xsq[:, 0:W], beta,
                    AluOpType.add, after=[tb] + list(apply_after))

    es_prep = contextlib.ExitStack()
    wtile = [sb_in(es_prep, f"wtile{i}", [128, INNER], F32) for i in range(2)]
    g1big = sb_in(es_prep, "g1big", [128, INNER], F32)
    ternt = sb_in(es_prep, "ternt", [128, INNER], BF16)
    stage = sb_in(es_prep, "stage", [128, INNER], BF16)
    wtfree = [None, None]

    def prep_weight(wname, dst_hbm, nrow, ncol, scol):
        ntile = nrow // 128
        tacc = p.op("vector", "memset", scal2[:, 0:1], 0.0)
        for t in range(ntile):
            td = p.dma(("sync", "scalar", "gpsimd")[t % 3],
                       wtile[t % 2][:, 0:ncol],
                       dt[wname][128 * t:128 * (t + 1), :],
                       after=[wtfree[t % 2]] if wtfree[t % 2] else [])
            tr = p.op("vector", "reduce_sum", scal2[:, 1:2],
                      wtile[t % 2][:, 0:ncol], AX, apply_absolute_value=True,
                      after=[td])
            tacc = p.op("vector", "tensor_tensor", scal2[:, 0:1],
                        scal2[:, 0:1], scal2[:, 1:2], AluOpType.add,
                        after=[tr])
            wtfree[t % 2] = tacc
        p.top("matmul", ps_a[0:1, 0:1], onesc_f[:], scal2[:, 0:1],
              start=True, stop=True, after=[tacc, t_onesf, psfree["a"]])
        tm = p.tmark()
        t5 = p.op("vector", "tensor_scalar", scal[0:1, scol:scol + 1],
                  ps_a[0:1, 0:1], 1.0 / (nrow * ncol), 1e-5,
                  AluOpType.mult, AluOpType.max, after=[tm])
        t6 = p.op("vector", "tensor_scalar_mul", scal[0:1, 15:16],
                  scal[0:1, scol:scol + 1], 0.5)
        p.top("matmul", ps_a[:, 4:5], ones_fr[:],
              scal[0:1, 15:16], start=True, stop=True, after=[t6, t5])
        tb128 = p.tmark()
        tpos = p.op("vector", "tensor_copy", thrv[:, 0:1], ps_a[:, 4:5],
                    after=[tb128])
        psfree["a"] = tpos
        tneg = p.op("vector", "tensor_scalar_mul", thrv[:, 1:2], thrv[:, 0:1],
                    -1.0)
        nb = ncol // 128
        stage_free = None
        tern_free = [None]
        for t in range(ntile):
            td = p.dma(("sync", "scalar", "gpsimd")[t % 3],
                       wtile[t % 2][:, 0:ncol],
                       dt[wname][128 * t:128 * (t + 1), :],
                       after=[wtfree[t % 2]])
            tg2 = p.op("vector", "tensor_scalar", g1big[:, 0:ncol],
                       wtile[t % 2][:, 0:ncol], thrv[:, 1:2], None,
                       AluOpType.is_le, after=[td, tneg])
            tq = p.op("vector", "scalar_tensor_tensor", ternt[:, 0:ncol],
                      wtile[t % 2][:, 0:ncol], thrv[:, 0:1], g1big[:, 0:ncol],
                      AluOpType.is_ge, AluOpType.subtract,
                      after=[tg2, tpos] + ([tern_free[0]] if tern_free[0]
                                           else []))
            wtfree[t % 2] = tq
            tev = None
            for g0 in range(0, nb, 8):
                g1 = min(nb, g0 + 8)
                key = "t" if (g0 // 8) % 2 == 0 else "t2"
                pst = ps_t if key == "t" else ps_t2
                for c in range(g0, g1):
                    aft = [tq, t_id, psfree[key]] if c == g0 else ()
                    p.top("matmul",
                          pst[:, 128 * (c - g0):128 * (c - g0) + 128],
                          ternt[:, 128 * c:128 * (c + 1)], identb[:],
                          start=(c == g0), stop=(c == g1 - 1),
                          is_transpose=True, after=aft)
                tt = p.tmark()
                tev = p.op("scalar", "copy", stage[:, 128 * g0:128 * g1],
                           pst[:, 0:128 * (g1 - g0)],
                           after=[tt, stage_free] if stage_free else [tt])
                psfree[key] = tev
                tern_free[0] = tt
            dstv = rawap(dst_hbm, 128 * t,
                         [[nrow, 128], [128 * nrow, nb], [1, 128]])
            srcv = rawap(stage, 0, [[INNER, 128], [128, nb], [1, 128]])
            stage_free = p.dma("sync", dstv, srcv, after=[tev])
        return stage_free

    if PREP_RUN:
        tw_g = prep_weight("wg", wgT_s, INNER, H, 2)
        tw_u = prep_weight("wu", wuT_s, INNER, H, 3)
        tw_d = prep_weight("wd", wdT_s, H, INNER, 4)
    else:
        tw_g = tw_u = tw_d = p.tok("d_sync")
    t_prep_gates = p.engine_gates()
    es_prep.close()

    es_attn = contextlib.ExitStack()

    def sba(name, shape, dtyp):
        return sb_in(es_attn, name, shape, dtyp)

    grev4 = sba("grev4", [128, 4 * WBAND], BF16)
    gfwd4 = sba("gfwd4", [128, 4 * WBAND], BF16)
    re_fm = sba("re_fm", [128, 8 * 512], BF16)
    xn_fm = sba("xn_fm", [128, 8 * S], BF16)
    ctx_h = sba("ctx_h", [64, S], BF16)
    wo_ctx = [sba(f"wo_ctx{i}", [128, 8 * 128], BF16) for i in range(2)]
    Kx = sba("Kx", [128, S], BF16)
    Qx = sba("Qx", [128, S], BF16)
    QxH = sba("QxH", [128, S], BF16)
    QxL = sba("QxL", [128, S], BF16)
    v_aug = sba("v_aug", [128, NT * (HD + 1)], BF16)
    bc_band = sba("bc_band", [128, WROW], BF16)
    bp_band = [sba(f"bp_band{i}", [128, WBAND], BF16) for i in range(2)]
    pt_buf = [sba(f"pt{i}", [128, S], BF16) for i in range(2)]
    c2pT = [sba(f"c2pT{i}", [128, 9 * 128], BF16) for i in range(2)]
    p2cT = [sba(f"p2cT{i}", [128, 9 * 128], BF16) for i in range(2)]
    ctile = [sba(f"ctile{i}", [128, 9 * 128], BF16) for i in range(2)]
    tabR = sba("tabR", [64, WBAND], BF16)
    tabF = sba("tabF", [64, WBAND], BF16)
    pkpm = sba("pkpm", [128, 4 * 64], BF16)
    pqpm = sba("pqpm", [128, 4 * 64], BF16)
    wsl = sba("wslice", [128, 5 * 512], BF16)
    den = sba("den", [128, S], F32)
    mrow = den[0:1, :]
    xn_tok = sba("xn_tok", [128, H], BF16)
    ps_ctx = es_attn.enter_context(nc.psum_tensor("ps_ctx", [128, S], F32))
    qsq = ctx_h
    es_rel = contextlib.ExitStack()
    lnb_g = sb_in(es_rel, "lnb_g", [128, H], F32)
    lnb_b = sb_in(es_rel, "lnb_b", [128, H], F32)
    re_tok = sb_in(es_rel, "re_tok", [128, H], BF16)

    for c in range(4):
        p.dma("sync", grev4[:, c * WBAND:(c + 1) * WBAND],
              dt["Grev"][c * 128:(c + 1) * 128, :])
        t_gtab = p.dma("sync", gfwd4[:, c * WBAND:(c + 1) * WBAND],
                       dt["Gfwd"][c * 128:(c + 1) * 128, :])
    t_g0 = p.dma("sync", lnb_g[0:1, :], rawap(dt["ln_g"], 0, [[H, 1], [1, H]]))
    t_b0 = p.dma("sync", lnb_b[0:1, :], rawap(dt["ln_b"], 0, [[H, 1], [1, H]]))
    t_g = t_b = None
    for (t0_, buf) in ((t_g0, lnb_g), (t_b0, lnb_b)):
        for nk in range(2):
            key = "a" if nk == 0 else "b"
            pst = ps_a if nk == 0 else ps_b
            p.top("matmul", pst[:, 0:512], ones_fr[:],
                  buf[0:1, 512 * nk:512 * (nk + 1)],
                  start=True, stop=True, after=[t0_, psfree[key]])
            tmm = p.tmark()
            tkn = p.op("vector", "tensor_copy",
                       buf[:, 512 * nk:512 * (nk + 1)], pst[:, 0:512],
                       after=[tmm])
            psfree[key] = tkn
        if buf is lnb_g:
            t_g = tkn
        else:
            t_b = tkn

    t_refm = None
    for rt in range(4 if REL_RUN else 0):
        tl = p.dma("sync", xt_f[rt % 2][:],
                   dt["rel"][128 * rt:128 * (rt + 1), :],
                   after=[xtfree[rt % 2], t_prep_gates[3]]
                   if xtfree[rt % 2] else [t_prep_gates[3]])
        ta = layernorm(xt_f[rt % 2][:], re_tok[:], H, gamma=lnb_g[:],
                       beta=lnb_b[:], after=[tl, t_g, t_b],
                       apply_after=[t_refm] if t_refm else [])
        xtfree[rt % 2] = ta
        for c in range(8):
            aft = [ta, t_id, psfree["t"]] if c == 0 else ()
            p.top("matmul", ps_t[:, 128 * c:128 * (c + 1)],
                  re_tok[:, 128 * c:128 * (c + 1)], identb[:],
                  start=(c == 0), stop=(c == 7), is_transpose=True, after=aft)
        tt = p.tmark()
        t_refm = p.op("scalar", "copy",
                      rawap(re_fm, 128 * rt, [[8 * 512, 128], [512, 8], [1, 128]]),
                      ps_t[:, 0:1024],
                      after=[tt, t_refm] if t_refm else [tt])
        psfree["t"] = t_refm
    if not REL_RUN:
        t_refm = p.tok("d_sync")
    es_rel.close()

    def win(kt):
        return max(0, kt - DMAX), min(NT, kt + DMAX + 1)

    t_batch_gate = None
    t_wo_loaded = None
    t_wsl_free = None
    normtok = [None] * (2 * NH)
    tabstore_tok = [None] * NH
    wo_view = bc_band

    for b in range(NB_RUN):
        _mk(f"b{b}_ln1")
        t_xn = None
        xn_tt = [None]
        for t in range(NT if LN1_RUN else 0):
            aft = [xtfree[t % 2]] if xtfree[t % 2] else []
            if t_batch_gate:
                aft.append(t_batch_gate)
            tl = p.dma("sync", xt_f[t % 2][:],
                       x_in[b * S + 128 * t:b * S + 128 * (t + 1), :],
                       after=aft)
            ta = layernorm(xt_f[t % 2][:], xn_tok[:], H, after=[tl],
                           apply_after=[xn_tt[0]] if xn_tt[0] else [])
            xtfree[t % 2] = ta
            for c in range(8):
                aft = [ta, psfree["t"]] + ([t_batch_gate] if t_batch_gate else []) \
                    if c == 0 else ()
                p.top("matmul", ps_t[:, 128 * c:128 * (c + 1)],
                      xn_tok[:, 128 * c:128 * (c + 1)], identb[:],
                      start=(c == 0), stop=(c == 7), is_transpose=True, after=aft)
            tt = p.tmark()
            xn_tt[0] = tt
            t_xn = p.op("scalar", "copy",
                        rawap(xn_fm, 128 * t, [[8 * S, 128], [S, 8], [1, 128]]),
                        ps_t[:, 0:1024], after=[tt, t_xn] if t_xn else [tt])
            psfree["t"] = t_xn
        if not LN1_RUN:
            t_xn = p.tok("d_sync")
        tap(f"xnfm_{b}", xn_fm[:], [t_xn])

        if b == 0:
            tcr1 = p.op("vector", "memset", rowbuf[0:1, :], 1.0)
            tks = []
            for dst_ten, row in ((Kx, 64), (Kx, 66), (Kx, 68),
                                 (QxH, 65), (QxL, 67)):
                tks.append(p.dma("sync", dst_ten[row:row + 1, :],
                                 rowbuf[0:1, :], after=[tcr1]))
            tcr0 = p.op("vector", "memset", rowbuf[0:1, :], 0.0,
                        after=[tks[-1]])
            for dst_ten, row in ((QxL, 65), (QxL, 66)):
                tks.append(p.dma("sync", dst_ten[row:row + 1, :],
                                 rowbuf[0:1, :], after=[tcr0]))
            t_cr = tks[-1]
            t_vone = p.op("vector", "memset",
                          rawap(v_aug, HD, [[NT * (HD + 1), 128], [HD + 1, NT]]),
                          1.0)
        t_head_gate = t_batch_gate

        for h in range(NH_RUN):
            _mk(f"head{h}_start")
            hg = [t_head_gate] if t_head_gate else []
            for k, nm in enumerate(("Wq", "Wk", "Wv", "Wpk", "Wpq")):
                tw = p.dma("gpsimd",
                           rawap(wsl, 512 * k, [[5 * 512, 128], [64, 8], [1, 64]]),
                           rawap(dt[nm], 64 * h, [[H, 128], [128 * H, 8], [1, 64]]),
                           after=([t_wsl_free] if t_wsl_free else hg)
                           if k == 0 else ())
            _mk(f"h{h}_wsl")
            for (wofs, dst_ten, bcol, sclv) in ((0, Qx, 0, SCALE),
                                                (1, Kx, 1, 1.0)):
                for c in range(8):
                    for nk in range(4):
                        aft = ([tw, t_xn, t_bq, psfree["ctx"]]
                               if c == 0 and nk == 0 else ())
                        p.top("matmul", ps_ctx[0:64, 512 * nk:512 * (nk + 1)],
                              wsl[:, 512 * wofs + 64 * c:512 * wofs + 64 * (c + 1)],
                              xn_fm[:, S * c + 512 * nk:S * c + 512 * (nk + 1)],
                              start=(c == 0), stop=(c == 7), after=aft)
                tmm = p.tmark()
                tev = p.op("vector", "tensor_scalar", dst_ten[0:64, :],
                           ps_ctx[0:64, :], float(sclv),
                           bvec[0:64, 16 * bcol + h:16 * bcol + h + 1],
                           AluOpType.mult, AluOpType.add, after=[tmm] + hg)
                psfree["ctx"] = tev
                if wofs == 0:
                    t_qf = tev
                else:
                    t_kf = tev
            tqc1 = p.op("vector", "tensor_copy", QxH[0:64, :], Qx[0:64, :],
                        after=[t_qf])
            tqc2 = p.op("vector", "tensor_copy", QxL[0:64, :], Qx[0:64, :])
            tap(f"qx_{b}_{h}", Qx[0:64, :], [t_qf])
            tap(f"kx_{b}_{h}", Kx[0:64, :], [t_kf])

            _mk(f"h{h}_qk")
            for nk in range(4):
                for c in range(8):
                    aft = ([t_kf, psfree["ctx"]] + hg if nk == 0 and c == 0
                           else ())
                    p.top("matmul", ps_ctx[0:64, 512 * nk:512 * (nk + 1)],
                          wsl[:, 2 * 512 + 64 * c:2 * 512 + 64 * (c + 1)],
                          xn_fm[:, S * c + 512 * nk:S * c + 512 * (nk + 1)],
                          start=(c == 0), stop=(c == 7), after=aft)
            tmm = p.tmark()
            if b > 0:
                t_wsl_free = tmm
            vd_st = ctx_h
            t_vd = p.op("vector", "tensor_scalar", vd_st[0:64, :],
                        ps_ctx[0:64, :], 1.0,
                        bvec[0:64, 32 + h:32 + h + 1],
                        AluOpType.mult, AluOpType.add, after=[tmm] + hg)
            psfree["ctx"] = t_vd
            for t in range(NT):
                aft = [t_vd, t_id, psfree["t"]] if t == 0 else ()
                p.top("matmul", ps_t[:, 64 * t:64 * (t + 1)],
                      vd_st[0:64, 128 * t:128 * (t + 1)], identb[0:64, 0:64],
                      start=(t == 0), stop=(t == NT - 1),
                      is_transpose=True, after=aft)
            t_vd_read = p.tmark()
            t_vev = p.op("scalar", "copy",
                         rawap(v_aug, 0,
                               [[NT * (HD + 1), 128], [HD + 1, NT], [1, HD]]),
                         ps_t[:, 0:NT * HD], after=[t_vd_read, t_vone])
            psfree["t"] = t_vev
            tap(f"vaug_{b}_{h}", v_aug[:, :], [t_vev])

            _mk(f"h{h}_v")
            if b > 0:
                t_tabR = p.dma("sync", tabR[:],
                               tabs_s[64 * h:64 * h + 64, 0:WBAND],
                               after=hg + [tabstore_tok[h]])
                t_tabF = p.dma("sync", tabF[:],
                               tabs_s[64 * h:64 * h + 64, WBAND:2 * WBAND])
                tap(f"tabR_{b}_{h}", tabR[:], [t_tabR])
                tap(f"tabF_{b}_{h}", tabF[:], [t_tabF])
            for (wofs, dstt, bcol, key) in ((3, pkpm, 48, "a"),
                                            (4, pqpm, 64, "b")) if b == 0 \
                    else ():
                pst = ps_a if key == "a" else ps_b
                pofs = 512 * (wofs - 3)
                pdst = bc_band[0:64, pofs:pofs + 512]
                for c in range(8):
                    aft = ([t_refm, tw, psfree[key]] + hg if c == 0 else ())
                    p.top("matmul", pst[0:64, 0:512],
                          wsl[:, 512 * wofs + 64 * c:512 * wofs + 64 * (c + 1)],
                          re_fm[:, 512 * c:512 * (c + 1)],
                          start=(c == 0), stop=(c == 7), after=aft)
                tmm = p.tmark()
                t_wsl_free = tmm
                t_pd = p.op("vector", "tensor_scalar", pdst,
                            pst[0:64, 0:512], 1.0,
                            bvec[0:64, bcol + h:bcol + h + 1],
                            AluOpType.mult, AluOpType.add, after=[tmm] + hg)
                psfree[key] = t_pd
                for pc in range(4):
                    aft = [t_pd, psfree["t2"]] if pc == 0 else ()
                    p.top("matmul", ps_t2[:, 64 * pc:64 * (pc + 1)],
                          bc_band[0:64, pofs + 128 * pc:pofs + 128 * (pc + 1)],
                          identb[0:64, 0:64], start=(pc == 0), stop=(pc == 3),
                          is_transpose=True, after=aft)
                tt = p.tmark()
                t_pm = p.op("scalar", "copy", dstt[:, 0:256],
                            ps_t2[:, 0:256], after=[tt])
                psfree["t2"] = t_pm
                if wofs == 3:
                    t_pk = t_pm
                else:
                    t_pq = t_pm
            for (srct, gt, dstt, sclt, tprev) in (
                    (pkpm, grev4, tabR, 1.0, t_pk),
                    (pqpm, gfwd4, tabF, SCALE, t_pq)) if b == 0 else ():
                t_tab = None
                for uc in range(3):
                    n0 = 512 * uc
                    n1 = min(WBAND, n0 + 512)
                    key = "a" if uc % 2 == 0 else "b"
                    pst = ps_a if key == "a" else ps_b
                    for pc in range(4):
                        aft = ([tprev, t_gtab, psfree[key]] + hg
                               if pc == 0 else ())
                        p.top("matmul", pst[0:64, 0:n1 - n0],
                              srct[:, 64 * pc:64 * (pc + 1)],
                              gt[:, WBAND * pc + n0:WBAND * pc + n1],
                              start=(pc == 0), stop=(pc == 3), after=aft)
                    tmm = p.tmark()
                    t_tab = p.op("scalar", "activation", dstt[:, n0:n1],
                                 pst[0:64, 0:n1 - n0], AF.Copy,
                                 scale=float(sclt),
                                 after=[tmm, t_tab] if t_tab else [tmm])
                    psfree[key] = t_tab
                if sclt == 1.0:
                    t_tabR = t_tab
                else:
                    t_tabF = t_tab
            if b == 0:
                p.dma("scalar", tabs_s[64 * h:64 * h + 64, 0:WBAND],
                      tabR[:], after=[t_tabR])
                tabstore_tok[h] = p.dma(
                    "scalar", tabs_s[64 * h:64 * h + 64, WBAND:2 * WBAND],
                    tabF[:], after=[t_tabF])
                tap(f"tabR_{b}_{h}", tabR[:], [t_tabR])
                tap(f"tabF_{b}_{h}", tabF[:], [t_tabF])

            _mk(f"h{h}_tabs")
            t_sat = []
            t_rowfree = None
            for (tab, t_tt, ucolA, ucolB, opd, t_opd,
                 dstA, rowA, dstB, rowB) in (
                    (tabR, t_tabR, U_SATH_REV, U_SATL_REV, Qx, t_qf,
                     QxH, 66, QxL, 68),
                    (tabF, t_tabF, U_SATH_FWD, U_SATL_FWD, Kx, t_kf,
                     Kx, 65, Kx, 67)):
                c0, c1 = (ucolA, ucolB) if ucolA < ucolB else (ucolB, ucolA)
                rA = 0 if ucolA == c0 else 1
                trow = None
                for nk in range(4):
                    aft = ([t_tt, t_opd, psfree["a"]] if nk == 0
                           else [psfree["a"]])
                    p.top("matmul", ps_a[0:2, 0:512],
                          rawap(tab, c0, [[WBAND, 64], [c1 - c0, 2]]),
                          opd[0:64, 512 * nk:512 * (nk + 1)],
                          start=True, stop=True, after=aft)
                    tmm = p.tmark()
                    aft2 = [tmm] + ([t_rowfree] if t_rowfree and nk == 0
                                    else [])
                    trow = p.op("scalar", "copy",
                                rowbuf[0:2, 512 * nk:512 * (nk + 1)],
                                ps_a[0:2, 0:512], after=aft2)
                    psfree["a"] = trow
                tA = p.dma("sync", dstA[rowA:rowA + 1, :],
                           rowbuf[rA:rA + 1, :], after=[trow])
                t_rowfree = p.dma("sync", dstB[rowB:rowB + 1, :],
                                  rowbuf[1 - rA:2 - rA, :], after=[trow])
                t_sat += [tA, t_rowfree]

            _mk(f"h{h}_sat")
            t1 = p.op("vector", "reduce_max", scal2[0:64, 2:3], Kx[0:64, :],
                      AX, apply_absolute_value=True, after=[t_kf])
            p.top("matmul", ps_a[0:1, 8:72], scal2[0:64, 2:3],
                  identf[0:64, 0:64], start=True, stop=True, is_transpose=True,
                  after=[t1, t_if, psfree["a"]])
            tmm = p.tmark()
            t2 = p.op("vector", "reduce_max", scal[0:1, 5:6], ps_a[0:1, 8:72],
                      AX, after=[tmm])
            psfree["a"] = t2
            tnorm = {}
            t_qsq_free = None
            for (tab, col, t_tt) in (((tabR, 8, t_tabR), (tabF, 10, t_tabF))
                                     if b == 0 else ()):
                tsq = p.op("scalar", "activation", qsq[0:64, 0:WBAND], tab[:],
                           AF.Square,
                           after=[t_tt] + t_sat + ([t_qsq_free]
                                                   if t_qsq_free else []))
                tmx = None
                for uc in range(3):
                    n0 = 512 * uc
                    n1 = min(WBAND, n0 + 512)
                    p.top("matmul", ps_a[0:1, 0:n1 - n0], ones_b64[:],
                          qsq[0:64, n0:n1], start=True, stop=True,
                          after=[tsq, psfree["a"]] if uc == 0
                          else [psfree["a"]])
                    tmm = p.tmark()
                    tr = p.op("vector", "reduce_max", scal[0:1, 6:7],
                              ps_a[0:1, 0:n1 - n0], AX, after=[tmm])
                    psfree["a"] = tr
                    if tmx is None:
                        tmx = p.op("vector", "tensor_copy",
                                   scal[0:1, col:col + 1], scal[0:1, 6:7],
                                   after=[tr])
                    else:
                        tmx = p.op("vector", "tensor_tensor",
                                   scal[0:1, col:col + 1],
                                   scal[0:1, col:col + 1], scal[0:1, 6:7],
                                   AluOpType.max, after=[tr])
                t_qsq_free = p.tmark()
                nbc = 2 * h + (0 if col == 8 else 1)
                tnorm[col] = p.op("scalar", "activation",
                                  normbuf[0:1, nbc:nbc + 1],
                                  scal[0:1, col:col + 1], AF.Sqrt,
                                  after=[tmx] +
                                  ([normtok[nbc]] if normtok[nbc] else []))
                normtok[nbc] = tnorm[col]
            tn8 = normtok[2 * h]
            tn10 = normtok[2 * h + 1]
            p.op("vector", "tensor_tensor", scal[0:1, 11:12], scal[0:1, 5:6],
                 normbuf[0:1, 2 * h:2 * h + 1], AluOpType.add,
                 after=[tn8, t2])
            tm1 = p.op("vector", "tensor_tensor", scal[0:1, 11:12],
                       scal[0:1, 11:12], scal[0:1, 11:12], AluOpType.mult)
            p.op("vector", "scalar_tensor_tensor", scal[0:1, 12:13],
                 scal[0:1, 5:6], 1.0, normbuf[0:1, 2 * h + 1:2 * h + 2],
                 AluOpType.bypass, AluOpType.mult, after=[tn10])
            tm2 = p.op("vector", "tensor_scalar_add", scal[0:1, 12:13],
                       scal[0:1, 12:13], 1.0)
            tqs = p.op("scalar", "activation", qsq[0:64, 0:S], Qx[0:64, :],
                       AF.Square, after=[t_qf, t_qsq_free] + t_sat)
            t_mr = None
            for nk in range(4):
                p.top("matmul", ps_a[0:1, 0:512], ones_b64[:],
                      qsq[0:64, 512 * nk:512 * (nk + 1)],
                      start=True, stop=True,
                      after=[tqs, psfree["a"]] if nk == 0
                      else [psfree["a"]])
                tmm = p.tmark()
                t_mr = p.op("scalar", "activation",
                            den[0:1, 512 * nk:512 * (nk + 1)],
                            ps_a[0:1, 0:512], AF.Sqrt,
                            scale=scal[0:1, 11:12], after=[tmm, tm1])
                psfree["a"] = t_mr
            trb = p.op("vector", "tensor_scalar", rowbuf[0:1, :], den[0:1, :],
                       scal[0:1, 12:13], -1.0, AluOpType.add, AluOpType.mult,
                       after=[t_mr, tm2, t_rowfree])
            p.dma("sync", Qx[64:65, :], rowbuf[0:1, :], after=[trb])
            p.dma("sync", QxH[64:65, :], rowbuf[0:1, :], after=[tqc1])
            t_m = p.dma("sync", QxL[64:65, :], rowbuf[0:1, :], after=[tqc2])
            tap(f"mrow_{b}_{h}", den[0:1, :], [t_mr])

            _mk(f"h{h}_mrow")
            t_bc = None
            cfree = [psfree["ctx"]] * 4
            ridx = 0
            for qt in range(NT):
                for uc in range(3):
                    n0 = 512 * uc
                    n1 = min(WBAND, n0 + 512)
                    r = ridx % 4
                    ridx += 1
                    aft = [t_tabR, t_qf, cfree[r]]
                    p.top("matmul", ps_ctx[:, 512 * r:512 * r + n1 - n0],
                          Qx[0:64, 128 * qt:128 * (qt + 1)],
                          tabR[:, n0:n1], start=True, stop=True, after=aft)
                    tmm = p.tmark()
                    t_bc = p.op("vector" if uc == 1 else "scalar",
                                "tensor_copy" if uc == 1 else "copy",
                                bc_band[:, WBAND * qt + n0:WBAND * qt + n1],
                                ps_ctx[:, 512 * r:512 * r + n1 - n0],
                                after=[tmm])
                    cfree[r] = t_bc
            tap(f"bc_{b}_{h}", bc_band[:, 0:WBAND], [t_bc])

            def make_bp(kt, buf, rdtok=None, extra=()):
                tb = None
                for uc in range(3):
                    n0 = 512 * uc
                    n1 = min(WBAND, n0 + 512)
                    key = "a" if uc % 2 == 0 else "b"
                    pst = ps_a if key == "a" else ps_b
                    aft = list(extra) + [psfree[key], t_kf, t_tabF]
                    p.top("matmul", pst[:, 0:n1 - n0],
                          Kx[0:64, 128 * kt:128 * (kt + 1)],
                          tabF[:, n0:n1], start=True, stop=True, after=aft)
                    tmm = p.tmark()
                    if uc == 1:
                        tb = p.op("scalar", "copy", buf[:, n0:n1],
                                  pst[:, 0:n1 - n0], after=[tmm])
                    else:
                        tb = p.op("vector", "tensor_copy", buf[:, n0:n1],
                                  pst[:, 0:n1 - n0],
                                  after=[tmm, rdtok] if (rdtok and uc == 0)
                                  else [tmm])
                    psfree[key] = tb
                return tb

            bfree = {"ctile": [None, None], "p2cT": [None, None],
                     "pt": [None, None], "c2pT": [None, None],
                     "bp": [None, None], "bpdma": [None, None]}
            t_bp = [None, None]

            def prep_kt_dma(kt):
                i = kt % 2
                qlo, qhi = win(kt)
                nd = qhi - qlo
                srcv = rawap(bp_band[i], 639 + 128 * (qlo - kt),
                             [[WBAND - 1, 128], [1, 128 * nd]])
                tp2 = p.dma("sync", p2cT[i][:, 0:128 * nd], srcv,
                            after=[t_bp[i], bfree["p2cT"][i]])
                bfree["bpdma"][i] = tp2
                srcv = rawap(bc_band, WBAND * kt + 639 + 1152 * (qlo - kt),
                             [[WROW - 1, 128], [1152, nd], [1, 128]])
                tct = p.dma("scalar", ctile[i][:, 0:128 * nd], srcv,
                            after=[t_bc, bfree["ctile"][i]])
                return tp2, tct

            def prep_kt_pe(kt, tp2, tct):
                i = kt % 2
                qlo, qhi = win(kt)
                nd = qhi - qlo
                tev = None
                for d in range(nd):
                    if d == 0:
                        aft = [tct, psfree["t2"]]
                    elif d % 8 == 0:
                        aft = [tev]
                    else:
                        aft = ()
                    p.top("matmul",
                          ps_t2[:, 128 * (d % 8):128 * (d % 8) + 128],
                          ctile[i][:, 128 * d:128 * (d + 1)], identb[:],
                          start=(d % 8 == 0),
                          stop=(d % 8 == 7 or d == nd - 1),
                          is_transpose=True, after=aft)
                    if d % 8 == 7 or d == nd - 1:
                        g0 = (d // 8) * 8
                        tt = p.tmark()
                        aft2 = [tt] + ([bfree["c2pT"][i]]
                                       if bfree["c2pT"][i] else [])
                        if FOLD_BANDS:
                            if g0 == 0:
                                aft2.append(tp2)
                            tev = p.op("vector", "tensor_tensor",
                                       c2pT[i][:, 128 * g0:128 * (d + 1)],
                                       ps_t2[:, 0:128 * (d + 1 - g0)],
                                       p2cT[i][:, 128 * g0:128 * (d + 1)],
                                       AluOpType.add, after=aft2)
                        else:
                            tev = p.op("scalar", "copy",
                                       c2pT[i][:, 128 * g0:128 * (d + 1)],
                                       ps_t2[:, 0:128 * (d + 1 - g0)],
                                       after=aft2)
                psfree["t2"] = tev
                bfree["ctile"][i] = p.tok("p_tensor")
                if FOLD_BANDS:
                    bfree["p2cT"][i] = tev
                return tp2, tev

            _mk(f"h{h}_bc")
            t_bp[0] = make_bp(0, bp_band[0])
            t_prep = [None, None]
            t_prep[0] = prep_kt_pe(0, *prep_kt_dma(0))

            for kt in range(NT):
                _mk(f"h{h}_kt{kt}")
                i = kt % 2
                qlo, qhi = win(kt)
                pt = pt_buf[i]
                tp2, tc2 = t_prep[i]
                if kt + 1 < NT:
                    j = (kt + 1) % 2
                    t_bp[j] = make_bp(kt + 1, bp_band[j],
                                      rdtok=bfree["bpdma"][j])
                    dma_next = prep_kt_dma(kt + 1)
                exp_toks = []
                for ch in range(4):
                    c0 = 512 * ch
                    key = "a" if ch % 2 == 0 else "b"
                    pst = ps_a if key == "a" else ps_b
                    qt0 = c0 // 128
                    segs = []
                    for qt in range(qt0, qt0 + 4):
                        r = "B" if qlo <= qt < qhi else ("L" if qt < qlo else "H")
                        if segs and segs[-1][0] == r:
                            segs[-1][2] = qt + 1
                        else:
                            segs.append([r, qt, qt + 1])
                    blo = max(c0, 128 * qlo)
                    bhi = min(c0 + 512, 128 * qhi)
                    fold = FOLD_BANDS and P2C_ON and C2P_ON
                    nband = ((1 if fold else int(P2C_ON) + int(C2P_ON))
                             if blo < bhi else 0)
                    nmm = len(segs) + nband
                    k = 0
                    aftc = [t_m, tp2, tc2, bfree["pt"][i], psfree[key],
                            t_cr, tqc1, tqc2] + t_sat
                    for (r, q0_, q1_) in segs:
                        n0 = 128 * q0_
                        n1 = 128 * q1_
                        if r == "B" or not SAT_ON:
                            lhsT = Kx[0:65, 128 * kt:128 * (kt + 1)]
                            rhs = Qx[0:65, n0:n1]
                        elif r == "L":
                            lhsT = Kx[0:69, 128 * kt:128 * (kt + 1)]
                            rhs = QxL[0:69, n0:n1]
                        else:
                            lhsT = Kx[0:67, 128 * kt:128 * (kt + 1)]
                            rhs = QxH[0:67, n0:n1]
                        k += 1
                        p.top("matmul", pst[:, n0 - c0:n1 - c0], lhsT, rhs,
                              start=(k == 1), stop=(k == nmm),
                              after=aftc if k == 1 else ())
                    if nband:
                        off = blo - 128 * qlo
                        nb_ = 1 if fold else int(P2C_ON) + int(C2P_ON)
                        if P2C_ON and not fold:
                            p.top("matmul", pst[:, blo - c0:bhi - c0], identb[:],
                                  p2cT[i][:, off:off + bhi - blo],
                                  start=False, stop=(nb_ == 1))
                        if C2P_ON:
                            p.top("matmul", pst[:, blo - c0:bhi - c0], identb[:],
                                  c2pT[i][:, off:off + bhi - blo],
                                  start=False, stop=True)
                        if nb_ == 0:
                            p.top("matmul", pst[:, c0 - c0:c0 - c0 + 1]
                                  if False else pst[:, 0:512],
                                  Kx[0:65, 128 * kt:128 * (kt + 1)],
                                  Qx[0:65, c0:c0 + 512],
                                  start=False, stop=True)
                    tmm = p.tmark()
                    tex = p.op("scalar", "activation", pt[:, c0:c0 + 512],
                               pst[:, 0:512], AF.Exp, after=[tmm])
                    psfree[key] = tex
                    exp_toks.append(tex)
                if not (FOLD_BANDS and P2C_ON and C2P_ON):
                    bfree["p2cT"][i] = p.tok("p_tensor")
                bfree["c2pT"][i] = p.tok("p_tensor")
                for ch in range(4):
                    aft = ([exp_toks[0], t_vev, t_vone]
                           if ch == 0 else [exp_toks[ch]])
                    if kt == 0:
                        aft = aft + [cfree[ch]]
                    p.top("matmul", ps_ctx[0:65, 512 * ch:512 * (ch + 1)],
                          v_aug[:, (HD + 1) * kt:(HD + 1) * (kt + 1)],
                          pt[:, 512 * ch:512 * (ch + 1)],
                          start=(kt == 0), stop=(kt == NT - 1), after=aft)
                tap(f"pt_{b}_{h}_{kt}", pt[:, :], exp_toks)
                bfree["pt"][i] = p.tmark()
                if kt + 1 < NT:
                    t_prep[j] = prep_kt_pe(kt + 1, *dma_next)

            _mk(f"h{h}_ctx")
            t_pv_done = bfree["pt"][(NT - 1) % 2]
            tden = p.op("scalar", "copy", den[64:65, :], ps_ctx[64:65, :],
                        after=[t_pv_done])
            tap(f"den_{b}_{h}", den[64:65, :], [tden])
            trec = p.op("vector", "reciprocal", den[64:65, :], den[64:65, :],
                        after=[tden])
            t_ctxh = None
            for ch in range(4):
                key = "a" if ch % 2 == 0 else "b"
                pst = ps_a if key == "a" else ps_b
                p.top("matmul", pst[0:64, 0:512], ones64[64:65, 0:64],
                      den[64:65, 512 * ch:512 * (ch + 1)],
                      start=True, stop=True,
                      after=[trec, t_o64, psfree[key]])
                tmm = p.tmark()
                aft_cb = [tmm] + ([t_ctxh] if t_ctxh else []) +                     ([shared["xsq"]] if shared["xsq"] else [])
                tcb = p.op("scalar", "copy", xsq[0:64, 0:512],
                           pst[0:64, 0:512], after=aft_cb)
                psfree[key] = tcb
                t_ctxh = p.op("vector", "tensor_tensor",
                              ctx_h[:, 512 * ch:512 * (ch + 1)],
                              ps_ctx[0:64, 512 * ch:512 * (ch + 1)],
                              xsq[0:64, 0:512], AluOpType.mult,
                              after=[tcb, t_ctxh] if t_ctxh else [tcb])
                shared["xsq"] = t_ctxh
            psfree["ctx"] = t_ctxh
            tap(f"ctxh_{b}_{h}", ctx_h[:, :], [t_ctxh])
            t_head_gate = p.dma("scalar", ctx_s[64 * h:64 * h + 64, :],
                                ctx_h[:, :], after=[t_ctxh])

        _mk(f"b{b}_wo")
        if WO_RUN:
            t_d_gate = p.tok("d_sync")
            t_bo = p.dma("sync", den[0:1, 0:H], rawap(dt["bo"], 0, [[H, 1], [1, H]]),
                         after=[t_head_gate])
            t_wo_loaded = None
            for c in range(8):
                t_wo_loaded = p.dma("gpsimd", wo_view[:, H * c:H * (c + 1)],
                                    dt["Wo"][128 * c:128 * (c + 1), :],
                                    after=[t_d_gate, t_head_gate] if c == 0 else ())
            t_x2 = None
            woc_free = [None, None]
            for t in range(NT):
                tlx = p.dma("scalar", xt_f[t % 2][:],
                            x_in[b * S + 128 * t:b * S + 128 * (t + 1), :],
                            after=[xtfree[t % 2]] if xtfree[t % 2] else [])
                tctx = p.dma("sync", rawap(wo_ctx[t % 2], 0,
                                           [[8 * 128, 128], [128, 8], [1, 128]]),
                             rawap(ctx_s, 128 * t, [[S, 128], [128 * S, 8], [1, 128]]),
                             after=[t_head_gate] + ([woc_free[t % 2]]
                                                    if woc_free[t % 2] else []))
                for nk in range(2):
                    key = "a" if nk == 0 else "b"
                    pst = ps_a if key == "a" else ps_b
                    for c in range(8):
                        aft = ([tctx, t_wo_loaded, psfree[key]]
                               if c == 0 else ())
                        p.top("matmul", pst[:, 0:512],
                              wo_ctx[t % 2][:, 128 * c:128 * (c + 1)],
                              wo_view[:, H * c + 512 * nk:H * c + 512 * (nk + 1)],
                              start=(c == 0), stop=False, after=aft)
                    p.top("matmul", pst[:, 0:512], ones_fr[0:1, :],
                          den[0:1, 512 * nk:512 * (nk + 1)],
                          start=False, stop=True,
                          after=[t_bo] if t == 0 and nk == 0 else ())
                tmm = p.tmark()
                ta = p.op("vector", "tensor_tensor", xt_f[t % 2][:, 0:512],
                          xt_f[t % 2][:, 0:512], ps_a[:, 0:512], AluOpType.add,
                          after=[tmm, tlx])
                tb = p.op("vector", "tensor_tensor", xt_f[t % 2][:, 512:1024],
                          xt_f[t % 2][:, 512:1024], ps_b[:, 0:512],
                          AluOpType.add)
                psfree["a"] = tb
                psfree["b"] = tb
                woc_free[t % 2] = p.tmark()
                xtfree[t % 2] = tb
                t_x2 = p.dma("sync", x2_s[b * S + 128 * t:b * S + 128 * (t + 1), :],
                             xt_f[t % 2][:], after=[tb])
                xtfree[t % 2] = t_x2

        else:
            t_x2 = t_head_gate
        t_batch_gate = t_x2

    gates = p.engine_gates()

    es_attn.close()

    if FFN_RUN:
        _mk("ffnA")
        es_fa = contextlib.ExitStack()
        wgT = sb_in(es_fa, "wgT", [128, 8 * INNER], BF16)
        wuT = sb_in(es_fa, "wuT", [128, 8 * INNER], BF16)
        gu = sb_in(es_fa, "gu", [128, INNER], F32)
        xn2 = sb_in(es_fa, "xn2", [128, H], F32)
        yq = sb_in(es_fa, "yq", [128, 1024], F32)
        qint = sb_in(es_fa, "qint", [128, 1024], I32)
        qflt = sb_in(es_fa, "qflt", [128, 1024], F32)
        gq1 = sb_in(es_fa, "gq1", [128, 1024], F32)
        gq2 = qint.bitcast(F32)
        xq_tok = sb_in(es_fa, "xq_tok", [128, H], BF16)
        xqT = sb_in(es_fa, "xqT", [128, H], BF16)
        guq_tok = sb_in(es_fa, "guq_tok", [128, INNER], BF16)
        stage2 = sb_in(es_fa, "stage2", [128, INNER], BF16)
        sgate = xsq
        dscv = sb_in(es_fa, "dscv", [128, 8], F32)
        ps_g = [es_fa.enter_context(nc.psum_tensor(f"ps_g{i}", [128, 512], F32))
                for i in range(2)]
        ps_u = [es_fa.enter_context(nc.psum_tensor(f"ps_u{i}", [128, 512], F32))
                for i in range(2)]

        tsw = None
        for col in (2, 3, 4):
            aft = (gates + [tw_g, tw_u, tw_d, psfree["a"]] if col == 2
                   else [psfree["a"]])
            p.top("matmul", ps_a[:, 4:5], ones_fr[:],
                  scal[0:1, col:col + 1], start=True, stop=True, after=aft)
            tmm = p.tmark()
            tsw = p.op("vector", "tensor_copy", thrv[:, col:col + 1],
                       ps_a[:, 4:5], after=[tmm])
            psfree["a"] = tsw

        t_wres = None
        for c in range(8):
            p.dma("sync", wgT[:, INNER * c:INNER * (c + 1)],
                  wgT_s[128 * c:128 * (c + 1), :],
                  after=gates + [tw_g] if c == 0 else [])
            t_wres = p.dma("sync", wuT[:, INNER * c:INNER * (c + 1)],
                           wuT_s[128 * c:128 * (c + 1), :])

        qshared = [None]
        qscale_war = [None]
        qfin = [None]

        def quantize(src, W, out_bf, t_src, dcol, chunk_toks=None,
                     scale_war=None, out_war=None):
            if chunk_toks is None:
                chunk_toks = []
            t0 = p.op("vector", "reduce_max", dscv[:, 0:1], src[:, 0:W], AX,
                      apply_absolute_value=True,
                      after=[t_src] + ([qscale_war[0]] if qscale_war[0]
                                       else []))
            t1a = p.op("vector", "tensor_single_scalar", dscv[:, 0:1],
                       dscv[:, 0:1], 1e-5, AluOpType.max, after=[t0])
            t1 = p.op("vector", "tensor_scalar_mul", dscv[:, dcol:dcol + 1],
                      dscv[:, 0:1], 1.0 / 127.0,
                      after=[scale_war] if scale_war else ())
            t2r = p.op("vector", "reciprocal", dscv[:, 1:2], dscv[:, 0:1])
            t2 = p.op("vector", "tensor_scalar_mul", dscv[:, 1:2],
                      dscv[:, 1:2], 127.0)
            qscale_war[0] = t2
            tq = None
            tzp = qshared[0]
            for q0 in range(0, W, 1024):
                ty = p.op("vector", "tensor_scalar_mul", yq[:, 0:1024],
                          src[:, q0:q0 + 1024], dscv[:, 1:2],
                          after=[t2] + ([tzp] if tzp else [])
                          + ([tq] if tq else ([qfin[0]] if qfin[0] else [])))
                tz = p.op("scalar", "copy", qint[:, 0:1024], yq[:, 0:1024],
                          after=[ty])
                tzp = tz
                tzf = p.op("scalar", "copy", qflt[:, 0:1024], qint[:, 0:1024])
                tg1 = p.op("vector", "scalar_tensor_tensor", gq1[:, 0:1024],
                           qflt[:, 0:1024], 0.5, yq[:, 0:1024], AluOpType.add,
                           AluOpType.is_le, after=[tzf])
                tg2 = p.op("vector", "scalar_tensor_tensor", gq2[:, 0:1024],
                           qflt[:, 0:1024], -0.5, yq[:, 0:1024], AluOpType.add,
                           AluOpType.is_ge)
                td = p.op("gpsimd", "tensor_tensor", gq1[:, 0:1024],
                          gq1[:, 0:1024], gq2[:, 0:1024], AluOpType.subtract,
                          after=[tg1, tg2])
                tq = p.op("gpsimd", "tensor_tensor", out_bf[:, q0:q0 + 1024],
                          qflt[:, 0:1024], gq1[:, 0:1024], AluOpType.add,
                          after=[td] + ([out_war] if out_war and q0 == 0
                                        else []))
                chunk_toks.append(tq)
            qshared[0] = tzp
            qfin[0] = tq
            return tq

        stage_free = None
        pipe = {"tsg": None, "tumm": None, "tdu": None, "xq_tt": None,
                "tds": None, "guq_tt": None, "gq_toks": None}

        def ffn_front(t):
            aft = [xtfree[t % 2]] if xtfree[t % 2] else []
            if t == 0:
                aft = aft + gates
            tl = p.dma("sync", xt_f[t % 2][:], x2_s[128 * t:128 * (t + 1), :],
                       after=aft)
            tln = layernorm(xt_f[t % 2][:], xn2[:], H, after=[tl],
                            apply_after=[pipe["tqx"]] if pipe.get("tqx")
                            else ())
            xtfree[t % 2] = tln
            tqx = quantize(xn2, H, xq_tok, tln, 5,
                           scale_war=pipe["tdu"], out_war=pipe["xq_tt"])
            pipe["tqx"] = tqx
            for c in range(8):
                aft2 = [tqx, psfree["t"]] if c == 0 else ()
                p.top("matmul", ps_t[:, 128 * c:128 * (c + 1)],
                      xq_tok[:, 128 * c:128 * (c + 1)], identb[:],
                      start=(c == 0), stop=(c == 7), is_transpose=True,
                      after=aft2)
            tt2 = p.tmark()
            pipe["xq_tt"] = tt2
            txq = p.op("scalar", "copy", xqT[:], ps_t[:, 0:1024],
                       after=[tt2, pipe["tumm"]] if pipe["tumm"] else [tt2])
            psfree["t"] = txq
            return tqx, txq

        def guq_tail(t, guq_toks):
            for g0 in range(0, 32, 8):
                key = "t" if (g0 // 8) % 2 == 0 else "t2"
                pst = ps_t if key == "t" else ps_t2
                for c in range(g0, g0 + 8):
                    aft = [guq_toks[g0 // 8], psfree[key]] if c == g0 else ()
                    p.top("matmul", pst[:, 128 * (c - g0):128 * (c - g0) + 128],
                          guq_tok[:, 128 * c:128 * (c + 1)], identb[:],
                          start=(c == g0), stop=(c == g0 + 7),
                          is_transpose=True, after=aft)
                tt = p.tmark()
                if g0 == 24:
                    pipe["guq_tt"] = tt
                tev = p.op("scalar", "copy", stage2[:, 128 * g0:128 * (g0 + 8)],
                           pst[:, 0:1024],
                           after=[tt, stg[0]] if stg[0] else [tt])
                psfree[key] = tev
            srcv = rawap(stage2, 0, [[INNER, 128], [128, 32], [1, 128]])
            dstv = rawap(guq_s, t * (128 * INNER),
                         [[128, 128], [128 * 128, 32], [1, 128]])
            stg[0] = p.dma("sync", dstv, srcv, after=[tev])

        stg = [None]
        guq_pend = None
        nxt_front = ffn_front(0)
        for t in range(NB * NT):
            tqx, txq = nxt_front
            tdg = p.op("vector", "tensor_tensor", dscv[:, 6:7], dscv[:, 5:6],
                       thrv[:, 2:3], AluOpType.mult,
                       after=[tqx, tsw] +
                       ([pipe["tsg"]] if pipe["tsg"] else []))
            tdu = p.op("vector", "tensor_tensor", dscv[:, 7:8], dscv[:, 5:6],
                       thrv[:, 3:4], AluOpType.mult)
            pipe["tdu"] = tdu
            t_gu = None
            t_gufree = [[], []]
            for ic in range(8):
                pg, pu = ps_g[ic % 2], ps_u[ic % 2]
                for c in range(8):
                    aft = (([txq, t_wres] if ic < 2 else [])
                           + t_gufree[ic % 2] if c == 0 else ())
                    p.top("matmul", pg[:, 0:512],
                          xqT[:, 128 * c:128 * (c + 1)],
                          wgT[:, INNER * c + 512 * ic:INNER * c + 512 * (ic + 1)],
                          start=(c == 0), stop=(c == 7), after=aft)
                tg_mm = p.tmark()
                for c in range(8):
                    p.top("matmul", pu[:, 0:512],
                          xqT[:, 128 * c:128 * (c + 1)],
                          wuT[:, INNER * c + 512 * ic:INNER * c + 512 * (ic + 1)],
                          start=(c == 0), stop=(c == 7))
                tu_mm = p.tmark()
                tsg = p.op("scalar", "activation", sgate[:, 0:512],
                           pg[:, 0:512],
                           AF.Silu, scale=dscv[:, 6:7],
                           after=[tg_mm, tdu] +
                           ([t_gu] if t_gu else
                            ([shared["xsq"]] if shared["xsq"] else [])))
                tsu = p.op("vector", "tensor_scalar_mul",
                           gu[:, 512 * ic:512 * (ic + 1)], pu[:, 0:512],
                           dscv[:, 7:8],
                           after=[tu_mm] + ([t_gu] if t_gu else [])
                           + ([pipe["gq_toks"][ic // 2]]
                              if pipe["gq_toks"] else []))
                t_gufree[ic % 2] = [tsg, tsu]
                t_gu = p.op("vector", "tensor_tensor",
                            gu[:, 512 * ic:512 * (ic + 1)],
                            gu[:, 512 * ic:512 * (ic + 1)], sgate[:, 0:512],
                            AluOpType.mult, after=[tsg, tsu])
                if ic == 7:
                    pipe["tsg"] = tsg
                    pipe["tumm"] = tu_mm
            shared["xsq"] = t_gu
            if t + 1 < NB * NT:
                nxt_front = ffn_front(t + 1)
            if guq_pend is not None:
                guq_tail(*guq_pend)
            guq_toks = []
            tqg = quantize(gu, INNER, guq_tok, t_gu, 3, chunk_toks=guq_toks,
                           scale_war=pipe["tds"], out_war=pipe["guq_tt"])
            tds = p.op("vector", "tensor_copy", dsc_all[:, t:t + 1],
                       dscv[:, 3:4], after=[tqg])
            pipe["tds"] = tds
            pipe["gq_toks"] = guq_toks
            guq_pend = (t, guq_toks)
        guq_tail(*guq_pend)
        t_a_gates = p.engine_gates()
        es_fa.close()

        _mk("ffnB")
        es_fb = contextlib.ExitStack()
        wdres = sb_in(es_fb, "wdres", [128, 32 * H], BF16)
        guqT_in = [sb_in(es_fb, f"guqT_in{i}", [128, INNER], BF16)
                   for i in range(2)]
        ps_d = es_fb.enter_context(nc.psum_tensor("ps_d", [128, 1024], F32))

        t_wd2 = None
        for c in range(32):
            t_wd2 = p.dma("sync", wdres[:, H * c:H * (c + 1)],
                          wdT_s[128 * c:128 * (c + 1), :],
                          after=t_a_gates + [tw_d] if c == 0 else [])
        ginfree = [None, None]
        t_psd_free = None
        for t in range(NB * NT):
            tg = p.dma("sync", guqT_in[t % 2][:],
                       rawap(guq_s, t * (128 * INNER),
                             [[128, 128], [16384, 32], [1, 128]]),
                       after=[ginfree[t % 2]] if ginfree[t % 2] else [])
            aft = [xtfree[t % 2]] if xtfree[t % 2] else []
            if t == 0:
                aft = aft + t_a_gates
            tlx = p.dma("scalar", xt_f[t % 2][:], x2_s[128 * t:128 * (t + 1), :],
                        after=aft)
            for nk in range(2):
                for c in range(32):
                    aft = [a for a in (tg, t_wd2, t_psd_free) if a]
                    p.top("matmul", ps_d[:, 512 * nk:512 * (nk + 1)],
                          guqT_in[t % 2][:, 128 * c:128 * (c + 1)],
                          wdres[:, H * c + 512 * nk:H * c + 512 * (nk + 1)],
                          start=(c == 0), stop=(c == 31),
                          after=aft if nk == 0 and c == 0 else ())
            tmm = p.tmark()
            ginfree[t % 2] = tmm
            tsc = p.op("vector", "tensor_tensor", dscv[0:128, 0:1]
                       if False else scal2[:, 3:4], dsc_all[:, t:t + 1],
                       thrv[:, 4:5], AluOpType.mult, after=[tmm, tsw])
            ta = p.op("vector", "scalar_tensor_tensor", xt_f[t % 2][:, 0:512],
                      ps_d[:, 0:512], scal2[:, 3:4], xt_f[t % 2][:, 0:512],
                      AluOpType.mult, AluOpType.add, after=[tsc, tlx])
            tb = p.op("vector", "scalar_tensor_tensor", xt_f[t % 2][:, 512:1024],
                      ps_d[:, 512:1024], scal2[:, 3:4], xt_f[t % 2][:, 512:1024],
                      AluOpType.mult, AluOpType.add)
            t_psd_free = tb
            tst = p.dma("sync", out_hbm[128 * t:128 * (t + 1), :],
                        xt_f[t % 2][:], after=[tb])
            xtfree[t % 2] = tst


    final_d = p.tok("d_sync")
    final_ds = p.tok("d_scalar")
    final_dg = p.tok("d_gpsimd")
    if TRUNC is not None:
        p.steps = p.steps[:TRUNC]

    with nc.Block() as block:
        @block.sync
        def _(sync):
            for e, fn in p.steps:
                if e == "sync":
                    fn(sync)
            if TRUNC is None:
                sync.wait_ge(p.sem["d_sync"], final_d[1])
                sync.wait_ge(p.sem["d_scalar"], final_ds[1])
                sync.wait_ge(p.sem["d_gpsimd"], final_dg[1])

        @block.vector
        def _(vector):
            for e, fn in p.steps:
                if e == "vector":
                    fn(vector)

        @block.scalar
        def _(scalar):
            for e, fn in p.steps:
                if e == "scalar":
                    fn(scalar)

        @block.tensor
        def _(tensor):
            for e, fn in p.steps:
                if e == "tensor":
                    fn(tensor)

        @block.gpsimd
        def _(gpsimd):
            for e, fn in p.steps:
                if e == "gpsimd":
                    fn(gpsimd)

    if FFN_RUN:
        es_fb.close()
    p.es.close()
    return nc


_last_results = None


def kernel(**inputs):
    global _last_results
    import ml_dtypes
    g_rev, g_fwd = _g_tables()
    ident = np.eye(128, dtype=np.float32)

    x = np.ascontiguousarray(np.asarray(inputs["L0_attn_hidden_states"],
                                        np.float32))
    base = {
        "Wq": np.ascontiguousarray(inputs["L0_attn_Wq"], dtype=np.float32),
        "bq": np.ascontiguousarray(inputs["L0_attn_bq"], dtype=np.float32),
        "Wk": np.ascontiguousarray(inputs["L0_attn_Wk"], dtype=np.float32),
        "bk": np.ascontiguousarray(inputs["L0_attn_bk"], dtype=np.float32),
        "Wv": np.ascontiguousarray(inputs["L0_attn_Wv"], dtype=np.float32),
        "bv": np.ascontiguousarray(inputs["L0_attn_bv"], dtype=np.float32),
        "Wo": np.ascontiguousarray(inputs["L0_attn_Wo"], dtype=np.float32),
        "bo": np.ascontiguousarray(inputs["L0_attn_bo"], dtype=np.float32),
        "Wpk": np.ascontiguousarray(inputs["L0_attn_Wpk"], dtype=np.float32),
        "bpk": np.ascontiguousarray(inputs["L0_attn_bpk"], dtype=np.float32),
        "Wpq": np.ascontiguousarray(inputs["L0_attn_Wpq"], dtype=np.float32),
        "bpq": np.ascontiguousarray(inputs["L0_attn_bpq"], dtype=np.float32),
        "rel": np.ascontiguousarray(inputs["L0_attn_rel_emb"], dtype=np.float32),
        "ln_g": np.ascontiguousarray(inputs["L0_attn_ln_g"], dtype=np.float32),
        "ln_b": np.ascontiguousarray(inputs["L0_attn_ln_b"], dtype=np.float32),
        "wg": np.ascontiguousarray(inputs["L0_ffn_w_gate"], dtype=np.float32),
        "wu": np.ascontiguousarray(inputs["L0_ffn_w_up"], dtype=np.float32),
        "wd": np.ascontiguousarray(inputs["L0_ffn_w_down"], dtype=np.float32),
        "Grev": g_rev, "Gfwd": g_fwd,
        "identb": ident.astype(ml_dtypes.bfloat16), "identf": ident,
        "bqk_t": np.ascontiguousarray(np.concatenate(
            [np.asarray(inputs["L0_attn_" + nm], np.float32).reshape(16, 64).T
             for nm in ("bq", "bk", "bv", "bpk", "bpq")], axis=1)),
    }
    in_maps = []
    for i in range(NCORE):
        m = dict(base)
        m["x"] = np.ascontiguousarray(x[NB * i:NB * (i + 1)].reshape(NB * S, H))
        in_maps.append(m)

    nc = build_nc()
    res = run_bass_kernel_spmd(nc, in_maps, core_ids=list(range(NCORE)))
    _last_results = res
    return np.concatenate(
        [res.results[i]["out"].reshape(NB, S, H) for i in range(NCORE)],
        axis=0)


# version 11
# speedup vs baseline: 1.1835x; 1.0001x over previous
import math
import contextlib
import numpy as np

import concourse.bass as bass
import concourse.mybir as mybir
from concourse.bass_utils import run_bass_kernel_spmd
from concourse.alu_op_type import AluOpType

F32 = mybir.dt.float32
BF16 = mybir.dt.bfloat16
I32 = mybir.dt.int32
AF = mybir.ActivationFunctionType
AX = mybir.AxisListType.X

H = 1024
NH = 16
HD = 64
S = 2048
NT = 16
NB = 2
INNER = 4096
SPAN = 256
SCALE = 1.0 / math.sqrt(HD * 3.0)
WBAND = 1280
DMAX = 4
NCORE = 8
WROW = NT * WBAND

TAPS = {}
NH_RUN = NH
NB_RUN = NB
FFN_RUN = True
WO_RUN = True
C2P_ON = True
P2C_ON = True
SAT_ON = True
PREP_RUN = True
REL_RUN = True
LN1_RUN = True
TRUNC = None
MARKS = []
FOLD_BANDS = True


def _cidx(r):
    r = np.asarray(r, np.int64)
    sign = np.sign(r).astype(np.float64)
    mid = 128
    abs_pos = np.where((r < mid) & (r > -mid), float(mid - 1),
                       np.abs(r)).astype(np.float64)
    log_pos = np.ceil(
        np.log(abs_pos / mid)
        / np.float64(np.float32(np.log((512 - 1) / mid))) * (mid - 1)) + mid
    bucket = np.where(abs_pos <= mid, r.astype(np.float64), log_pos * sign)
    return np.clip(bucket.astype(np.int64) + SPAN, 0, 2 * SPAN - 1)


def _g_tables():
    import ml_dtypes
    u = np.arange(WBAND)
    g_rev = np.zeros((2 * SPAN, WBAND), np.float32)
    g_fwd = np.zeros((2 * SPAN, WBAND), np.float32)
    g_rev[_cidx(639 - u), u] = 1.0
    g_fwd[_cidx(u - 639), u] = 1.0
    return (g_rev.astype(ml_dtypes.bfloat16), g_fwd.astype(ml_dtypes.bfloat16))


_UREV = _cidx(639 - np.arange(WBAND))
_UFWD = _cidx(np.arange(WBAND) - 639)
U_SATH_REV = int(np.argmax(_UREV == 511))
U_SATL_REV = int(np.argmax(_UREV == 0))
U_SATH_FWD = int(np.argmax(_UFWD == 511))
U_SATL_FWD = int(np.argmax(_UFWD == 0))


class Plan:
    def __init__(self, nc):
        self.nc = nc
        self.steps = []
        self.sem = {}
        self.cnt = {}
        self.es = contextlib.ExitStack()
        for s_ in ("d_sync", "d_scalar", "d_gpsimd", "p_vector", "p_scalar",
                   "p_tensor", "p_gpsimd"):
            self.sem[s_] = self.es.enter_context(nc.semaphore(s_))
            self.cnt[s_] = 0

    def emit(self, eng, fn):
        self.steps.append((eng, fn))

    def wait(self, eng, tok):
        if tok is None:
            return
        name, val = tok
        h = self.sem[name]
        self.emit(eng, lambda e, h=h, v=val: e.wait_ge(h, v))

    def tok(self, name):
        return (name, self.cnt[name])

    def dma(self, q, out, in_, after=(), slow=False):
        for t in after:
            self.wait(q, t)
        name = "d_" + q
        self.cnt[name] += 16
        h = self.sem[name]
        if slow:
            self.emit(q, lambda e, o=out, i=in_, h=h:
                      e.dma_start(out=o, in_=i,
                                  allow_slow_non_contiguous=True).then_inc(h, 16))
        else:
            self.emit(q, lambda e, o=out, i=in_, h=h:
                      e.dma_start(out=o, in_=i).then_inc(h, 16))
        return (name, self.cnt[name])

    def op(self, eng, method, *args, after=(), **kw):
        for t in after:
            self.wait(eng, t)
        name = "p_" + eng
        h = self.sem[name]
        pre = self.cnt[name]
        self.cnt[name] += 1
        v = self.cnt[name]

        def fn(e, m=method, a=args, k=kw, h=h, pre=pre):
            e.wait_ge(h, pre)
            getattr(e, m)(*a, **k).then_inc(h, 1)
        self.emit(eng, fn)
        return (name, v)

    def top(self, method, *args, after=(), **kw):
        for t in after:
            self.wait("tensor", t)
        cell = {"inc": 0}
        self._last_top = cell
        h = self.sem["p_tensor"]

        def fn(e, m=method, a=args, k=kw, cell=cell, h=h):
            inst = getattr(e, m)(*a, **k)
            if cell["inc"]:
                inst.then_inc(h, cell["inc"])
        self.emit("tensor", fn)

    def tmark(self):
        name = "p_tensor"
        cell = getattr(self, "_last_top", None)
        if cell is None:
            return (name, self.cnt[name])
        self.cnt[name] += 1
        cell["inc"] += 1
        return (name, self.cnt[name])

    def engine_gates(self):
        return [self.tok("p_vector"), self.tok("p_scalar"),
                self.tok("p_gpsimd"), self.tmark(),
                self.tok("d_sync"), self.tok("d_scalar"),
                self.tok("d_gpsimd")]


def rawap(t, offset, dims):
    ap = t.ap().copy() if callable(getattr(t, "ap", None)) else t.copy()
    ap.offset = offset
    while len(ap.ap) > 1:
        ap.ap.pop()
    ap.ap[0] = list(dims[0])
    for d in dims[1:]:
        ap.ap.append(list(d))
    return ap


def build_nc():
    nc = bass.Bass("TRN2", target_bir_lowering=False,
                   detect_race_conditions=False,
                   dynamic_dma_scratch_size=16384)
    p = Plan(nc)

    def _mk(lbl):
        MARKS.append((lbl, len(p.steps)))
    dt = {}

    def din(name, shape, dtype=F32):
        dt[name] = nc.dram_tensor(name, shape, dtype, kind="ExternalInput").ap()
        return dt[name]

    x_in = din("x", [NB * S, H])
    for nm in ("Wq", "Wk", "Wv", "Wo", "Wpk", "Wpq"):
        din(nm, [H, H])
    for nm in ("bq", "bk", "bv", "bo", "bpk", "bpq", "ln_g", "ln_b"):
        din(nm, [H])
    din("rel", [2 * SPAN, H])
    din("wg", [INNER, H])
    din("wu", [INNER, H])
    din("wd", [H, INNER])
    din("Grev", [2 * SPAN, WBAND], BF16)
    din("Gfwd", [2 * SPAN, WBAND], BF16)
    din("identb", [128, 128], BF16)
    din("identf", [128, 128], F32)
    din("bqk_t", [64, 80])

    out_hbm = nc.dram_tensor("out", [NB * S, H], F32, kind="ExternalOutput").ap()
    x2_s = nc.dram_tensor("x2_s", [NB * S, H], F32, kind="Internal").ap()
    tabs_s = nc.dram_tensor("tabs_s", [NH * 64, 2 * WBAND], BF16,
                            kind="Internal").ap()
    ctx_s = nc.dram_tensor("ctx_s", [H, S], BF16, kind="Internal").ap()
    wgT_s = nc.dram_tensor("wgT_s", [H, INNER], BF16, kind="Internal").ap()
    wuT_s = nc.dram_tensor("wuT_s", [H, INNER], BF16, kind="Internal").ap()
    wdT_s = nc.dram_tensor("wdT_s", [INNER, H], BF16, kind="Internal").ap()
    guq_s = nc.dram_tensor("guq_s", [NB * NT, 128 * INNER], BF16,
                           kind="Internal").ap()

    tap_out = {}

    def tap(tag, ap, after):
        if tag not in TAPS:
            return
        shape, dtyp = TAPS[tag]
        nm = "dbg_" + tag
        if nm not in tap_out:
            tap_out[nm] = nc.dram_tensor(nm, shape, dtyp,
                                         kind="ExternalOutput").ap()
        p.dma("sync", tap_out[nm], ap, after=after)

    def sb_in(stk, name, shape, dtyp):
        return stk.enter_context(nc.sbuf_tensor(name, shape, dtyp))

    es0 = p.es

    def sb(name, shape, dtyp):
        return sb_in(es0, name, shape, dtyp)

    identb = sb("identb_t", [128, 128], BF16)
    identf = sb("identf_t", [128, 128], F32)
    ones_bf = sb("ones_bf", [128, 128], BF16)
    ones_fr = sb("ones_fr", [1, 128], F32)
    ones64 = sb("ones64", [128, 64], F32)
    ones_b64 = sb("ones_b64", [64, 1], BF16)
    onesc_f = sb("onesc_f", [128, 1], F32)
    bvec = sb("bvec", [128, 5 * 16], F32)
    scal = sb("scal", [1, 16], F32)
    scal2 = sb("scal2", [128, 8], F32)
    st1 = sb("st1", [128, 8], F32)
    xsq = sb("xsq", [128, H], F32)
    xt_f = [sb(f"xt_f{i}", [128, H], F32) for i in range(2)]
    thrv = sb("thrv", [128, 8], F32)
    rowbuf = sb("rowbuf", [2, S], BF16)
    normbuf = sb("normbuf", [1, 2 * NH], F32)
    dsc_all = sb("dsc_all", [128, NB * NT], F32)

    ps_a = es0.enter_context(nc.psum_tensor("ps_a", [128, 512], F32))
    ps_b = es0.enter_context(nc.psum_tensor("ps_b", [128, 512], F32))
    ps_t = es0.enter_context(nc.psum_tensor("ps_t", [128, 1024], BF16))
    ps_t2 = es0.enter_context(nc.psum_tensor("ps_t2", [128, 1024], BF16))

    psfree = {"a": None, "b": None, "t": None, "t2": None, "ctx": None}
    xtfree = [None, None]

    t_id = p.dma("sync", identb[:], dt["identb"])
    t_if = p.dma("sync", identf[:], dt["identf"])
    p.dma("sync", bvec[0:64, 0:80], dt["bqk_t"])
    t_ldc = p.tok("d_sync")

    t_ones = p.op("vector", "memset", ones_bf[:], 1.0)
    p.op("vector", "memset", ones_fr[:], 1.0)
    t_o64 = p.op("vector", "memset", ones64[:], 1.0)
    p.op("vector", "memset", ones_b64[:], 1.0)
    t_onesf = p.op("vector", "memset", onesc_f[:], 1.0)
    t_bq = p.op("vector", "tensor_scalar_mul", bvec[0:64, 0:16],
                bvec[0:64, 0:16], SCALE, after=[t_ldc])
    t_c127 = p.op("vector", "memset", thrv[:, 5:6], 127.0)

    shared = {"xsq": None}

    def layernorm(xf, tgt, W, gamma=None, beta=None, after=(), apply_after=()):
        aft0 = list(after) + ([shared["xsq"]] if shared["xsq"] else [])
        t0 = p.op("scalar", "activation", xsq[:, 0:W], xf, AF.Square,
                  after=aft0)
        t1 = p.op("vector", "reduce_sum", st1[:, 0:1], xf, AX, after=after)
        t2 = p.op("vector", "reduce_sum", st1[:, 1:2], xsq[:, 0:W], AX,
                  after=[t0])
        shared["xsq"] = t2
        p.op("vector", "tensor_scalar_mul", st1[:, 2:3], st1[:, 0:1], 1.0 / W,
             after=[t1])
        p.op("vector", "tensor_scalar_mul", st1[:, 3:4], st1[:, 1:2], 1.0 / W,
             after=[t2])
        p.op("vector", "tensor_tensor", st1[:, 4:5], st1[:, 2:3], st1[:, 2:3],
             AluOpType.mult)
        tvar = p.op("vector", "scalar_tensor_tensor", st1[:, 5:6], st1[:, 3:4],
                    1e-5, st1[:, 4:5], AluOpType.add, AluOpType.subtract)
        t3a = p.op("scalar", "activation", st1[:, 4:5], st1[:, 5:6], AF.Sqrt,
                   after=[tvar])
        t3 = p.op("vector", "reciprocal", st1[:, 6:7], st1[:, 4:5],
                  after=[t3a])
        if gamma is None:
            return p.op("vector", "tensor_scalar", tgt, xf, st1[:, 2:3],
                        st1[:, 6:7], AluOpType.subtract, AluOpType.mult,
                        after=[t3] + list(apply_after))
        ta = p.op("vector", "tensor_scalar", xsq[:, 0:W], xf, st1[:, 2:3],
                  st1[:, 6:7], AluOpType.subtract, AluOpType.mult, after=[t3])
        tb = p.op("vector", "tensor_tensor", xsq[:, 0:W], xsq[:, 0:W], gamma,
                  AluOpType.mult, after=[ta])
        shared["xsq"] = tb
        return p.op("vector", "tensor_tensor", tgt, xsq[:, 0:W], beta,
                    AluOpType.add, after=[tb] + list(apply_after))

    es_prep = contextlib.ExitStack()
    wtile = [sb_in(es_prep, f"wtile{i}", [128, INNER], F32) for i in range(2)]
    g1big = sb_in(es_prep, "g1big", [128, INNER], F32)
    ternt = sb_in(es_prep, "ternt", [128, INNER], BF16)
    stage = sb_in(es_prep, "stage", [128, INNER], BF16)
    wtfree = [None, None]

    def prep_weight(wname, dst_hbm, nrow, ncol, scol):
        ntile = nrow // 128
        tacc = p.op("vector", "memset", scal2[:, 0:1], 0.0)
        for t in range(ntile):
            td = p.dma(("sync", "scalar", "gpsimd")[t % 3],
                       wtile[t % 2][:, 0:ncol],
                       dt[wname][128 * t:128 * (t + 1), :],
                       after=[wtfree[t % 2]] if wtfree[t % 2] else [])
            tr = p.op("vector", "reduce_sum", scal2[:, 1:2],
                      wtile[t % 2][:, 0:ncol], AX, apply_absolute_value=True,
                      after=[td])
            tacc = p.op("vector", "tensor_tensor", scal2[:, 0:1],
                        scal2[:, 0:1], scal2[:, 1:2], AluOpType.add,
                        after=[tr])
            wtfree[t % 2] = tacc
        p.top("matmul", ps_a[0:1, 0:1], onesc_f[:], scal2[:, 0:1],
              start=True, stop=True, after=[tacc, t_onesf, psfree["a"]])
        tm = p.tmark()
        t5 = p.op("vector", "tensor_scalar", scal[0:1, scol:scol + 1],
                  ps_a[0:1, 0:1], 1.0 / (nrow * ncol), 1e-5,
                  AluOpType.mult, AluOpType.max, after=[tm])
        t6 = p.op("vector", "tensor_scalar_mul", scal[0:1, 15:16],
                  scal[0:1, scol:scol + 1], 0.5)
        p.top("matmul", ps_a[:, 4:5], ones_fr[:],
              scal[0:1, 15:16], start=True, stop=True, after=[t6, t5])
        tb128 = p.tmark()
        tpos = p.op("vector", "tensor_copy", thrv[:, 0:1], ps_a[:, 4:5],
                    after=[tb128])
        psfree["a"] = tpos
        tneg = p.op("vector", "tensor_scalar_mul", thrv[:, 1:2], thrv[:, 0:1],
                    -1.0)
        nb = ncol // 128
        stage_free = None
        tern_free = [None]
        for t in range(ntile):
            td = p.dma(("sync", "scalar", "gpsimd")[t % 3],
                       wtile[t % 2][:, 0:ncol],
                       dt[wname][128 * t:128 * (t + 1), :],
                       after=[wtfree[t % 2]])
            tg2 = p.op("vector", "tensor_scalar", g1big[:, 0:ncol],
                       wtile[t % 2][:, 0:ncol], thrv[:, 1:2], None,
                       AluOpType.is_le, after=[td, tneg])
            tq = p.op("vector", "scalar_tensor_tensor", ternt[:, 0:ncol],
                      wtile[t % 2][:, 0:ncol], thrv[:, 0:1], g1big[:, 0:ncol],
                      AluOpType.is_ge, AluOpType.subtract,
                      after=[tg2, tpos] + ([tern_free[0]] if tern_free[0]
                                           else []))
            wtfree[t % 2] = tq
            tev = None
            for g0 in range(0, nb, 8):
                g1 = min(nb, g0 + 8)
                key = "t" if (g0 // 8) % 2 == 0 else "t2"
                pst = ps_t if key == "t" else ps_t2
                for c in range(g0, g1):
                    aft = [tq, t_id, psfree[key]] if c == g0 else ()
                    p.top("matmul",
                          pst[:, 128 * (c - g0):128 * (c - g0) + 128],
                          ternt[:, 128 * c:128 * (c + 1)], identb[:],
                          start=(c == g0), stop=(c == g1 - 1),
                          is_transpose=True, after=aft)
                tt = p.tmark()
                tev = p.op("scalar", "copy", stage[:, 128 * g0:128 * g1],
                           pst[:, 0:128 * (g1 - g0)],
                           after=[tt, stage_free] if stage_free else [tt])
                psfree[key] = tev
                tern_free[0] = tt
            dstv = rawap(dst_hbm, 128 * t,
                         [[nrow, 128], [128 * nrow, nb], [1, 128]])
            srcv = rawap(stage, 0, [[INNER, 128], [128, nb], [1, 128]])
            stage_free = p.dma("sync", dstv, srcv, after=[tev])
        return stage_free

    if PREP_RUN:
        tw_g = prep_weight("wg", wgT_s, INNER, H, 2)
        tw_u = prep_weight("wu", wuT_s, INNER, H, 3)
        tw_d = prep_weight("wd", wdT_s, H, INNER, 4)
    else:
        tw_g = tw_u = tw_d = p.tok("d_sync")
    t_prep_gates = p.engine_gates()
    es_prep.close()

    es_attn = contextlib.ExitStack()

    def sba(name, shape, dtyp):
        return sb_in(es_attn, name, shape, dtyp)

    grev4 = sba("grev4", [128, 4 * WBAND], BF16)
    gfwd4 = sba("gfwd4", [128, 4 * WBAND], BF16)
    re_fm = sba("re_fm", [128, 8 * 512], BF16)
    xn_fm = sba("xn_fm", [128, 8 * S], BF16)
    ctx_h = sba("ctx_h", [64, S], BF16)
    wo_ctx = [sba(f"wo_ctx{i}", [128, 8 * 128], BF16) for i in range(2)]
    Kx = sba("Kx", [128, S], BF16)
    Qx = sba("Qx", [128, S], BF16)
    QxH = sba("QxH", [128, S], BF16)
    QxL = sba("QxL", [128, S], BF16)
    v_aug = sba("v_aug", [128, NT * (HD + 1)], BF16)
    bc_band = sba("bc_band", [128, WROW], BF16)
    bp_band = [sba(f"bp_band{i}", [128, WBAND], BF16) for i in range(2)]
    pt_buf = [sba(f"pt{i}", [128, S], BF16) for i in range(2)]
    c2pT = [sba(f"c2pT{i}", [128, 9 * 128], BF16) for i in range(2)]
    p2cT = [sba(f"p2cT{i}", [128, 9 * 128], BF16) for i in range(2)]
    ctile = [sba(f"ctile{i}", [128, 9 * 128], BF16) for i in range(2)]
    tabR = sba("tabR", [64, WBAND], BF16)
    tabF = sba("tabF", [64, WBAND], BF16)
    pkpm = sba("pkpm", [128, 4 * 64], BF16)
    pqpm = sba("pqpm", [128, 4 * 64], BF16)
    wsl = sba("wslice", [128, 5 * 512], BF16)
    den = sba("den", [128, S], F32)
    mrow = den[0:1, :]
    xn_tok = sba("xn_tok", [128, H], BF16)
    ps_ctx = es_attn.enter_context(nc.psum_tensor("ps_ctx", [128, S], F32))
    qsq = ctx_h
    es_rel = contextlib.ExitStack()
    lnb_g = sb_in(es_rel, "lnb_g", [128, H], F32)
    lnb_b = sb_in(es_rel, "lnb_b", [128, H], F32)
    re_tok = sb_in(es_rel, "re_tok", [128, H], BF16)

    for c in range(4):
        p.dma("sync", grev4[:, c * WBAND:(c + 1) * WBAND],
              dt["Grev"][c * 128:(c + 1) * 128, :])
        t_gtab = p.dma("sync", gfwd4[:, c * WBAND:(c + 1) * WBAND],
                       dt["Gfwd"][c * 128:(c + 1) * 128, :])
    t_g0 = p.dma("sync", lnb_g[0:1, :], rawap(dt["ln_g"], 0, [[H, 1], [1, H]]))
    t_b0 = p.dma("sync", lnb_b[0:1, :], rawap(dt["ln_b"], 0, [[H, 1], [1, H]]))
    t_g = t_b = None
    for (t0_, buf) in ((t_g0, lnb_g), (t_b0, lnb_b)):
        for nk in range(2):
            key = "a" if nk == 0 else "b"
            pst = ps_a if nk == 0 else ps_b
            p.top("matmul", pst[:, 0:512], ones_fr[:],
                  buf[0:1, 512 * nk:512 * (nk + 1)],
                  start=True, stop=True, after=[t0_, psfree[key]])
            tmm = p.tmark()
            tkn = p.op("vector", "tensor_copy",
                       buf[:, 512 * nk:512 * (nk + 1)], pst[:, 0:512],
                       after=[tmm])
            psfree[key] = tkn
        if buf is lnb_g:
            t_g = tkn
        else:
            t_b = tkn

    t_refm = None
    for rt in range(4 if REL_RUN else 0):
        tl = p.dma("sync", xt_f[rt % 2][:],
                   dt["rel"][128 * rt:128 * (rt + 1), :],
                   after=[xtfree[rt % 2], t_prep_gates[3]]
                   if xtfree[rt % 2] else [t_prep_gates[3]])
        ta = layernorm(xt_f[rt % 2][:], re_tok[:], H, gamma=lnb_g[:],
                       beta=lnb_b[:], after=[tl, t_g, t_b],
                       apply_after=[t_refm] if t_refm else [])
        xtfree[rt % 2] = ta
        for c in range(8):
            aft = [ta, t_id, psfree["t"]] if c == 0 else ()
            p.top("matmul", ps_t[:, 128 * c:128 * (c + 1)],
                  re_tok[:, 128 * c:128 * (c + 1)], identb[:],
                  start=(c == 0), stop=(c == 7), is_transpose=True, after=aft)
        tt = p.tmark()
        t_refm = p.op("scalar", "copy",
                      rawap(re_fm, 128 * rt, [[8 * 512, 128], [512, 8], [1, 128]]),
                      ps_t[:, 0:1024],
                      after=[tt, t_refm] if t_refm else [tt])
        psfree["t"] = t_refm
    if not REL_RUN:
        t_refm = p.tok("d_sync")
    es_rel.close()

    def win(kt):
        return max(0, kt - DMAX), min(NT, kt + DMAX + 1)

    t_batch_gate = None
    t_wo_loaded = None
    t_wsl_free = None
    normtok = [None] * (2 * NH)
    tabstore_tok = [None] * NH
    wo_view = bc_band

    for b in range(NB_RUN):
        _mk(f"b{b}_ln1")
        t_xn = None
        xn_tt = [None]
        for t in range(NT if LN1_RUN else 0):
            aft = [xtfree[t % 2]] if xtfree[t % 2] else []
            tl = p.dma("sync", xt_f[t % 2][:],
                       x_in[b * S + 128 * t:b * S + 128 * (t + 1), :],
                       after=aft)
            ta = layernorm(xt_f[t % 2][:], xn_tok[:], H, after=[tl],
                           apply_after=[xn_tt[0]] if xn_tt[0] else [])
            xtfree[t % 2] = ta
            for c in range(8):
                aft = [ta, psfree["t"]] if c == 0 else ()
                p.top("matmul", ps_t[:, 128 * c:128 * (c + 1)],
                      xn_tok[:, 128 * c:128 * (c + 1)], identb[:],
                      start=(c == 0), stop=(c == 7), is_transpose=True, after=aft)
            tt = p.tmark()
            xn_tt[0] = tt
            t_xn = p.op("scalar", "copy",
                        rawap(xn_fm, 128 * t, [[8 * S, 128], [S, 8], [1, 128]]),
                        ps_t[:, 0:1024], after=[tt, t_xn] if t_xn else [tt])
            psfree["t"] = t_xn
        if not LN1_RUN:
            t_xn = p.tok("d_sync")
        tap(f"xnfm_{b}", xn_fm[:], [t_xn])

        if b == 0:
            tcr1 = p.op("vector", "memset", rowbuf[0:1, :], 1.0)
            tks = []
            for dst_ten, row in ((Kx, 64), (Kx, 66), (Kx, 68),
                                 (QxH, 65), (QxL, 67)):
                tks.append(p.dma("sync", dst_ten[row:row + 1, :],
                                 rowbuf[0:1, :], after=[tcr1]))
            tcr0 = p.op("vector", "memset", rowbuf[0:1, :], 0.0,
                        after=[tks[-1]])
            for dst_ten, row in ((QxL, 65), (QxL, 66)):
                tks.append(p.dma("sync", dst_ten[row:row + 1, :],
                                 rowbuf[0:1, :], after=[tcr0]))
            t_cr = tks[-1]
            t_vone = p.op("vector", "memset",
                          rawap(v_aug, HD, [[NT * (HD + 1), 128], [HD + 1, NT]]),
                          1.0)
        t_head_gate = t_batch_gate

        for h in range(NH_RUN):
            _mk(f"head{h}_start")
            hg = [t_head_gate] if t_head_gate else []
            for k, nm in enumerate(("Wq", "Wk", "Wv", "Wpk", "Wpq")):
                tw = p.dma("gpsimd",
                           rawap(wsl, 512 * k, [[5 * 512, 128], [64, 8], [1, 64]]),
                           rawap(dt[nm], 64 * h, [[H, 128], [128 * H, 8], [1, 64]]),
                           after=([t_wsl_free] if t_wsl_free else hg)
                           if k == 0 else ())
            _mk(f"h{h}_wsl")
            for (wofs, dst_ten, bcol, sclv) in ((0, Qx, 0, SCALE),
                                                (1, Kx, 1, 1.0)):
                for c in range(8):
                    for nk in range(4):
                        aft = ([tw, t_xn, t_bq, psfree["ctx"]]
                               if c == 0 and nk == 0 else ())
                        p.top("matmul", ps_ctx[0:64, 512 * nk:512 * (nk + 1)],
                              wsl[:, 512 * wofs + 64 * c:512 * wofs + 64 * (c + 1)],
                              xn_fm[:, S * c + 512 * nk:S * c + 512 * (nk + 1)],
                              start=(c == 0), stop=(c == 7), after=aft)
                tmm = p.tmark()
                tev = p.op("vector", "tensor_scalar", dst_ten[0:64, :],
                           ps_ctx[0:64, :], float(sclv),
                           bvec[0:64, 16 * bcol + h:16 * bcol + h + 1],
                           AluOpType.mult, AluOpType.add, after=[tmm] + hg)
                psfree["ctx"] = tev
                if wofs == 0:
                    t_qf = tev
                else:
                    t_kf = tev
            tqc1 = p.op("vector", "tensor_copy", QxH[0:64, :], Qx[0:64, :],
                        after=[t_qf])
            tqc2 = p.op("vector", "tensor_copy", QxL[0:64, :], Qx[0:64, :])
            tap(f"qx_{b}_{h}", Qx[0:64, :], [t_qf])
            tap(f"kx_{b}_{h}", Kx[0:64, :], [t_kf])

            _mk(f"h{h}_qk")
            for nk in range(4):
                for c in range(8):
                    aft = ([t_kf, psfree["ctx"]] + hg if nk == 0 and c == 0
                           else ())
                    p.top("matmul", ps_ctx[0:64, 512 * nk:512 * (nk + 1)],
                          wsl[:, 2 * 512 + 64 * c:2 * 512 + 64 * (c + 1)],
                          xn_fm[:, S * c + 512 * nk:S * c + 512 * (nk + 1)],
                          start=(c == 0), stop=(c == 7), after=aft)
            tmm = p.tmark()
            if b > 0:
                t_wsl_free = tmm
            vd_st = ctx_h
            t_vd = p.op("vector", "tensor_scalar", vd_st[0:64, :],
                        ps_ctx[0:64, :], 1.0,
                        bvec[0:64, 32 + h:32 + h + 1],
                        AluOpType.mult, AluOpType.add, after=[tmm] + hg)
            psfree["ctx"] = t_vd
            for t in range(NT):
                aft = [t_vd, t_id, psfree["t"]] if t == 0 else ()
                p.top("matmul", ps_t[:, 64 * t:64 * (t + 1)],
                      vd_st[0:64, 128 * t:128 * (t + 1)], identb[0:64, 0:64],
                      start=(t == 0), stop=(t == NT - 1),
                      is_transpose=True, after=aft)
            t_vd_read = p.tmark()
            t_vev = p.op("scalar", "copy",
                         rawap(v_aug, 0,
                               [[NT * (HD + 1), 128], [HD + 1, NT], [1, HD]]),
                         ps_t[:, 0:NT * HD], after=[t_vd_read, t_vone])
            psfree["t"] = t_vev
            tap(f"vaug_{b}_{h}", v_aug[:, :], [t_vev])

            _mk(f"h{h}_v")
            if b > 0:
                t_tabR = p.dma("sync", tabR[:],
                               tabs_s[64 * h:64 * h + 64, 0:WBAND],
                               after=hg + [tabstore_tok[h]])
                t_tabF = p.dma("sync", tabF[:],
                               tabs_s[64 * h:64 * h + 64, WBAND:2 * WBAND])
                tap(f"tabR_{b}_{h}", tabR[:], [t_tabR])
                tap(f"tabF_{b}_{h}", tabF[:], [t_tabF])
            for (wofs, dstt, bcol, key) in ((3, pkpm, 48, "a"),
                                            (4, pqpm, 64, "b")) if b == 0 \
                    else ():
                pst = ps_a if key == "a" else ps_b
                pofs = 512 * (wofs - 3)
                pdst = bc_band[0:64, pofs:pofs + 512]
                for c in range(8):
                    aft = ([t_refm, tw, psfree[key]] + hg if c == 0 else ())
                    p.top("matmul", pst[0:64, 0:512],
                          wsl[:, 512 * wofs + 64 * c:512 * wofs + 64 * (c + 1)],
                          re_fm[:, 512 * c:512 * (c + 1)],
                          start=(c == 0), stop=(c == 7), after=aft)
                tmm = p.tmark()
                t_wsl_free = tmm
                t_pd = p.op("vector", "tensor_scalar", pdst,
                            pst[0:64, 0:512], 1.0,
                            bvec[0:64, bcol + h:bcol + h + 1],
                            AluOpType.mult, AluOpType.add, after=[tmm] + hg)
                psfree[key] = t_pd
                for pc in range(4):
                    aft = [t_pd, psfree["t2"]] if pc == 0 else ()
                    p.top("matmul", ps_t2[:, 64 * pc:64 * (pc + 1)],
                          bc_band[0:64, pofs + 128 * pc:pofs + 128 * (pc + 1)],
                          identb[0:64, 0:64], start=(pc == 0), stop=(pc == 3),
                          is_transpose=True, after=aft)
                tt = p.tmark()
                t_pm = p.op("scalar", "copy", dstt[:, 0:256],
                            ps_t2[:, 0:256], after=[tt])
                psfree["t2"] = t_pm
                if wofs == 3:
                    t_pk = t_pm
                else:
                    t_pq = t_pm
            for (srct, gt, dstt, sclt, tprev) in (
                    (pkpm, grev4, tabR, 1.0, t_pk),
                    (pqpm, gfwd4, tabF, SCALE, t_pq)) if b == 0 else ():
                t_tab = None
                for uc in range(3):
                    n0 = 512 * uc
                    n1 = min(WBAND, n0 + 512)
                    key = "a" if uc % 2 == 0 else "b"
                    pst = ps_a if key == "a" else ps_b
                    for pc in range(4):
                        aft = ([tprev, t_gtab, psfree[key]] + hg
                               if pc == 0 else ())
                        p.top("matmul", pst[0:64, 0:n1 - n0],
                              srct[:, 64 * pc:64 * (pc + 1)],
                              gt[:, WBAND * pc + n0:WBAND * pc + n1],
                              start=(pc == 0), stop=(pc == 3), after=aft)
                    tmm = p.tmark()
                    t_tab = p.op("scalar", "activation", dstt[:, n0:n1],
                                 pst[0:64, 0:n1 - n0], AF.Copy,
                                 scale=float(sclt),
                                 after=[tmm, t_tab] if t_tab else [tmm])
                    psfree[key] = t_tab
                if sclt == 1.0:
                    t_tabR = t_tab
                else:
                    t_tabF = t_tab
            if b == 0:
                p.dma("scalar", tabs_s[64 * h:64 * h + 64, 0:WBAND],
                      tabR[:], after=[t_tabR])
                tabstore_tok[h] = p.dma(
                    "scalar", tabs_s[64 * h:64 * h + 64, WBAND:2 * WBAND],
                    tabF[:], after=[t_tabF])
                tap(f"tabR_{b}_{h}", tabR[:], [t_tabR])
                tap(f"tabF_{b}_{h}", tabF[:], [t_tabF])

            _mk(f"h{h}_tabs")
            t_sat = []
            t_rowfree = None
            for (tab, t_tt, ucolA, ucolB, opd, t_opd,
                 dstA, rowA, dstB, rowB) in (
                    (tabR, t_tabR, U_SATH_REV, U_SATL_REV, Qx, t_qf,
                     QxH, 66, QxL, 68),
                    (tabF, t_tabF, U_SATH_FWD, U_SATL_FWD, Kx, t_kf,
                     Kx, 65, Kx, 67)):
                c0, c1 = (ucolA, ucolB) if ucolA < ucolB else (ucolB, ucolA)
                rA = 0 if ucolA == c0 else 1
                trow = None
                for nk in range(4):
                    aft = ([t_tt, t_opd, psfree["a"]] if nk == 0
                           else [psfree["a"]])
                    p.top("matmul", ps_a[0:2, 0:512],
                          rawap(tab, c0, [[WBAND, 64], [c1 - c0, 2]]),
                          opd[0:64, 512 * nk:512 * (nk + 1)],
                          start=True, stop=True, after=aft)
                    tmm = p.tmark()
                    aft2 = [tmm] + ([t_rowfree] if t_rowfree and nk == 0
                                    else [])
                    trow = p.op("scalar", "copy",
                                rowbuf[0:2, 512 * nk:512 * (nk + 1)],
                                ps_a[0:2, 0:512], after=aft2)
                    psfree["a"] = trow
                tA = p.dma("sync", dstA[rowA:rowA + 1, :],
                           rowbuf[rA:rA + 1, :], after=[trow])
                t_rowfree = p.dma("sync", dstB[rowB:rowB + 1, :],
                                  rowbuf[1 - rA:2 - rA, :], after=[trow])
                t_sat += [tA, t_rowfree]

            _mk(f"h{h}_sat")
            t1 = p.op("vector", "reduce_max", scal2[0:64, 2:3], Kx[0:64, :],
                      AX, apply_absolute_value=True, after=[t_kf])
            p.top("matmul", ps_a[0:1, 8:72], scal2[0:64, 2:3],
                  identf[0:64, 0:64], start=True, stop=True, is_transpose=True,
                  after=[t1, t_if, psfree["a"]])
            tmm = p.tmark()
            t2 = p.op("vector", "reduce_max", scal[0:1, 5:6], ps_a[0:1, 8:72],
                      AX, after=[tmm])
            psfree["a"] = t2
            tnorm = {}
            t_qsq_free = None
            for (tab, col, t_tt) in (((tabR, 8, t_tabR), (tabF, 10, t_tabF))
                                     if b == 0 else ()):
                tsq = p.op("scalar", "activation", qsq[0:64, 0:WBAND], tab[:],
                           AF.Square,
                           after=[t_tt] + t_sat + ([t_qsq_free]
                                                   if t_qsq_free else []))
                tmx = None
                for uc in range(3):
                    n0 = 512 * uc
                    n1 = min(WBAND, n0 + 512)
                    p.top("matmul", ps_a[0:1, 0:n1 - n0], ones_b64[:],
                          qsq[0:64, n0:n1], start=True, stop=True,
                          after=[tsq, psfree["a"]] if uc == 0
                          else [psfree["a"]])
                    tmm = p.tmark()
                    tr = p.op("vector", "reduce_max", scal[0:1, 6:7],
                              ps_a[0:1, 0:n1 - n0], AX, after=[tmm])
                    psfree["a"] = tr
                    if tmx is None:
                        tmx = p.op("vector", "tensor_copy",
                                   scal[0:1, col:col + 1], scal[0:1, 6:7],
                                   after=[tr])
                    else:
                        tmx = p.op("vector", "tensor_tensor",
                                   scal[0:1, col:col + 1],
                                   scal[0:1, col:col + 1], scal[0:1, 6:7],
                                   AluOpType.max, after=[tr])
                t_qsq_free = p.tmark()
                nbc = 2 * h + (0 if col == 8 else 1)
                tnorm[col] = p.op("scalar", "activation",
                                  normbuf[0:1, nbc:nbc + 1],
                                  scal[0:1, col:col + 1], AF.Sqrt,
                                  after=[tmx] +
                                  ([normtok[nbc]] if normtok[nbc] else []))
                normtok[nbc] = tnorm[col]
            tn8 = normtok[2 * h]
            tn10 = normtok[2 * h + 1]
            p.op("vector", "tensor_tensor", scal[0:1, 11:12], scal[0:1, 5:6],
                 normbuf[0:1, 2 * h:2 * h + 1], AluOpType.add,
                 after=[tn8, t2])
            tm1 = p.op("vector", "tensor_tensor", scal[0:1, 11:12],
                       scal[0:1, 11:12], scal[0:1, 11:12], AluOpType.mult)
            p.op("vector", "scalar_tensor_tensor", scal[0:1, 12:13],
                 scal[0:1, 5:6], 1.0, normbuf[0:1, 2 * h + 1:2 * h + 2],
                 AluOpType.bypass, AluOpType.mult, after=[tn10])
            tm2 = p.op("vector", "tensor_scalar_add", scal[0:1, 12:13],
                       scal[0:1, 12:13], 1.0)
            tqs = p.op("scalar", "activation", qsq[0:64, 0:S], Qx[0:64, :],
                       AF.Square, after=[t_qf, t_qsq_free] + t_sat)
            t_mr = None
            for nk in range(4):
                p.top("matmul", ps_a[0:1, 0:512], ones_b64[:],
                      qsq[0:64, 512 * nk:512 * (nk + 1)],
                      start=True, stop=True,
                      after=[tqs, psfree["a"]] if nk == 0
                      else [psfree["a"]])
                tmm = p.tmark()
                t_mr = p.op("scalar", "activation",
                            den[0:1, 512 * nk:512 * (nk + 1)],
                            ps_a[0:1, 0:512], AF.Sqrt,
                            scale=scal[0:1, 11:12], after=[tmm, tm1])
                psfree["a"] = t_mr
            trb = p.op("vector", "tensor_scalar", rowbuf[0:1, :], den[0:1, :],
                       scal[0:1, 12:13], -1.0, AluOpType.add, AluOpType.mult,
                       after=[t_mr, tm2, t_rowfree])
            p.dma("sync", Qx[64:65, :], rowbuf[0:1, :], after=[trb])
            p.dma("sync", QxH[64:65, :], rowbuf[0:1, :], after=[tqc1])
            t_m = p.dma("sync", QxL[64:65, :], rowbuf[0:1, :], after=[tqc2])
            tap(f"mrow_{b}_{h}", den[0:1, :], [t_mr])

            _mk(f"h{h}_mrow")
            t_bc = None
            cfree = [psfree["ctx"]] * 4
            ridx = 0
            for qt in range(NT):
                for uc in range(3):
                    n0 = 512 * uc
                    n1 = min(WBAND, n0 + 512)
                    r = ridx % 4
                    ridx += 1
                    aft = [t_tabR, t_qf, cfree[r]]
                    p.top("matmul", ps_ctx[:, 512 * r:512 * r + n1 - n0],
                          Qx[0:64, 128 * qt:128 * (qt + 1)],
                          tabR[:, n0:n1], start=True, stop=True, after=aft)
                    tmm = p.tmark()
                    t_bc = p.op("vector" if uc == 1 else "scalar",
                                "tensor_copy" if uc == 1 else "copy",
                                bc_band[:, WBAND * qt + n0:WBAND * qt + n1],
                                ps_ctx[:, 512 * r:512 * r + n1 - n0],
                                after=[tmm])
                    cfree[r] = t_bc
            tap(f"bc_{b}_{h}", bc_band[:, 0:WBAND], [t_bc])

            def make_bp(kt, buf, rdtok=None, extra=()):
                tb = None
                for uc in range(3):
                    n0 = 512 * uc
                    n1 = min(WBAND, n0 + 512)
                    key = "a" if uc % 2 == 0 else "b"
                    pst = ps_a if key == "a" else ps_b
                    aft = list(extra) + [psfree[key], t_kf, t_tabF]
                    p.top("matmul", pst[:, 0:n1 - n0],
                          Kx[0:64, 128 * kt:128 * (kt + 1)],
                          tabF[:, n0:n1], start=True, stop=True, after=aft)
                    tmm = p.tmark()
                    if uc == 1:
                        tb = p.op("scalar", "copy", buf[:, n0:n1],
                                  pst[:, 0:n1 - n0], after=[tmm])
                    else:
                        tb = p.op("vector", "tensor_copy", buf[:, n0:n1],
                                  pst[:, 0:n1 - n0],
                                  after=[tmm, rdtok] if (rdtok and uc == 0)
                                  else [tmm])
                    psfree[key] = tb
                return tb

            bfree = {"ctile": [None, None], "p2cT": [None, None],
                     "pt": [None, None], "c2pT": [None, None],
                     "bp": [None, None], "bpdma": [None, None]}
            t_bp = [None, None]

            def prep_kt_dma(kt):
                i = kt % 2
                qlo, qhi = win(kt)
                nd = qhi - qlo
                srcv = rawap(bp_band[i], 639 + 128 * (qlo - kt),
                             [[WBAND - 1, 128], [1, 128 * nd]])
                tp2 = p.dma("sync", p2cT[i][:, 0:128 * nd], srcv,
                            after=[t_bp[i], bfree["p2cT"][i]])
                bfree["bpdma"][i] = tp2
                srcv = rawap(bc_band, WBAND * kt + 639 + 1152 * (qlo - kt),
                             [[WROW - 1, 128], [1152, nd], [1, 128]])
                tct = p.dma("sync", ctile[i][:, 0:128 * nd], srcv,
                            after=[t_bc, bfree["ctile"][i]])
                return tp2, tct

            def prep_kt_pe(kt, tp2, tct):
                i = kt % 2
                qlo, qhi = win(kt)
                nd = qhi - qlo
                tev = None
                for d in range(nd):
                    if d == 0:
                        aft = [tct, psfree["t2"]]
                    elif d % 8 == 0:
                        aft = [tev]
                    else:
                        aft = ()
                    p.top("matmul",
                          ps_t2[:, 128 * (d % 8):128 * (d % 8) + 128],
                          ctile[i][:, 128 * d:128 * (d + 1)], identb[:],
                          start=(d % 8 == 0),
                          stop=(d % 8 == 7 or d == nd - 1),
                          is_transpose=True, after=aft)
                    if d % 8 == 7 or d == nd - 1:
                        g0 = (d // 8) * 8
                        tt = p.tmark()
                        aft2 = [tt] + ([bfree["c2pT"][i]]
                                       if bfree["c2pT"][i] else [])
                        if FOLD_BANDS:
                            if g0 == 0:
                                aft2.append(tp2)
                            tev = p.op("vector", "tensor_tensor",
                                       c2pT[i][:, 128 * g0:128 * (d + 1)],
                                       ps_t2[:, 0:128 * (d + 1 - g0)],
                                       p2cT[i][:, 128 * g0:128 * (d + 1)],
                                       AluOpType.add, after=aft2)
                        else:
                            tev = p.op("scalar", "copy",
                                       c2pT[i][:, 128 * g0:128 * (d + 1)],
                                       ps_t2[:, 0:128 * (d + 1 - g0)],
                                       after=aft2)
                psfree["t2"] = tev
                bfree["ctile"][i] = p.tok("p_tensor")
                if FOLD_BANDS:
                    bfree["p2cT"][i] = tev
                return tp2, tev

            _mk(f"h{h}_bc")
            t_bp[0] = make_bp(0, bp_band[0])
            t_prep = [None, None]
            t_prep[0] = prep_kt_pe(0, *prep_kt_dma(0))

            for kt in range(NT):
                _mk(f"h{h}_kt{kt}")
                i = kt % 2
                qlo, qhi = win(kt)
                pt = pt_buf[i]
                tp2, tc2 = t_prep[i]
                if kt + 1 < NT:
                    j = (kt + 1) % 2
                    t_bp[j] = make_bp(kt + 1, bp_band[j],
                                      rdtok=bfree["bpdma"][j])
                    dma_next = prep_kt_dma(kt + 1)
                exp_toks = []
                for ch in range(4):
                    c0 = 512 * ch
                    key = "a" if ch % 2 == 0 else "b"
                    pst = ps_a if key == "a" else ps_b
                    qt0 = c0 // 128
                    segs = []
                    for qt in range(qt0, qt0 + 4):
                        r = "B" if qlo <= qt < qhi else ("L" if qt < qlo else "H")
                        if segs and segs[-1][0] == r:
                            segs[-1][2] = qt + 1
                        else:
                            segs.append([r, qt, qt + 1])
                    blo = max(c0, 128 * qlo)
                    bhi = min(c0 + 512, 128 * qhi)
                    fold = FOLD_BANDS and P2C_ON and C2P_ON
                    nband = ((1 if fold else int(P2C_ON) + int(C2P_ON))
                             if blo < bhi else 0)
                    nmm = len(segs) + nband
                    k = 0
                    aftc = [t_m, tp2, tc2, bfree["pt"][i], psfree[key],
                            t_cr, tqc1, tqc2] + t_sat
                    for (r, q0_, q1_) in segs:
                        n0 = 128 * q0_
                        n1 = 128 * q1_
                        if r == "B" or not SAT_ON:
                            lhsT = Kx[0:65, 128 * kt:128 * (kt + 1)]
                            rhs = Qx[0:65, n0:n1]
                        elif r == "L":
                            lhsT = Kx[0:69, 128 * kt:128 * (kt + 1)]
                            rhs = QxL[0:69, n0:n1]
                        else:
                            lhsT = Kx[0:67, 128 * kt:128 * (kt + 1)]
                            rhs = QxH[0:67, n0:n1]
                        k += 1
                        p.top("matmul", pst[:, n0 - c0:n1 - c0], lhsT, rhs,
                              start=(k == 1), stop=(k == nmm),
                              after=aftc if k == 1 else ())
                    if nband:
                        off = blo - 128 * qlo
                        nb_ = 1 if fold else int(P2C_ON) + int(C2P_ON)
                        if P2C_ON and not fold:
                            p.top("matmul", pst[:, blo - c0:bhi - c0], identb[:],
                                  p2cT[i][:, off:off + bhi - blo],
                                  start=False, stop=(nb_ == 1))
                        if C2P_ON:
                            p.top("matmul", pst[:, blo - c0:bhi - c0], identb[:],
                                  c2pT[i][:, off:off + bhi - blo],
                                  start=False, stop=True)
                        if nb_ == 0:
                            p.top("matmul", pst[:, c0 - c0:c0 - c0 + 1]
                                  if False else pst[:, 0:512],
                                  Kx[0:65, 128 * kt:128 * (kt + 1)],
                                  Qx[0:65, c0:c0 + 512],
                                  start=False, stop=True)
                    tmm = p.tmark()
                    tex = p.op("scalar", "activation", pt[:, c0:c0 + 512],
                               pst[:, 0:512], AF.Exp, after=[tmm])
                    psfree[key] = tex
                    exp_toks.append(tex)
                if not (FOLD_BANDS and P2C_ON and C2P_ON):
                    bfree["p2cT"][i] = p.tok("p_tensor")
                bfree["c2pT"][i] = p.tok("p_tensor")
                for ch in range(4):
                    aft = ([exp_toks[0], t_vev, t_vone]
                           if ch == 0 else [exp_toks[ch]])
                    if kt == 0:
                        aft = aft + [cfree[ch]]
                    p.top("matmul", ps_ctx[0:65, 512 * ch:512 * (ch + 1)],
                          v_aug[:, (HD + 1) * kt:(HD + 1) * (kt + 1)],
                          pt[:, 512 * ch:512 * (ch + 1)],
                          start=(kt == 0), stop=(kt == NT - 1), after=aft)
                tap(f"pt_{b}_{h}_{kt}", pt[:, :], exp_toks)
                bfree["pt"][i] = p.tmark()
                if kt + 1 < NT:
                    t_prep[j] = prep_kt_pe(kt + 1, *dma_next)

            _mk(f"h{h}_ctx")
            t_pv_done = bfree["pt"][(NT - 1) % 2]
            tden = p.op("scalar", "copy", den[64:65, :], ps_ctx[64:65, :],
                        after=[t_pv_done])
            tap(f"den_{b}_{h}", den[64:65, :], [tden])
            trec = p.op("vector", "reciprocal", den[64:65, :], den[64:65, :],
                        after=[tden])
            t_ctxh = None
            for ch in range(4):
                key = "a" if ch % 2 == 0 else "b"
                pst = ps_a if key == "a" else ps_b
                p.top("matmul", pst[0:64, 0:512], ones64[64:65, 0:64],
                      den[64:65, 512 * ch:512 * (ch + 1)],
                      start=True, stop=True,
                      after=[trec, t_o64, psfree[key]])
                tmm = p.tmark()
                aft_cb = [tmm] + ([t_ctxh] if t_ctxh else []) +                     ([shared["xsq"]] if shared["xsq"] else [])
                tcb = p.op("scalar", "copy", xsq[0:64, 0:512],
                           pst[0:64, 0:512], after=aft_cb)
                psfree[key] = tcb
                t_ctxh = p.op("vector", "tensor_tensor",
                              ctx_h[:, 512 * ch:512 * (ch + 1)],
                              ps_ctx[0:64, 512 * ch:512 * (ch + 1)],
                              xsq[0:64, 0:512], AluOpType.mult,
                              after=[tcb, t_ctxh] if t_ctxh else [tcb])
                shared["xsq"] = t_ctxh
            psfree["ctx"] = t_ctxh
            tap(f"ctxh_{b}_{h}", ctx_h[:, :], [t_ctxh])
            t_head_gate = p.dma("scalar", ctx_s[64 * h:64 * h + 64, :],
                                ctx_h[:, :], after=[t_ctxh])

        _mk(f"b{b}_wo")
        if WO_RUN:
            t_d_gate = p.tok("d_sync")
            t_bo = p.dma("sync", den[0:1, 0:H], rawap(dt["bo"], 0, [[H, 1], [1, H]]),
                         after=[t_head_gate])
            t_wo_loaded = None
            for c in range(8):
                t_wo_loaded = p.dma("gpsimd", wo_view[:, H * c:H * (c + 1)],
                                    dt["Wo"][128 * c:128 * (c + 1), :],
                                    after=[t_d_gate, t_head_gate] if c == 0 else ())
            t_x2 = None
            woc_free = [None, None]
            for t in range(NT):
                tlx = p.dma("scalar", xt_f[t % 2][:],
                            x_in[b * S + 128 * t:b * S + 128 * (t + 1), :],
                            after=[xtfree[t % 2]] if xtfree[t % 2] else [])
                tctx = p.dma("sync", rawap(wo_ctx[t % 2], 0,
                                           [[8 * 128, 128], [128, 8], [1, 128]]),
                             rawap(ctx_s, 128 * t, [[S, 128], [128 * S, 8], [1, 128]]),
                             after=[t_head_gate] + ([woc_free[t % 2]]
                                                    if woc_free[t % 2] else []))
                for nk in range(2):
                    key = "a" if nk == 0 else "b"
                    pst = ps_a if key == "a" else ps_b
                    for c in range(8):
                        aft = ([tctx, t_wo_loaded, psfree[key]]
                               if c == 0 else ())
                        p.top("matmul", pst[:, 0:512],
                              wo_ctx[t % 2][:, 128 * c:128 * (c + 1)],
                              wo_view[:, H * c + 512 * nk:H * c + 512 * (nk + 1)],
                              start=(c == 0), stop=False, after=aft)
                    p.top("matmul", pst[:, 0:512], ones_fr[0:1, :],
                          den[0:1, 512 * nk:512 * (nk + 1)],
                          start=False, stop=True,
                          after=[t_bo] if t == 0 and nk == 0 else ())
                tmm = p.tmark()
                ta = p.op("vector", "tensor_tensor", xt_f[t % 2][:, 0:512],
                          xt_f[t % 2][:, 0:512], ps_a[:, 0:512], AluOpType.add,
                          after=[tmm, tlx])
                tb = p.op("vector", "tensor_tensor", xt_f[t % 2][:, 512:1024],
                          xt_f[t % 2][:, 512:1024], ps_b[:, 0:512],
                          AluOpType.add)
                psfree["a"] = tb
                psfree["b"] = tb
                woc_free[t % 2] = p.tmark()
                xtfree[t % 2] = tb
                t_x2 = p.dma("sync", x2_s[b * S + 128 * t:b * S + 128 * (t + 1), :],
                             xt_f[t % 2][:], after=[tb])
                xtfree[t % 2] = t_x2

        else:
            t_x2 = t_head_gate
        t_batch_gate = t_x2

    gates = p.engine_gates()

    es_attn.close()

    if FFN_RUN:
        _mk("ffnA")
        es_fa = contextlib.ExitStack()
        wgT = sb_in(es_fa, "wgT", [128, 8 * INNER], BF16)
        wuT = sb_in(es_fa, "wuT", [128, 8 * INNER], BF16)
        gu = sb_in(es_fa, "gu", [128, INNER], F32)
        xn2 = sb_in(es_fa, "xn2", [128, H], F32)
        yq = sb_in(es_fa, "yq", [128, 1024], F32)
        qint = sb_in(es_fa, "qint", [128, 1024], I32)
        qflt = sb_in(es_fa, "qflt", [128, 1024], F32)
        gq1 = sb_in(es_fa, "gq1", [128, 1024], F32)
        gq2 = qint.bitcast(F32)
        xq_tok = sb_in(es_fa, "xq_tok", [128, H], BF16)
        xqT = sb_in(es_fa, "xqT", [128, H], BF16)
        guq_tok = sb_in(es_fa, "guq_tok", [128, INNER], BF16)
        stage2 = sb_in(es_fa, "stage2", [128, INNER], BF16)
        sgate = xsq
        dscv = sb_in(es_fa, "dscv", [128, 8], F32)
        ps_g = [es_fa.enter_context(nc.psum_tensor(f"ps_g{i}", [128, 512], F32))
                for i in range(2)]
        ps_u = [es_fa.enter_context(nc.psum_tensor(f"ps_u{i}", [128, 512], F32))
                for i in range(2)]

        tsw = None
        for col in (2, 3, 4):
            aft = (gates + [tw_g, tw_u, tw_d, psfree["a"]] if col == 2
                   else [psfree["a"]])
            p.top("matmul", ps_a[:, 4:5], ones_fr[:],
                  scal[0:1, col:col + 1], start=True, stop=True, after=aft)
            tmm = p.tmark()
            tsw = p.op("vector", "tensor_copy", thrv[:, col:col + 1],
                       ps_a[:, 4:5], after=[tmm])
            psfree["a"] = tsw

        t_wres = None
        for c in range(8):
            p.dma("sync", wgT[:, INNER * c:INNER * (c + 1)],
                  wgT_s[128 * c:128 * (c + 1), :],
                  after=gates + [tw_g] if c == 0 else [])
            t_wres = p.dma("sync", wuT[:, INNER * c:INNER * (c + 1)],
                           wuT_s[128 * c:128 * (c + 1), :])

        qshared = [None]
        qscale_war = [None]
        qfin = [None]

        def quantize(src, W, out_bf, t_src, dcol, chunk_toks=None,
                     scale_war=None, out_war=None):
            if chunk_toks is None:
                chunk_toks = []
            t0 = p.op("vector", "reduce_max", dscv[:, 0:1], src[:, 0:W], AX,
                      apply_absolute_value=True,
                      after=[t_src] + ([qscale_war[0]] if qscale_war[0]
                                       else []))
            t1a = p.op("vector", "tensor_single_scalar", dscv[:, 0:1],
                       dscv[:, 0:1], 1e-5, AluOpType.max, after=[t0])
            t1 = p.op("vector", "tensor_scalar_mul", dscv[:, dcol:dcol + 1],
                      dscv[:, 0:1], 1.0 / 127.0,
                      after=[scale_war] if scale_war else ())
            t2r = p.op("vector", "reciprocal", dscv[:, 1:2], dscv[:, 0:1])
            t2 = p.op("vector", "tensor_scalar_mul", dscv[:, 1:2],
                      dscv[:, 1:2], 127.0)
            qscale_war[0] = t2
            tq = None
            tzp = qshared[0]
            for q0 in range(0, W, 1024):
                ty = p.op("vector", "tensor_scalar_mul", yq[:, 0:1024],
                          src[:, q0:q0 + 1024], dscv[:, 1:2],
                          after=[t2] + ([tzp] if tzp else [])
                          + ([tq] if tq else ([qfin[0]] if qfin[0] else [])))
                tz = p.op("scalar", "copy", qint[:, 0:1024], yq[:, 0:1024],
                          after=[ty])
                tzp = tz
                tzf = p.op("scalar", "copy", qflt[:, 0:1024], qint[:, 0:1024])
                tg1 = p.op("vector", "scalar_tensor_tensor", gq1[:, 0:1024],
                           qflt[:, 0:1024], 0.5, yq[:, 0:1024], AluOpType.add,
                           AluOpType.is_le, after=[tzf])
                tg2 = p.op("vector", "scalar_tensor_tensor", gq2[:, 0:1024],
                           qflt[:, 0:1024], -0.5, yq[:, 0:1024], AluOpType.add,
                           AluOpType.is_ge)
                td = p.op("gpsimd", "tensor_tensor", gq1[:, 0:1024],
                          gq1[:, 0:1024], gq2[:, 0:1024], AluOpType.subtract,
                          after=[tg1, tg2])
                tq = p.op("gpsimd", "tensor_tensor", out_bf[:, q0:q0 + 1024],
                          qflt[:, 0:1024], gq1[:, 0:1024], AluOpType.add,
                          after=[td] + ([out_war] if out_war and q0 == 0
                                        else []))
                chunk_toks.append(tq)
            qshared[0] = tzp
            qfin[0] = tq
            return tq

        stage_free = None
        pipe = {"tsg": None, "tumm": None, "tdu": None, "xq_tt": None,
                "tds": None, "guq_tt": None}

        def ffn_front(t):
            aft = [xtfree[t % 2]] if xtfree[t % 2] else []
            if t == 0:
                aft = aft + gates
            tl = p.dma("sync", xt_f[t % 2][:], x2_s[128 * t:128 * (t + 1), :],
                       after=aft)
            tln = layernorm(xt_f[t % 2][:], xn2[:], H, after=[tl],
                            apply_after=[pipe["tqx"]] if pipe.get("tqx")
                            else ())
            xtfree[t % 2] = tln
            tqx = quantize(xn2, H, xq_tok, tln, 5,
                           scale_war=pipe["tdu"], out_war=pipe["xq_tt"])
            pipe["tqx"] = tqx
            for c in range(8):
                aft2 = [tqx, psfree["t"]] if c == 0 else ()
                p.top("matmul", ps_t[:, 128 * c:128 * (c + 1)],
                      xq_tok[:, 128 * c:128 * (c + 1)], identb[:],
                      start=(c == 0), stop=(c == 7), is_transpose=True,
                      after=aft2)
            tt2 = p.tmark()
            pipe["xq_tt"] = tt2
            txq = p.op("scalar", "copy", xqT[:], ps_t[:, 0:1024],
                       after=[tt2, pipe["tumm"]] if pipe["tumm"] else [tt2])
            psfree["t"] = txq
            return tqx, txq

        nxt_front = ffn_front(0)
        for t in range(NB * NT):
            tqx, txq = nxt_front
            tdg = p.op("vector", "tensor_tensor", dscv[:, 6:7], dscv[:, 5:6],
                       thrv[:, 2:3], AluOpType.mult,
                       after=[tqx, tsw] +
                       ([pipe["tsg"]] if pipe["tsg"] else []))
            tdu = p.op("vector", "tensor_tensor", dscv[:, 7:8], dscv[:, 5:6],
                       thrv[:, 3:4], AluOpType.mult)
            pipe["tdu"] = tdu
            t_gu = None
            t_gufree = [[], []]
            for ic in range(8):
                pg, pu = ps_g[ic % 2], ps_u[ic % 2]
                for c in range(8):
                    aft = (([txq, t_wres] if ic < 2 else [])
                           + t_gufree[ic % 2] if c == 0 else ())
                    p.top("matmul", pg[:, 0:512],
                          xqT[:, 128 * c:128 * (c + 1)],
                          wgT[:, INNER * c + 512 * ic:INNER * c + 512 * (ic + 1)],
                          start=(c == 0), stop=(c == 7), after=aft)
                tg_mm = p.tmark()
                for c in range(8):
                    p.top("matmul", pu[:, 0:512],
                          xqT[:, 128 * c:128 * (c + 1)],
                          wuT[:, INNER * c + 512 * ic:INNER * c + 512 * (ic + 1)],
                          start=(c == 0), stop=(c == 7))
                tu_mm = p.tmark()
                tsg = p.op("scalar", "activation", sgate[:, 0:512],
                           pg[:, 0:512],
                           AF.Silu, scale=dscv[:, 6:7],
                           after=[tg_mm, tdu] +
                           ([t_gu] if t_gu else
                            ([shared["xsq"]] if shared["xsq"] else [])))
                tsu = p.op("vector", "tensor_scalar_mul",
                           gu[:, 512 * ic:512 * (ic + 1)], pu[:, 0:512],
                           dscv[:, 7:8],
                           after=[tu_mm] + ([t_gu] if t_gu else []))
                t_gufree[ic % 2] = [tsg, tsu]
                t_gu = p.op("vector", "tensor_tensor",
                            gu[:, 512 * ic:512 * (ic + 1)],
                            gu[:, 512 * ic:512 * (ic + 1)], sgate[:, 0:512],
                            AluOpType.mult, after=[tsg, tsu])
                if ic == 7:
                    pipe["tsg"] = tsg
                    pipe["tumm"] = tu_mm
            shared["xsq"] = t_gu
            if t + 1 < NB * NT:
                nxt_front = ffn_front(t + 1)
            guq_toks = []
            tqg = quantize(gu, INNER, guq_tok, t_gu, 3, chunk_toks=guq_toks,
                           scale_war=pipe["tds"], out_war=pipe["guq_tt"])
            tds = p.op("vector", "tensor_copy", dsc_all[:, t:t + 1],
                       dscv[:, 3:4], after=[tqg])
            pipe["tds"] = tds
            for g0 in range(0, 32, 8):
                key = "t" if (g0 // 8) % 2 == 0 else "t2"
                pst = ps_t if key == "t" else ps_t2
                for c in range(g0, g0 + 8):
                    aft = [guq_toks[g0 // 8], psfree[key]] if c == g0 else ()
                    p.top("matmul", pst[:, 128 * (c - g0):128 * (c - g0) + 128],
                          guq_tok[:, 128 * c:128 * (c + 1)], identb[:],
                          start=(c == g0), stop=(c == g0 + 7),
                          is_transpose=True, after=aft)
                tt = p.tmark()
                if g0 == 24:
                    pipe["guq_tt"] = tt
                tev = p.op("scalar", "copy", stage2[:, 128 * g0:128 * (g0 + 8)],
                           pst[:, 0:1024],
                           after=[tt, stage_free] if stage_free else [tt])
                psfree[key] = tev
            srcv = rawap(stage2, 0, [[INNER, 128], [128, 32], [1, 128]])
            dstv = rawap(guq_s, t * (128 * INNER),
                         [[128, 128], [128 * 128, 32], [1, 128]])
            stage_free = p.dma("sync", dstv, srcv, after=[tev])
        t_a_gates = p.engine_gates()
        es_fa.close()

        _mk("ffnB")
        es_fb = contextlib.ExitStack()
        wdres = sb_in(es_fb, "wdres", [128, 32 * H], BF16)
        guqT_in = [sb_in(es_fb, f"guqT_in{i}", [128, INNER], BF16)
                   for i in range(2)]
        ps_d = [es_fb.enter_context(nc.psum_tensor(f"ps_d{i}", [128, 1024],
                                                   F32)) for i in range(2)]

        t_wd2 = None
        for c in range(32):
            t_wd2 = p.dma("sync", wdres[:, H * c:H * (c + 1)],
                          wdT_s[128 * c:128 * (c + 1), :],
                          after=t_a_gates + [tw_d] if c == 0 else [])
        ginfree = [None, None]
        t_psd_free = [None, None]
        for t in range(NB * NT):
            tg = p.dma("sync", guqT_in[t % 2][:],
                       rawap(guq_s, t * (128 * INNER),
                             [[128, 128], [16384, 32], [1, 128]]),
                       after=[ginfree[t % 2]] if ginfree[t % 2] else [])
            aft = [xtfree[t % 2]] if xtfree[t % 2] else []
            if t == 0:
                aft = aft + t_a_gates
            tlx = p.dma("scalar", xt_f[t % 2][:], x2_s[128 * t:128 * (t + 1), :],
                        after=aft)
            for nk in range(2):
                for c in range(32):
                    aft = [a for a in (tg, t_wd2, t_psd_free[t % 2]) if a]
                    p.top("matmul", ps_d[t % 2][:, 512 * nk:512 * (nk + 1)],
                          guqT_in[t % 2][:, 128 * c:128 * (c + 1)],
                          wdres[:, H * c + 512 * nk:H * c + 512 * (nk + 1)],
                          start=(c == 0), stop=(c == 31),
                          after=aft if nk == 0 and c == 0 else ())
            tmm = p.tmark()
            ginfree[t % 2] = tmm
            tsc = p.op("vector", "tensor_tensor", dscv[0:128, 0:1]
                       if False else scal2[:, 3:4], dsc_all[:, t:t + 1],
                       thrv[:, 4:5], AluOpType.mult, after=[tmm, tsw])
            ta = p.op("vector", "scalar_tensor_tensor", xt_f[t % 2][:, 0:512],
                      ps_d[t % 2][:, 0:512], scal2[:, 3:4],
                      xt_f[t % 2][:, 0:512],
                      AluOpType.mult, AluOpType.add, after=[tsc, tlx])
            tb = p.op("vector", "scalar_tensor_tensor", xt_f[t % 2][:, 512:1024],
                      ps_d[t % 2][:, 512:1024], scal2[:, 3:4],
                      xt_f[t % 2][:, 512:1024],
                      AluOpType.mult, AluOpType.add)
            t_psd_free[t % 2] = tb
            tst = p.dma("sync", out_hbm[128 * t:128 * (t + 1), :],
                        xt_f[t % 2][:], after=[tb])
            xtfree[t % 2] = tst


    final_d = p.tok("d_sync")
    final_ds = p.tok("d_scalar")
    final_dg = p.tok("d_gpsimd")
    if TRUNC is not None:
        p.steps = p.steps[:TRUNC]

    with nc.Block() as block:
        @block.sync
        def _(sync):
            for e, fn in p.steps:
                if e == "sync":
                    fn(sync)
            if TRUNC is None:
                sync.wait_ge(p.sem["d_sync"], final_d[1])
                sync.wait_ge(p.sem["d_scalar"], final_ds[1])
                sync.wait_ge(p.sem["d_gpsimd"], final_dg[1])

        @block.vector
        def _(vector):
            for e, fn in p.steps:
                if e == "vector":
                    fn(vector)

        @block.scalar
        def _(scalar):
            for e, fn in p.steps:
                if e == "scalar":
                    fn(scalar)

        @block.tensor
        def _(tensor):
            for e, fn in p.steps:
                if e == "tensor":
                    fn(tensor)

        @block.gpsimd
        def _(gpsimd):
            for e, fn in p.steps:
                if e == "gpsimd":
                    fn(gpsimd)

    if FFN_RUN:
        es_fb.close()
    p.es.close()
    return nc


_last_results = None


def kernel(**inputs):
    global _last_results
    import ml_dtypes
    g_rev, g_fwd = _g_tables()
    ident = np.eye(128, dtype=np.float32)

    x = np.ascontiguousarray(np.asarray(inputs["L0_attn_hidden_states"],
                                        np.float32))
    base = {
        "Wq": np.ascontiguousarray(inputs["L0_attn_Wq"], dtype=np.float32),
        "bq": np.ascontiguousarray(inputs["L0_attn_bq"], dtype=np.float32),
        "Wk": np.ascontiguousarray(inputs["L0_attn_Wk"], dtype=np.float32),
        "bk": np.ascontiguousarray(inputs["L0_attn_bk"], dtype=np.float32),
        "Wv": np.ascontiguousarray(inputs["L0_attn_Wv"], dtype=np.float32),
        "bv": np.ascontiguousarray(inputs["L0_attn_bv"], dtype=np.float32),
        "Wo": np.ascontiguousarray(inputs["L0_attn_Wo"], dtype=np.float32),
        "bo": np.ascontiguousarray(inputs["L0_attn_bo"], dtype=np.float32),
        "Wpk": np.ascontiguousarray(inputs["L0_attn_Wpk"], dtype=np.float32),
        "bpk": np.ascontiguousarray(inputs["L0_attn_bpk"], dtype=np.float32),
        "Wpq": np.ascontiguousarray(inputs["L0_attn_Wpq"], dtype=np.float32),
        "bpq": np.ascontiguousarray(inputs["L0_attn_bpq"], dtype=np.float32),
        "rel": np.ascontiguousarray(inputs["L0_attn_rel_emb"], dtype=np.float32),
        "ln_g": np.ascontiguousarray(inputs["L0_attn_ln_g"], dtype=np.float32),
        "ln_b": np.ascontiguousarray(inputs["L0_attn_ln_b"], dtype=np.float32),
        "wg": np.ascontiguousarray(inputs["L0_ffn_w_gate"], dtype=np.float32),
        "wu": np.ascontiguousarray(inputs["L0_ffn_w_up"], dtype=np.float32),
        "wd": np.ascontiguousarray(inputs["L0_ffn_w_down"], dtype=np.float32),
        "Grev": g_rev, "Gfwd": g_fwd,
        "identb": ident.astype(ml_dtypes.bfloat16), "identf": ident,
        "bqk_t": np.ascontiguousarray(np.concatenate(
            [np.asarray(inputs["L0_attn_" + nm], np.float32).reshape(16, 64).T
             for nm in ("bq", "bk", "bv", "bpk", "bpq")], axis=1)),
    }
    in_maps = []
    for i in range(NCORE):
        m = dict(base)
        m["x"] = np.ascontiguousarray(x[NB * i:NB * (i + 1)].reshape(NB * S, H))
        in_maps.append(m)

    nc = build_nc()
    res = run_bass_kernel_spmd(nc, in_maps, core_ids=list(range(NCORE)))
    _last_results = res
    return np.concatenate(
        [res.results[i]["out"].reshape(NB, S, H) for i in range(NCORE)],
        axis=0)
